# version 10
# baseline (speedup 1.0000x reference)
import math
import numpy as np
import ml_dtypes
import concourse.bass as bass
import concourse.mybir as mybir
from concourse.bass_utils import run_bass_kernel_spmd

F32 = mybir.dt.float32
BF16 = mybir.dt.bfloat16
I32 = mybir.dt.int32
AF = mybir.ActivationFunctionType
ALU = mybir.AluOpType
AXX = mybir.AxisListType.X
BF = ml_dtypes.bfloat16


class Cfg:
    def __init__(self, DM=4096, S=2048, H=32, HKV=8, NCL=64):
        self.B = 4
        self.DM = DM
        self.S = S
        self.H = H
        self.HKV = HKV
        self.G = H // HKV
        self.HD = 128
        self.NCL = NCL
        self.F = DM // 128
        self.QF = H * self.HD
        self.KF = HKV * self.HD
        self.NT = S // 256
        self.TPC = self.NT * 128
        self.ST = S // 128
        start = 2 ** (-(2 ** (-(math.log2(H) - 3))))
        self.slopes = [start * start ** i for i in range(H)]
        self.EPS_LN = 1e-5
        self.EPS_RMS = 1e-6
        self.TEMP = 0.1
        self.THRESH = 0.01
        self.NEGBIG = 3.0e32


ENGS = ("pe", "dve", "act", "pool", "sp")


class Buf:
    __slots__ = ("w", "rs", "ws")

    def __init__(self):
        self.w = None
        self.rs = []
        self.ws = {}


class Prog:
    def __init__(self, nc):
        self.nc = nc
        self.ops = {e: [] for e in ENGS}
        self.cnt = {e: 0 for e in ENGS}
        self.esem = {e: nc.alloc_semaphore(f"es_{e}") for e in ENGS}
        self.pe_sem_name = self.esem["pe"].name
        self.wm = {e: {} for e in ENGS}
        self.dsem = {}

    def _waits(self, e, deps):
        best = {}
        for d in deps:
            if d is None:
                continue
            sem, val = d
            k = sem.name
            if e == "pe" and k == self.pe_sem_name:
                continue
            if k not in best or val > best[k][1]:
                best[k] = (sem, val)
        waits = []
        for k, (sem, val) in sorted(best.items()):
            if self.wm[e].get(k, 0) >= val:
                continue
            waits.append((sem, val))
            self.wm[e][k] = val
        return waits

    def _deps(self, r, w, wa, deps):
        out = list(deps)
        for b in r:
            if b.w is not None:
                out.append(b.w)
            out.extend(b.ws.values())
        for b in w:
            if b.w is not None:
                out.append(b.w)
            out.extend(b.ws.values())
            out.extend(b.rs)
        for b in wa:
            pass
        return out

    def _book(self, tok, r, w, wa):
        for b in r:
            b.rs.append(tok)
        for b in w:
            b.w = tok
            b.rs = []
        for b in wa:
            b.ws[tok[0].name] = tok
        return tok

    def do(self, e, fn, r=(), w=(), wa=(), deps=()):
        waits = self._waits(e, self._deps(r, w, wa, deps))
        self.cnt[e] += 1
        tok = (self.esem[e], self.cnt[e])
        self.ops[e].append((waits, fn, self.esem[e], 1))
        return self._book(tok, r, w, wa)

    def dma(self, q, fn, stream, r=(), w=(), wa=(), deps=(), inc=16):
        waits = self._waits(q, self._deps(r, w, wa, deps))
        if stream not in self.dsem:
            self.dsem[stream] = [self.nc.alloc_semaphore("ds_" + stream), 0]
        ent = self.dsem[stream]
        ent[1] += inc
        tok = (ent[0], ent[1])
        self.ops[q].append((waits, fn, ent[0], inc))
        return self._book(tok, r, w, wa)

    def cc(self, fn, stream, r=(), w=(), deps=()):
        return self.dma("pool", fn, stream, r=r, w=w, deps=deps, inc=1)

    def wait_only(self, e, deps):
        waits = self._waits(e, [d for d in deps if d is not None])
        if waits:
            self.ops[e].append((waits, None, None, 0))

    def emit(self):
        nc = self.nc

        def mk(e):
            def body(eng):
                for waits, fn, sem, inc in self.ops[e]:
                    for (s, v) in waits:
                        eng.wait_ge(s, v)
                    if fn is not None:
                        inst = fn(eng)
                        if inst is not None and sem is not None:
                            inst.then_inc(sem, inc)
            return body

        with nc.Block() as block:
            block.sync(mk("sp"))
            block.gpsimd(mk("pool"))
            block.tensor(mk("pe"))
            block.vector(mk("dve"))
            block.scalar(mk("act"))


def build_program(cfg: Cfg, stop_phase=99):
    nc = bass.Bass("TRN2", target_bir_lowering=False)
    DM, S, H, HKV, G, HD = cfg.DM, cfg.S, cfg.H, cfg.HKV, cfg.G, cfg.HD
    F, QF, KF, NT, NCL, ST = cfg.F, cfg.QF, cfg.KF, cfg.NT, cfg.NCL, cfg.ST
    TPC = cfg.TPC
    HC = QF // 128
    KC = KF // 128
    QCG, QW = (QF // 512, 512) if QF >= 512 else (1, QF)
    KCG, KW = (KF // 512, 512) if KF >= 512 else (1, KF)
    DCG, DW = (DM // 512, 512) if DM >= 512 else (1, DM)
    NSL = 4 if F % 4 == 0 else F
    NTK = 2 * NT
    BIGA = HKV * S
    BIGC = max(2 * ST * KF, F * TPC)
    PCNT = min(NT, 8)

    d_x = nc.dram_tensor("xt", [TPC, DM], F32, kind="ExternalInput").ap()
    d_c2 = nc.dram_tensor("c2in", [128, NT * (NTK + 1)], F32, kind="ExternalInput").ap()
    d_msk = nc.dram_tensor("mskin", [128, 3 * 512], BF16, kind="ExternalInput").ap()
    d_wq = nc.dram_tensor("wqT", [DM, QF], F32, kind="ExternalInput").ap()
    d_wk = nc.dram_tensor("wkT", [DM, KF], F32, kind="ExternalInput").ap()
    d_wv = nc.dram_tensor("wvT", [DM, KF], F32, kind="ExternalInput").ap()
    d_wo = nc.dram_tensor("woT", [QF, DM], F32, kind="ExternalInput").ap()
    d_wf = nc.dram_tensor("wfT", [DM, DM], F32, kind="ExternalInput").ap()
    d_ct = nc.dram_tensor("ctT", [DM, NCL], F32, kind="ExternalInput").ap()
    d_bias = nc.dram_tensor("fbias", [1, DM], F32, kind="ExternalInput").ap()
    d_qw = nc.dram_tensor("qnw", [1, HD], F32, kind="ExternalInput").ap()
    d_kw = nc.dram_tensor("knw", [1, HD], F32, kind="ExternalInput").ap()
    d_asn = nc.dram_tensor("asn", [1, DM], I32, kind="ExternalInput").ap()
    d_out = nc.dram_tensor("out", [TPC, DM], F32, kind="ExternalOutput").ap()

    d_xnT = nc.dram_tensor("xnT_s", [128, F * TPC], BF16).ap()
    d_qh = nc.dram_tensor("qh_s", [TPC, QF], BF16).ap()
    d_ot = nc.dram_tensor("ot_s", [HC, 128, NT * 128], BF16).ap()
    d_exi = nc.dram_tensor("exi_s", [TPC, 2 * KF], BF16)
    d_exo = nc.dram_tensor("exo_s", [2 * TPC, 2 * KF], BF16)
    d_cli = nc.dram_tensor("cli_s", [NCL, 1], F32)
    d_clo = nc.dram_tensor("clo_s", [NCL, 1], F32)

    ident_bf = nc.inline_tensor(np.eye(128, dtype=np.float32).astype(BF), "c_ident").ap()
    ident_f32 = nc.inline_tensor(np.eye(128, dtype=np.float32), "c_identf").ap()
    dnp = (np.arange(128)[None, :] - np.arange(128)[:, None]).astype(np.float32)
    d_D = nc.inline_tensor(dnp, "c_D").ap()
    ones_bf = nc.inline_tensor(np.ones((1, 128), dtype=np.float32).astype(BF), "c_ones").ap()
    iota_cl = nc.inline_tensor(np.arange(NCL, dtype=np.float32)[:, None], "c_iota").ap()
    ones_f32r = nc.inline_tensor(np.ones((1, 128), dtype=np.float32), "c_onesf").ap()
    ones_bcol = nc.inline_tensor(np.ones((128, 1), dtype=np.float32).astype(BF), "c_onesc").ap()

    A = nc.alloc_sbuf_tensor
    big = A("bigb", [128, BIGC], BF16)
    argb = A("argb", [128, NTK * 512], F32)

    class _View:
        __slots__ = ("t", "off")

        def __init__(self, t, off):
            self.t, self.off = t, off

        def __getitem__(self, idx):
            p, f = idx
            return self.t[p, f.start + self.off:f.stop + self.off]

    xa = _View(argb, 0)
    xb = _View(argb, DM)
    wsl = [A(f"wsl{i}", [128, NSL * 512], BF16) for i in range(3)]
    qs = [A(f"qs{i}", [128, max(QF, DM, TPC)], BF16) for i in range(2)]
    qt = A("qt", [128, max(QF, DM)], BF16)
    tr = A("tr", [128, max(NTK * 128, F * NCL)], F32)
    sc1 = A("sc1", [128, DM], BF16)
    sc2 = A("sc2", [128, max(DM, QF)], BF16)
    kvst = [A(f"kvst{i}", [128, max(2 * KF, NTK * 128)], BF16) for i in range(2)]
    stg4 = [A(f"stg{i}", [128, 512], BF16) for i in range(4)]
    rsb = A("rsb", [128, DM], BF16)
    CLF = F * NCL
    assert 2 * CLF <= DM
    outst = [A(f"outst{i}", [128, DW], F32) for i in range(2)]
    yf = A("yf", [128, 2 * DW], F32)
    wbck = A("wbck", [128, 128], BF16)
    mskt = A("mskt", [128, 3 * 512], BF16)
    pt = [_View(sc1, 0), _View(sc1, 512), _View(sc1, 1024), _View(sc1, 1536)]
    outsg = [_View(sc1, 2048), _View(sc1, 2560)]
    sgb = _View(sc1, 3072)
    cmx = A("cmx", [128, 2 * NTK], F32)
    biasg = A("biasg", [128, 4 * NTK], F32)
    r1b = A("r1b", [1, 128], F32)
    onesf = A("onesf", [1, 128], F32)
    onescol = A("onescol", [128, 1], BF16)
    smc = A("smc", [128, 16], F32)
    qselc = A("qselc", [128, NT], F32)
    invq = A("invq", [128, 4 * H], F32)
    invk = A("invk", [128, 4 * HKV], F32)
    clm = A("clm", [NCL, 1], F32)
    clsel = A("clsel", [NCL, 1], BF16)
    rowsb = A("rowsb", [1, 4 * 512], BF16)
    qwr = A("qwr", [1, 2 * HD], F32)
    qwrb = A("qwrb", [1, 2 * HD], BF16)
    ic = A("ic", [128, 128], BF16)
    icf = A("icf", [128, 128], F32)
    dtt = A("dtt", [128, 128], F32)
    onr = A("onr", [1, 128], BF16)
    iotc = A("iotc", [NCL, 1], F32)
    c2s = A("c2s", [128, NT * (NTK + 1)], F32)
    asn_i = A("asni", [1, 512], I32)

    PP = [nc.alloc_psum_tensor(f"pp{i}", [128, 512], F32) for i in range(8)]

    P = Prog(nc)
    bBA, bBB = Buf(), Buf()
    bXa, bXb = Buf(), Buf()
    bW = [Buf() for _ in wsl]
    bQs = [Buf(), Buf()]
    bQt = Buf()
    bTr = Buf()
    bS1, bS2 = Buf(), Buf()
    bArg = [Buf() for _ in range(NTK)]
    bPt = [Buf() for _ in range(4)]
    bOsg = [Buf(), Buf()]
    bCm, bBias, bR1, bSg, bR4, bMsk = (Buf() for _ in range(6))
    bKv = [Buf(), Buf()]
    bStg = [Buf() for _ in stg4]
    bRsb = Buf()
    bOut = [Buf(), Buf()]
    bYf = Buf()
    bWbk = Buf()
    bSmc = Buf()
    bQsel = Buf()
    bInvq, bInvk = Buf(), Buf()
    bClm, bClsel = Buf(), Buf()
    bRowsb = Buf()
    bQwr, bQwrb = Buf(), Buf()
    bConst = Buf()
    bC2 = Buf()
    bAsn = Buf()
    bP = [Buf() for _ in PP]
    bDxnT, bDqh, bDot, bDexi, bDexo, bDcli, bDclo, bDout = (Buf() for _ in range(8))

    def xn_buf(c):
        return bBA if c < F // 2 else bBB

    def MM(out_ap, lhsT_ap, rhs_ap, start, stop):
        def fn(e, a=out_ap, b=lhsT_ap, c=rhs_ap, s1=start, s2=stop):
            return e.matmul(a, b, c, start=s1, stop=s2)
        return fn

    P.dma("pool", lambda e: e.dma_start(ic[:, :], ident_bf), "cst1", w=[bConst])
    P.dma("pool", lambda e: e.dma_start(icf[:, :], ident_f32), "cst2", w=[bConst])
    P.dma("pool", lambda e: e.dma_start(dtt[:, :], d_D), "cst3", w=[bConst])
    P.dma("pool", lambda e: e.dma_start(onr[:, :], ones_bf), "cst4", w=[bConst])
    P.dma("pool", lambda e: e.dma_start(iotc[:, :], iota_cl), "cst5", w=[bConst])
    P.dma("pool", lambda e: e.dma_start(c2s[:, :], d_c2), "cst6", w=[bC2])
    P.dma("pool", lambda e: e.dma_start(mskt[:, :], d_msk), "cst12", w=[bMsk])
    P.dma("pool", lambda e: e.dma_start(onesf[:, :], ones_f32r), "cst13", w=[bConst])
    P.dma("pool", lambda e: e.dma_start(onescol[:, :], ones_bcol), "cst14", w=[bConst])
    P.dma("pool", lambda e: e.dma_start(qwr[:, 0:HD], d_qw), "cst8", w=[bQwr])
    P.dma("pool", lambda e: e.dma_start(qwr[:, HD:2 * HD], d_kw), "cst9", w=[bQwr])
    P.dma("pool", lambda e: e.dma_start(
        tr[:, 0:F * NCL].rearrange("p (c n) -> p c n", c=F),
        d_ct.rearrange("(c p) n -> p c n", p=128)), "cst11", w=[bTr])

    P.do("act", lambda e: e.activation(rsb[:, 0:CLF], tr[:, 0:F * NCL], AF.Copy),
         r=[bTr], w=[bRsb])
    P.do("dve", lambda e: e.tensor_tensor(rsb[:, CLF:2 * CLF], tr[:, 0:F * NCL], rsb[:, 0:CLF],
                                          ALU.subtract), r=[bTr, bRsb], w=[bRsb])
    P.do("dve", lambda e: e.tensor_copy(qwrb[:, HD:2 * HD], qwr[:, HD:2 * HD]),
         r=[bQwr], w=[bQwrb])
    P.do("pe", MM(PP[7][:, 0:128], onr[:, :], qwrb[:, HD:2 * HD], True, True),
         r=[bConst, bQwrb], w=[bP[7]])
    P.do("act", lambda e: e.activation(wbck[:, :], PP[7][:, 0:128], AF.Copy), r=[bP[7]], w=[bWbk])

    def _finish0():
        toks = []
        for name, (sem, val) in P.dsem.items():
            if val:
                toks.append((sem, val))
        P.wait_only("sp", toks)
        P.emit()
        return nc

    if stop_phase <= 0:
        return _finish0()
    NT1 = NT if stop_phase > 90 or stop_phase >= 1.5 else NT
    import os
    SUB = int(os.environ.get("KSUB", "99"))
    for m in range(NT if SUB >= 10 else 1):
        P.dma("sp", lambda e, m=m: e.dma_start(xa[:, 0:DM], d_x[m * 128:(m + 1) * 128, :]),
              "x0", w=[bXa])
        P.do("dve", lambda e: e.reduce_sum(smc[:, 0:1], xa[:, 0:DM], AXX), r=[bXa], w=[bSmc])
        P.do("dve", lambda e: e.tensor_scalar(smc[:, 1:2], smc[:, 0:1], 1.0 / DM, None, ALU.mult),
             r=[bSmc], w=[bSmc])
        P.do("act", lambda e: e.activation(sc2[:, 0:DM], xa[:, 0:DM], AF.Square,
                                           bias=smc[:, 1:2], scale=-1.0, accum_out=smc[:, 2:3]),
             r=[bXa, bSmc], w=[bS2, bSmc])
        P.do("dve", lambda e: e.tensor_scalar(smc[:, 3:4], smc[:, 2:3], 1.0 / DM, cfg.EPS_LN,
                                              ALU.mult, ALU.add), r=[bSmc], w=[bSmc])
        P.do("act", lambda e: e.activation(smc[:, 4:5], smc[:, 3:4], AF.Ln), r=[bSmc], w=[bSmc])
        P.do("act", lambda e: e.activation(smc[:, 5:6], smc[:, 4:5], AF.Exp, scale=-0.5),
             r=[bSmc], w=[bSmc])
        P.do("dve", lambda e: e.tensor_scalar(xb[:, 0:DM], xa[:, 0:DM], smc[:, 1:2], smc[:, 5:6],
                                              ALU.subtract, ALU.mult), r=[bXa, bSmc], w=[bXb])
        P.do("act", lambda e: e.activation(sc2[:, 0:DM], xb[:, 0:DM], AF.Copy), r=[bXb], w=[bS2])
        P.do("dve", lambda e: e.tensor_tensor(sc1[:, 0:DM], xb[:, 0:DM], sc2[:, 0:DM],
                                              ALU.subtract), r=[bXb, bS2], w=[bS1])
        if SUB < 2:
            continue
        for cb in range(0, F, 4):
            nb = min(4, F - cb)
            pih = (cb // 4) % 2
            pil = 2 + (cb // 4) % 2

            def fn_hi(e, cb=cb, nb=nb, pih=pih):
                last = None
                for i in range(nb):
                    last = e.matmul(PP[pih][:, i * 128:(i + 1) * 128],
                                    sc2[:, (cb + i) * 128:(cb + i + 1) * 128], ic[:, :],
                                    start=True, stop=True)
                return last
            P.do("pe", fn_hi, r=[bS2, bConst], w=[bP[pih]])

            def fn_lo(e, cb=cb, nb=nb, pil=pil):
                last = None
                for i in range(nb):
                    last = e.matmul(PP[pil][:, i * 128:(i + 1) * 128],
                                    sc1[:, (cb + i) * 128:(cb + i + 1) * 128], ic[:, :],
                                    start=True, stop=True)
                return last
            P.do("pe", fn_lo, r=[bS1, bConst], w=[bP[pil]])
            for i in range(nb):
                c = cb + i
                P.do("act", lambda e, c=c, i=i, pih=pih, m=m:
                     e.activation(big[:, c * TPC + m * 128:c * TPC + (m + 1) * 128],
                                  PP[pih][:, i * 128:(i + 1) * 128], AF.Copy),
                     r=[bP[pih]], w=[xn_buf(c)])
            P.do("act", lambda e, cb=cb, nb=nb, pil=pil:
                 e.activation(qs[0][:, cb * 128:(cb + nb) * 128], PP[pil][:, 0:nb * 128], AF.Copy),
                 r=[bP[pil]], w=[bQs[0]])

        if SUB < 3:
            continue

        def fn_cent(e, m=m):
            last = None
            for c in range(F):
                hiT = big[:, c * TPC + m * 128:c * TPC + (m + 1) * 128]
                loT = qs[0][:, c * 128:(c + 1) * 128]
                last = e.matmul(PP[7][:, 0:NCL], hiT, rsb[:, c * NCL:(c + 1) * NCL],
                                start=(c == 0), stop=False)
                last = e.matmul(PP[7][:, 0:NCL], hiT, rsb[:, CLF + c * NCL:CLF + (c + 1) * NCL],
                                start=False, stop=False)
                last = e.matmul(PP[7][:, 0:NCL], loT, rsb[:, c * NCL:(c + 1) * NCL],
                                start=False, stop=(c == F - 1))
            return last
        P.do("pe", fn_cent, r=[bBA, bBB, bQs[0], bRsb], w=[bP[7]])
        if SUB < 4:
            continue
        SUB2 = int(os.environ.get("KSUB2", "99"))
        P.do("dve", lambda e: e.reduce_max(smc[:, 6:7], PP[7][:, 0:NCL], AXX, negate=True),
             r=[bP[7]], w=[bSmc])
        if SUB2 < 1:
            continue
        P.do("dve", lambda e: e.tensor_scalar(smc[:, 7:8], smc[:, 6:7], 1.0 / cfg.TEMP, None,
                                              ALU.mult), r=[bSmc], w=[bSmc])
        if SUB2 < 2:
            continue
        EXPV = int(os.environ.get("KEXPV", "0"))
        if EXPV == 1:
            P.do("act", lambda e: e.activation(tr[:, 0:NCL], PP[7][:, 0:NCL], AF.Exp,
                                               bias=smc[:, 7:8], scale=1.0 / cfg.TEMP),
                 r=[bP[7], bSmc], w=[bTr, bSmc])
        elif EXPV == 2:
            P.do("act", lambda e: e.activation(tr[:, 0:NCL], PP[7][:, 0:NCL], AF.Exp,
                                               scale=1.0 / cfg.TEMP, accum_out=smc[:, 8:9]),
                 r=[bP[7], bSmc], w=[bTr, bSmc])
        elif EXPV == 3:
            P.do("dve", lambda e: e.tensor_copy(xb[:, 0:NCL], PP[7][:, 0:NCL]), r=[bP[7]], w=[bXb])
            P.do("act", lambda e: e.activation(tr[:, 0:NCL], xb[:, 0:NCL], AF.Exp,
                                               bias=smc[:, 7:8], scale=1.0 / cfg.TEMP,
                                               accum_out=smc[:, 8:9]),
                 r=[bXb, bSmc], w=[bTr, bSmc])
        elif EXPV == 5:
            P.do("act", lambda e: e.activation(smc[:, 8:9], smc[:, 7:8], AF.Copy),
                 r=[bSmc], w=[bSmc])
        elif EXPV == 6:
            P.do("act", lambda e: e.activation(sc1[:, 0:NCL], PP[7][:, 0:NCL], AF.Exp,
                                               bias=smc[:, 7:8], scale=1.0 / cfg.TEMP,
                                               accum_out=smc[:, 8:9]),
                 r=[bP[7], bSmc], w=[bS1, bSmc])
        elif EXPV == 4:
            P.do("act", lambda e: e.activation(tr[:, 0:NCL], PP[7][:, 0:NCL], AF.Copy,
                                               bias=smc[:, 7:8], scale=1.0 / cfg.TEMP,
                                               accum_out=smc[:, 8:9]),
                 r=[bP[7], bSmc], w=[bTr, bSmc])
        else:
            P.do("act", lambda e: e.activation(tr[:, 0:NCL], PP[7][:, 0:NCL], AF.Exp,
                                               bias=smc[:, 7:8], scale=1.0 / cfg.TEMP,
                                               accum_out=smc[:, 8:9]),
                 r=[bP[7], bSmc], w=[bTr, bSmc])
        if SUB2 < 3:
            continue
        P.do("dve", lambda e, m=m: e.tensor_scalar(qselc[:, m:m + 1], smc[:, 8:9],
                                                   1.0 / cfg.THRESH, None, ALU.is_lt),
             r=[bSmc], w=[bQsel])
        if SUB2 < 4:
            continue
        P.do("dve", lambda e: e.reciprocal(smc[:, 9:10], smc[:, 8:9]), r=[bSmc], w=[bSmc])
        if SUB2 < 5:
            continue
        P.do("dve", lambda e: e.tensor_scalar(tr[:, 0:NCL], tr[:, 0:NCL], smc[:, 9:10], None,
                                              ALU.mult), r=[bSmc], w=[bTr])
        if SUB < 5:
            continue
        P.do("pe", lambda e: e.transpose(PP[6][0:NCL, 0:128], tr[:, 0:NCL], icf[:, :]),
             r=[bTr, bConst], w=[bP[6]])
        P.do("dve", lambda e: e.reduce_max(smc[0:NCL, 10:11], PP[6][0:NCL, 0:128], AXX),
             r=[bP[6]], w=[bSmc])
        if m == 0:
            P.do("dve", lambda e: e.tensor_copy(clm[:, :], smc[0:NCL, 10:11]), r=[bSmc], w=[bClm])
        else:
            P.do("dve", lambda e: e.tensor_tensor(clm[:, :], clm[:, :], smc[0:NCL, 10:11],
                                                  ALU.max), r=[bSmc, bClm], w=[bClm])

    for i in range(4):
        Wc = F * TPC // 4
        P.dma("sp", lambda e, i=i, Wc=Wc: e.dma_start(d_xnT[:, i * Wc:(i + 1) * Wc],
                                                      big[:, i * Wc:(i + 1) * Wc]),
              f"xnw{i}", r=[bBA, bBB], wa=[bDxnT])
    P.dma("sp", lambda e: e.dma_start(d_cli.ap(), clm[:, :]), "clw", r=[bClm], wa=[bDcli])

    slab_ctr = [0]

    def load_slab(dram_ap, c0, nchunks, col0, width, stream):
        si = slab_ctr[0] % len(wsl)
        slab_ctr[0] += 1
        t, b = wsl[si], bW[si]
        stream = f"wsl{si}"
        src = dram_ap[c0 * 128:(c0 + nchunks) * 128, col0:col0 + width]
        P.dma("pool", lambda e, t=t, src=src, nchunks=nchunks, width=width:
              e.dma_start(t[:, 0:nchunks * width].rearrange("p (c n) -> p c n", c=nchunks),
                          src.rearrange("(c p) n -> p c n", p=128)),
              stream, w=[b])
        return t, b

    def proj_phase(dram_w, ncg, cw, nchunks_total, out_cb, which, after_first_slab=None):
        for cg in range(ncg):
            for c0 in range(0, F, NSL):
                t, b = load_slab(dram_w, c0, NSL, cg * cw, cw, "w" + which)
                if after_first_slab is not None:
                    after_first_slab()
                    after_first_slab = None
                for m in range(NT):
                    def fn(e, c0=c0, t=t, m=m, cw=cw):
                        last = None
                        for i in range(NSL):
                            c = c0 + i
                            last = e.matmul(PP[m % 8][:, 0:cw],
                                            big[:, c * TPC + m * 128:c * TPC + (m + 1) * 128],
                                            t[:, i * cw:(i + 1) * cw],
                                            start=(c == 0), stop=(c == F - 1))
                        return last
                    P.do("pe", fn, r=[bBA, bBB, b], w=[bP[m % 8]])
            for m in range(NT):
                out_cb(m, cg, PP[m % 8], bP[m % 8])

    def _finish():
        toks = []
        for name, (sem, val) in P.dsem.items():
            if val:
                toks.append((sem, val))
        P.wait_only("sp", toks)
        P.emit()
        return nc

    if stop_phase <= 1:
        return _finish()
    def kv_evac(kind):
        def cb(m, cg, psum, pbuf):
            nh = KW // 128
            if kind == "v":
                for hb in range(nh):
                    si = (m * nh + hb) % 4
                    kvh = cg * nh + hb
                    P.do("dve", lambda e, si=si, hb=hb, psum=psum:
                         e.tensor_copy(stg4[si][:, 0:128], psum[:, hb * 128:(hb + 1) * 128]),
                         r=[pbuf], w=[bStg[si]])
                    P.dma("sp", lambda e, si=si, m=m, kvh=kvh:
                          e.dma_start(d_exi.ap()[m * 128:(m + 1) * 128, KF + kvh * 128:KF + (kvh + 1) * 128],
                                      stg4[si][:, 0:128]),
                          f"stg{si}", r=[bStg[si]], wa=[bDexi])
            else:
                for hb in range(nh):
                    kvh = cg * nh + hb
                    P.do("act", lambda e, hb=hb, kvh=kvh, psum=psum:
                         e.activation(sc1[:, 0:128], psum[:, hb * 128:(hb + 1) * 128], AF.Square,
                                      accum_out=invk[:, kvh:kvh + 1]),
                         r=[pbuf], w=[bS1, bInvk])
                h0 = cg * nh
                P.do("dve", lambda e, h0=h0, nh=nh:
                     e.tensor_scalar(invk[:, HKV + h0:HKV + h0 + nh], invk[:, h0:h0 + nh],
                                     1.0 / HD, cfg.EPS_RMS, ALU.mult, ALU.add),
                     r=[bInvk], w=[bInvk])
                P.do("act", lambda e, h0=h0, nh=nh:
                     e.activation(invk[:, 2 * HKV + h0:2 * HKV + h0 + nh],
                                  invk[:, HKV + h0:HKV + h0 + nh], AF.Ln), r=[bInvk], w=[bInvk])
                P.do("act", lambda e, h0=h0, nh=nh:
                     e.activation(invk[:, 3 * HKV + h0:3 * HKV + h0 + nh],
                                  invk[:, 2 * HKV + h0:2 * HKV + h0 + nh], AF.Exp, scale=-0.5),
                     r=[bInvk], w=[bInvk])
                for hb in range(nh):
                    si = (m * nh + hb) % 4
                    kvh = cg * nh + hb
                    P.do("dve", lambda e, si=si, hb=hb, kvh=kvh, psum=psum:
                         e.scalar_tensor_tensor(stg4[si][:, 0:128], psum[:, hb * 128:(hb + 1) * 128],
                                                invk[:, 3 * HKV + kvh:3 * HKV + kvh + 1],
                                                wbck[:, :], ALU.mult, ALU.mult),
                         r=[pbuf, bInvk, bWbk], w=[bStg[si]])
                    P.dma("sp", lambda e, si=si, m=m, kvh=kvh:
                          e.dma_start(d_exi.ap()[m * 128:(m + 1) * 128, kvh * 128:(kvh + 1) * 128],
                                      stg4[si][:, 0:128]),
                          f"stg{si}", r=[bStg[si]], wa=[bDexi])
        return cb

    proj_phase(d_wk, KCG, KW, KC, kv_evac("k"), "k")
    proj_phase(d_wv, KCG, KW, KC, kv_evac("v"), "v")

    if stop_phase <= 2:
        return _finish()
    for h in range(H):
        sc = 1.0 / (math.sqrt(HD) * cfg.slopes[h])
        P.do("dve", lambda e, sc=sc: e.tensor_scalar(qwrb[:, 0:HD], qwr[:, 0:HD], sc, None,
                                                     ALU.mult), r=[bQwr], w=[bQwrb])
        pi = 4 + (h % 2)
        P.do("pe", MM(PP[pi][:, 0:128], onr[:, :], qwrb[:, 0:HD], True, True),
             r=[bConst, bQwrb], w=[bP[pi]])
        P.do("act", lambda e, h=h, pi=pi: e.activation(sc2[:, h * 128:(h + 1) * 128],
                                                       PP[pi][:, 0:128], AF.Copy),
             r=[bP[pi]], w=[bS2])

    def q_evac(m, cg, psum, pbuf):
        nh = QW // 128
        for hb in range(nh):
            h = cg * nh + hb
            P.do("act", lambda e, hb=hb, h=h, psum=psum:
                 e.activation(sc1[:, 0:128], psum[:, hb * 128:(hb + 1) * 128], AF.Square,
                              accum_out=invq[:, h:h + 1]), r=[pbuf], w=[bS1, bInvq])
        h0 = cg * nh
        P.do("dve", lambda e, h0=h0, nh=nh:
             e.tensor_scalar(invq[:, H + h0:H + h0 + nh], invq[:, h0:h0 + nh], 1.0 / HD,
                             cfg.EPS_RMS, ALU.mult, ALU.add), r=[bInvq], w=[bInvq])
        P.do("act", lambda e, h0=h0, nh=nh:
             e.activation(invq[:, 2 * H + h0:2 * H + h0 + nh], invq[:, H + h0:H + h0 + nh], AF.Ln),
             r=[bInvq], w=[bInvq])
        P.do("act", lambda e, h0=h0, nh=nh:
             e.activation(invq[:, 3 * H + h0:3 * H + h0 + nh], invq[:, 2 * H + h0:2 * H + h0 + nh],
                          AF.Exp, scale=-0.5), r=[bInvq], w=[bInvq])
        for hb in range(nh):
            si = (m * nh + hb) % 4
            h = cg * nh + hb
            P.do("dve", lambda e, si=si, hb=hb, h=h, psum=psum:
                 e.scalar_tensor_tensor(stg4[si][:, 0:128], psum[:, hb * 128:(hb + 1) * 128],
                                        invq[:, 3 * H + h:3 * H + h + 1],
                                        sc2[:, h * 128:(h + 1) * 128], ALU.mult, ALU.mult),
                 r=[pbuf, bInvq, bS2], w=[bStg[si]])
            P.dma("sp", lambda e, si=si, m=m, h=h:
                  e.dma_start(d_qh[m * 128:(m + 1) * 128, h * 128:(h + 1) * 128], stg4[si][:, 0:128]),
                  f"stg{si}", r=[bStg[si]], wa=[bDqh])

    def fn_ar(e):
        return e.collective_compute("AllReduce", ALU.max,
                                    replica_groups=[[0, 1, 2, 3, 4, 5, 6, 7]],
                                    ins=[d_cli.ap().opt()], outs=[d_clo.ap().opt()])

    def fn_ag(e):
        return e.collective_compute("AllGather", ALU.bypass,
                                    replica_groups=[[0, 1], [2, 3], [4, 5], [6, 7]],
                                    ins=[d_exi.ap().opt()], outs=[d_exo.ap().opt()])

    def emit_ccs():
        P.cc(fn_ag, "ccag", r=[bDexi], w=[bDexo])
        P.cc(fn_ar, "ccar", r=[bDcli], w=[bDclo])

    proj_phase(d_wq, QCG, QW, HC, q_evac, "q", after_first_slab=emit_ccs)

    if stop_phase <= 3:
        return _finish()
    for t in range(ST):
        rb = (t % 2) * TPC + (t // 2) * 128
        st, sb = kvst[t % 2], bKv[t % 2]
        P.dma("sp", lambda e, st=st, rb=rb: e.dma_start(st[:, 0:KF],
                                                        d_exo.ap()[rb:rb + 128, 0:KF]),
              f"exr{t % 2}", r=[bDexo], w=[sb])
        P.dma("sp", lambda e, t=t, rb=rb: e.dma_start(big[:, BIGA + t * KF:BIGA + (t + 1) * KF],
                                                      d_exo.ap()[rb:rb + 128, KF:2 * KF]),
              "exrv", r=[bDexo], w=[bBB])
        for kb in range(0, KC, 4):
            nb = min(4, KC - kb)
            pi = (kb // 4) % 2

            def fn_tr(e, st=st, kb=kb, nb=nb, pi=pi):
                last = None
                for i in range(nb):
                    last = e.matmul(PP[pi][:, i * 128:(i + 1) * 128],
                                    st[:, (kb + i) * 128:(kb + i + 1) * 128], ic[:, :],
                                    start=True, stop=True)
                return last
            P.do("pe", fn_tr, r=[sb, bConst], w=[bP[pi]])
            for i in range(nb):
                kv = kb + i
                P.do("act", lambda e, kv=kv, t=t, pi=pi, i=i:
                     e.activation(big[:, kv * S + t * 128:kv * S + (t + 1) * 128],
                                  PP[pi][:, i * 128:(i + 1) * 128], AF.Copy),
                     r=[bP[pi]], w=[bBA])

    if stop_phase <= 35:
        return _finish()
    fxab = []
    for _b in (bXa, bXb, bTr, bS1):
        if _b.w is not None:
            fxab.append(_b.w)
        fxab.extend(_b.rs)
        fxab.extend(_b.ws.values())
    r4bt = tr
    DTT0, MSKA, MSKB = 0, 512, 1024

    for j in range(NT):
        stiles = 2 * j + 2
        stg, sbq = qs[j % 2], bQs[j % 2]
        if j == 0:
            P.dma("sp", lambda e, stg=stg: e.dma_start(stg[:, 0:QF], d_qh[0:128, :]),
                  "qhr0", r=[bDqh], w=[sbq])
        for cb in range(0, HC, 4):
            pi = 6 + (cb // 4) % 2

            def fn_trq(e, stg=stg, cb=cb, pi=pi):
                last = None
                for i in range(4):
                    last = e.matmul(PP[pi][:, i * 128:(i + 1) * 128],
                                    stg[:, (cb + i) * 128:(cb + i + 1) * 128], ic[:, :],
                                    start=True, stop=True)
                return last
            P.do("pe", fn_trq, r=[sbq, bConst], w=[bP[pi]])
            P.do("act", lambda e, cb=cb, pi=pi:
                 e.activation(qt[:, cb * 128:(cb + 4) * 128], PP[pi][:, 0:512], AF.Copy),
                 r=[bP[pi]], w=[bQt])
        if j + 1 < NT:
            stg2, sbq2 = qs[(j + 1) % 2], bQs[(j + 1) % 2]
            P.dma("sp", lambda e, stg2=stg2, j=j:
                  e.dma_start(stg2[:, 0:QF], d_qh[(j + 1) * 128:(j + 2) * 128, :]),
                  f"qhr{(j + 1) % 2}", r=[bDqh], w=[sbq2])

        cc0 = j * (NTK + 1)
        P.do("pe", lambda e, cc0=cc0, stiles=stiles:
             e.matmul(PP[6][:, 0:stiles], onesf[:, :], c2s[0:1, cc0:cc0 + stiles],
                      start=True, stop=True),
             r=[bConst, bC2], w=[bP[6]])
        P.do("dve", lambda e, stiles=stiles:
             e.tensor_copy(cmx[:, NTK:NTK + stiles], PP[6][:, 0:stiles]),
             r=[bP[6]], w=[bCm])

        def emit_sttr(g, kt, j=j):
            pb = kt % 2
            mo = MSKA if kt == 2 * j else (MSKB if kt == 2 * j + 1 else DTT0)
            P.do("pe", MM(PP[pb][:, 0:512],
                          big[:, g * S + kt * 128:g * S + (kt + 1) * 128],
                          qt[:, g * 512:(g + 1) * 512], True, True),
                 r=[bBA, bQt], w=[bP[pb]])
            P.do("dve", lambda e, kt=kt, pb=pb, mo=mo:
                 e.scalar_tensor_tensor(argb[:, kt * 512:(kt + 1) * 512],
                                        PP[pb][:, 0:512], cmx[:, NTK + kt:NTK + kt + 1],
                                        mskt[:, mo:mo + 512], ALU.add, ALU.add),
                 r=[bP[pb], bMsk, bCm], w=[bArg[kt]], deps=fxab)
            P.do("dve", lambda e, kt=kt:
                 e.reduce_max(cmx[:, kt:kt + 1], argb[:, kt * 512:(kt + 1) * 512],
                              AXX),
                 r=[bArg[kt]], w=[bCm])

        def emit_b1tail(g, j=j, stiles=stiles):
            P.do("dve", lambda e, stiles=stiles:
                 e.reduce_max(smc[:, 0:1], cmx[:, 0:stiles], AXX),
                 r=[bCm], w=[bSmc])

        def emit_b2(g, j=j, stiles=stiles):
            P.do("pe", lambda e: e.matmul(PP[6][0:1, 0:128], smc[:, 0:1], icf[:, :],
                                          start=True, stop=True),
                 r=[bSmc, bConst], w=[bP[6]])
            P.do("dve", lambda e: e.reduce_max(r1b[0:1, 32:33], PP[6][0:1, 0:128], AXX),
                 r=[bP[6]], w=[bR1])
            for i in range(4):
                h = 4 * g + i
                P.do("dve", lambda e, i=i, h=h:
                     e.tensor_scalar(r1b[0:1, 64 + i:64 + i + 1],
                                     r1b[0:1, 32:33], -cfg.slopes[h], None, ALU.mult),
                     r=[bR1], w=[bR1])
            P.do("pe", lambda e: e.matmul(PP[6][:, 0:4], onesf[:, :], r1b[0:1, 64:68],
                                          start=True, stop=True),
                 r=[bR1, bConst], w=[bP[6]])
            P.do("dve", lambda e: e.tensor_copy(biasg[:, 0:4], PP[6][:, 0:4]),
                 r=[bP[6]], w=[bBias])

        def emit_cexp(g, kt0, j=j, stiles=stiles):
            qn = min(4, stiles - kt0)
            for i in range(4):
                h = 4 * g + i
                inv = argb[:, kt0 * 512:(kt0 + qn) * 512].rearrange(
                    "p (u c) -> p u c", u=qn)[:, :, i * 128:(i + 1) * 128]
                outv = sc1[:, 0:qn * 512].rearrange(
                    "p (u c) -> p u c", u=qn)[:, :, i * 128:(i + 1) * 128]
                P.do("act", lambda e, inv=inv, outv=outv, h=h:
                     e.activation(outv, inv, AF.Exp, bias=biasg[:, h - 4 * g:h - 4 * g + 1],
                                  scale=cfg.slopes[h]),
                     r=[bArg[u] for u in range(kt0, kt0 + qn)] + [bBias],
                     w=[bPt[u] for u in range(qn)], deps=fxab)

        def emit_ck(g, kt, j=j, stiles=stiles):
            po, ps = 2 + (g % 2), 4 + (g % 2)
            tp = kt % 4
            P.do("pe", lambda e, kt=kt, tp=tp, ps=ps, stiles=stiles:
                 e.matmul(PP[ps][0:1, 0:512], onescol[:, :], pt[tp][:, 0:512],
                          start=(kt == 0), stop=(kt == stiles - 1)),
                 r=[bPt[tp], bConst], w=[bP[ps]])
            P.do("pe", lambda e, kt=kt, tp=tp, po=po, g=g, stiles=stiles:
                 e.matmul(PP[po][:, 0:512],
                          big[:, BIGA + kt * KF + g * 128:BIGA + kt * KF + g * 128 + 128],
                          pt[tp][:, 0:512],
                          start=(kt == 0), stop=(kt == stiles - 1)),
                 r=[bPt[tp], bBB], w=[bP[po]])

        def emit_d(g, j=j):
            po, ps = 2 + (g % 2), 4 + (g % 2)
            og = g % 2
            P.do("dve", lambda e, ps=ps: e.tensor_copy(sgb[0:1, 0:512], PP[ps][0:1, 0:512]),
                 r=[bP[ps]], w=[bSg], deps=fxab)
            P.do("pe", lambda e: e.matmul(PP[7][:, 0:512], onr[:, :], sgb[0:1, 0:512],
                                          start=True, stop=True),
                 r=[bSg, bConst], w=[bP[7]])
            P.do("dve", lambda e: e.reciprocal(r4bt[:, 512:1024], PP[7][:, 0:512]),
                 r=[bP[7]], w=[bR4], deps=fxab)
            P.do("dve", lambda e, po=po, og=og:
                 e.tensor_tensor(outsg[og][:, 0:512], PP[po][:, 0:512], r4bt[:, 512:1024],
                                 ALU.mult),
                 r=[bP[po], bR4], w=[bOsg[og]], deps=fxab)
            for i in range(4):
                h = 4 * g + i
                P.dma("sp", lambda e, h=h, j=j, i=i, og=og:
                      e.dma_start(d_ot[h, :, j * 128:(j + 1) * 128],
                                  outsg[og][:, i * 128:(i + 1) * 128]),
                      f"otw{og}", r=[bOsg[og]], wa=[bDot])

        for g in range(HKV):
            for kt in range(stiles):
                if g > 0:
                    if kt % 4 == 0:
                        emit_cexp(g - 1, kt)
                    emit_ck(g - 1, kt)
                emit_sttr(g, kt)
            emit_b1tail(g)
            emit_b2(g)
            if g > 1:
                emit_d(g - 2)
        for kt in range(stiles):
            if kt % 4 == 0:
                emit_cexp(HKV - 1, kt)
            emit_ck(HKV - 1, kt)
        emit_d(HKV - 2)
        emit_d(HKV - 1)

    if stop_phase <= 4:
        return _finish()
    for i in range(4):
        Wc = F * TPC // 4
        P.dma("sp", lambda e, i=i, Wc=Wc: e.dma_start(big[:, i * Wc:(i + 1) * Wc],
                                                      d_xnT[:, i * Wc:(i + 1) * Wc]),
              f"xnr{i}", r=[bDxnT], w=[bBA] if i < 2 else [bBB])
    P.dma("sp", lambda e: e.dma_start(clm[:, :], d_clo.ap()), "clr", r=[bDclo], w=[bClm])
    P.do("dve", lambda e: e.tensor_scalar(clsel[:, :], clm[:, :], cfg.THRESH, None, ALU.is_gt),
         r=[bClm], w=[bClsel])
    for gc in range(DCG):
        P.dma("pool", lambda e, gc=gc: e.dma_start(asn_i[:, 0:DW], d_asn[:, gc * DW:(gc + 1) * DW]),
              "asnl", r=[bAsn], w=[bAsn])
        P.do("dve", lambda e: e.tensor_copy(yf[0:1, 0:DW], asn_i[:, 0:DW]), r=[bAsn], w=[bYf])
        P.do("dve", lambda e: e.tensor_copy(rowsb[:, 1024:1024 + DW], yf[0:1, 0:DW]),
             r=[bYf], w=[bRowsb])
        P.do("pe", MM(PP[2][0:NCL, 0:DW], onr[:, 0:NCL], rowsb[:, 1024:1024 + DW], True, True),
             r=[bConst, bRowsb], w=[bP[2]])
        P.do("dve", lambda e, gc=gc: e.tensor_scalar(qt[0:NCL, gc * DW:(gc + 1) * DW],
                                                     PP[2][0:NCL, 0:DW], iotc[:, :], None,
                                                     ALU.is_equal),
             r=[bP[2], bConst], w=[bQt])
        P.do("pe", lambda e, gc=gc: e.matmul(PP[3][0:1, 0:DW], clsel[:, :],
                                             qt[0:NCL, gc * DW:(gc + 1) * DW],
                                             start=True, stop=True),
             r=[bClsel, bQt], w=[bP[3]])
        P.do("dve", lambda e: e.tensor_copy(rowsb[:, 1536:1536 + DW], PP[3][0:1, 0:DW]),
             r=[bP[3]], w=[bRowsb])
        P.do("pe", lambda e: e.matmul(PP[2][:, 0:DW], onr[:, :], rowsb[:, 1536:1536 + DW],
                                      start=True, stop=True),
             r=[bConst, bRowsb], w=[bP[2]])
        P.do("dve", lambda e, gc=gc: e.tensor_copy(rsb[:, gc * DW:(gc + 1) * DW], PP[2][:, 0:DW]),
             r=[bP[2]], w=[bRsb])

    fx5 = []
    for _b in bArg:
        if _b.w is not None:
            fx5.append(_b.w)
        fx5.extend(_b.rs)
        fx5.extend(_b.ws.values())
    for cg in range(DCG):
        P.dma("sp", lambda e, cg=cg: e.dma_start(
            xa[:, 0:NT * DW].rearrange("p (m n) -> p m n", m=NT),
            d_x[:, cg * DW:(cg + 1) * DW].rearrange("(m p) n -> p m n", p=128)),
            "xs", w=[bXa], deps=fx5)
        P.dma("pool", lambda e, cg=cg: e.dma_start(yf[0:1, 0:DW], d_bias[:, cg * DW:(cg + 1) * DW]),
              "bil", r=[bYf], w=[bYf])
        P.do("act", lambda e: e.activation(rowsb[:, 0:DW], yf[0:1, 0:DW], AF.Copy),
             r=[bYf], w=[bRowsb])
        P.do("dve", lambda e: e.tensor_tensor(rowsb[:, 512:512 + DW], yf[0:1, 0:DW],
                                              rowsb[:, 0:DW], ALU.subtract),
             r=[bYf, bRowsb], w=[bRowsb])
        allP = [bP[mm % 8] for mm in range(PCNT)]
        for c0 in range(0, HC, NSL):
            ns = min(NSL, HC - c0)
            t, b = load_slab(d_wo, c0, ns, cg * DW, DW, "wo")
            for i in range(ns):
                c = c0 + i
                stg, sbo = qs[c % 2], bQs[c % 2]
                P.dma("sp", lambda e, stg=stg, c=c: e.dma_start(
                    stg[:, 0:TPC], d_ot[c, :, :]),
                    f"otr{c % 2}", r=[bDot], w=[sbo])

                def fn(e, stg=stg, t=t, i=i, c=c):
                    last = None
                    for m in range(NT):
                        last = e.matmul(PP[m % 8][:, 0:DW], stg[:, m * 128:(m + 1) * 128],
                                        t[:, i * DW:(i + 1) * DW],
                                        start=(c == 0), stop=(c == HC - 1))
                    return last
                P.do("pe", fn, r=[sbo, b], w=allP)
        for m in range(NT):
            P.do("dve", lambda e, m=m: e.tensor_copy(xb[:, m * DW:(m + 1) * DW],
                                                     PP[m % 8][:, 0:DW]),
                 r=[bP[m % 8]], w=[bXb], deps=fx5)
        for c0 in range(0, F, NSL):
            t, b = load_slab(d_wf, c0, NSL, cg * DW, DW, "wf")
            for i in range(NSL):
                c = c0 + i

                def fn(e, t=t, i=i, c=c):
                    last = None
                    for m in range(NT):
                        last = e.matmul(PP[m % 8][:, 0:DW],
                                        big[:, c * TPC + m * 128:c * TPC + (m + 1) * 128],
                                        t[:, i * DW:(i + 1) * DW],
                                        start=(c == 0), stop=False)
                    return last
                P.do("pe", fn, r=[bBA, bBB, b], w=allP)

        def fn_bias(e, cg=cg):
            last = None
            for m in range(NT):
                last = e.matmul(PP[m % 8][:, 0:DW], onr[:, :],
                                rowsb[:, 0:DW], start=False, stop=False)
                last = e.matmul(PP[m % 8][:, 0:DW], onr[:, :],
                                rowsb[:, 512:512 + DW], start=False, stop=True)
            return last
        P.do("pe", fn_bias, r=[bConst, bRowsb], w=allP)
        for m in range(NT):
            osl, obf = outst[m % 2], bOut[m % 2]
            P.do("dve", lambda e, m=m, cg=cg:
                 e.scalar_tensor_tensor(yf[:, 0:DW], PP[m % 8][:, 0:DW], qselc[:, m:m + 1],
                                        rsb[:, cg * DW:(cg + 1) * DW], ALU.mult, ALU.mult),
                 r=[bP[m % 8], bQsel, bRsb], w=[bYf])
            P.do("dve", lambda e, m=m:
                 e.tensor_tensor(yf[:, DW:2 * DW], yf[:, 0:DW], xa[:, m * DW:(m + 1) * DW],
                                 ALU.add), r=[bYf, bXa], w=[bYf])
            P.do("dve", lambda e, m=m, osl=osl:
                 e.tensor_tensor(osl[:, :], yf[:, DW:2 * DW], xb[:, m * DW:(m + 1) * DW],
                                 ALU.add), r=[bYf, bXb], w=[obf])
            P.dma("sp", lambda e, m=m, cg=cg, osl=osl:
                  e.dma_start(d_out[m * 128:(m + 1) * 128, cg * DW:(cg + 1) * DW], osl[:, :]),
                  f"outw{m % 2}", r=[obf], wa=[bDout])

    P.wait_only("sp", list(bDout.ws.values()) + ([bDout.w] if bDout.w else []))
    P.emit()
    return nc



def make_inputs(cfg, x, Wq, Wk, Wv, Wo, qw, kw, Wf, fbias, cent, asn):
    B, S, DM, NT = cfg.B, cfg.S, cfg.DM, cfg.NT
    x = np.asarray(x, dtype=np.float32)
    shared = {
        "wqT": np.ascontiguousarray(np.asarray(Wq, dtype=np.float32).T),
        "wkT": np.ascontiguousarray(np.asarray(Wk, dtype=np.float32).T),
        "wvT": np.ascontiguousarray(np.asarray(Wv, dtype=np.float32).T),
        "woT": np.ascontiguousarray(np.asarray(Wo, dtype=np.float32).T),
        "wfT": np.ascontiguousarray(np.asarray(Wf, dtype=np.float32).T),
        "ctT": np.ascontiguousarray(np.asarray(cent, dtype=np.float32).T),
        "fbias": np.asarray(fbias, dtype=np.float32).reshape(1, DM),
        "qnw": np.asarray(qw, dtype=np.float32).reshape(1, cfg.HD),
        "knw": np.asarray(kw, dtype=np.float32).reshape(1, cfg.HD),
        "asn": np.asarray(asn).reshape(1, DM).astype(np.int32),
    }
    xt = x.reshape(B, cfg.ST, 128, DM)
    NTK = 2 * NT
    dttn = (np.arange(128)[:, None] - np.arange(128)[None, :]).astype(np.float32)
    tri = (dttn > 0).astype(np.float32)
    in_maps = []
    for c in range(8):
        b, par = c // 2, c % 2
        xs = np.ascontiguousarray(xt[b, par::2].reshape(cfg.TPC, DM))
        c2 = np.zeros((128, NT * (NTK + 1)), dtype=np.float32)
        for j in range(NT):
            for kt in range(2 * j + 2):
                c2[:, j * (NTK + 1) + kt] = 128.0 * (kt - (2 * j + par))
        if par == 0:
            mskA = dttn + tri * (-cfg.NEGBIG)
            mskB = dttn - cfg.NEGBIG
        else:
            mskA = dttn
            mskB = dttn + tri * (-cfg.NEGBIG)
        m3 = np.concatenate([np.tile(dttn, (1, 4)), np.tile(mskA, (1, 4)),
                             np.tile(mskB, (1, 4))], axis=1).astype(BF)
        in_maps.append(dict(shared, xt=xs, c2in=c2, mskin=m3))
    return in_maps


def assemble(cfg, outs):
    B, S, DM, NT = cfg.B, cfg.S, cfg.DM, cfg.NT
    full = np.zeros((B, S, DM), dtype=np.float32)
    for c in range(8):
        b, par = c // 2, c % 2
        o = np.asarray(outs[c]).reshape(NT, 128, DM)
        for t in range(NT):
            full[b, (2 * t + par) * 128:(2 * t + par + 1) * 128, :] = o[t]
    return full


_FULL = Cfg()


def kernel(L0_attn_x, L0_attn_Wq, L0_attn_Wk, L0_attn_Wv, L0_attn_Wo,
           L0_attn_q_norm_w, L0_attn_k_norm_w,
           L0_ffn_weight, L0_ffn_bias, L0_ffn_centroids, L0_ffn_assignments):
    cfg = _FULL
    nc = build_program(cfg)
    in_maps = make_inputs(cfg, L0_attn_x, L0_attn_Wq, L0_attn_Wk, L0_attn_Wv, L0_attn_Wo,
                          L0_attn_q_norm_w, L0_attn_k_norm_w, L0_ffn_weight, L0_ffn_bias,
                          L0_ffn_centroids, L0_ffn_assignments)
    res = run_bass_kernel_spmd(nc, in_maps, core_ids=list(range(8)))
    return assemble(cfg, [r["out"] for r in res.results])


# version 11
# speedup vs baseline: 1.0497x; 1.0497x over previous
import math
import numpy as np
import ml_dtypes
import concourse.bass as bass
import concourse.mybir as mybir
from concourse.bass_utils import run_bass_kernel_spmd

F32 = mybir.dt.float32
BF16 = mybir.dt.bfloat16
I32 = mybir.dt.int32
AF = mybir.ActivationFunctionType
ALU = mybir.AluOpType
AXX = mybir.AxisListType.X
BF = ml_dtypes.bfloat16


class Cfg:
    def __init__(self, DM=4096, S=2048, H=32, HKV=8, NCL=64):
        self.B = 4
        self.DM = DM
        self.S = S
        self.H = H
        self.HKV = HKV
        self.G = H // HKV
        self.HD = 128
        self.NCL = NCL
        self.F = DM // 128
        self.QF = H * self.HD
        self.KF = HKV * self.HD
        self.NT = S // 256
        self.TPC = self.NT * 128
        self.ST = S // 128
        start = 2 ** (-(2 ** (-(math.log2(H) - 3))))
        self.slopes = [start * start ** i for i in range(H)]
        self.EPS_LN = 1e-5
        self.EPS_RMS = 1e-6
        self.TEMP = 0.1
        self.THRESH = 0.01
        self.NEGBIG = 3.0e32


ENGS = ("pe", "dve", "act", "pool", "sp")


class Buf:
    __slots__ = ("w", "rs", "ws")

    def __init__(self):
        self.w = None
        self.rs = []
        self.ws = {}


class Prog:
    def __init__(self, nc):
        self.nc = nc
        self.ops = {e: [] for e in ENGS}
        self.cnt = {e: 0 for e in ENGS}
        self.esem = {e: nc.alloc_semaphore(f"es_{e}") for e in ENGS}
        self.pe_sem_name = self.esem["pe"].name
        self.wm = {e: {} for e in ENGS}
        self.dsem = {}

    def _waits(self, e, deps):
        best = {}
        for d in deps:
            if d is None:
                continue
            sem, val = d
            k = sem.name
            if e == "pe" and k == self.pe_sem_name:
                continue
            if k not in best or val > best[k][1]:
                best[k] = (sem, val)
        waits = []
        for k, (sem, val) in sorted(best.items()):
            if self.wm[e].get(k, 0) >= val:
                continue
            waits.append((sem, val))
            self.wm[e][k] = val
        return waits

    def _deps(self, r, w, wa, deps):
        out = list(deps)
        for b in r:
            if b.w is not None:
                out.append(b.w)
            out.extend(b.ws.values())
        for b in w:
            if b.w is not None:
                out.append(b.w)
            out.extend(b.ws.values())
            out.extend(b.rs)
        for b in wa:
            pass
        return out

    def _book(self, tok, r, w, wa):
        for b in r:
            b.rs.append(tok)
        for b in w:
            b.w = tok
            b.rs = []
        for b in wa:
            b.ws[tok[0].name] = tok
        return tok

    def do(self, e, fn, r=(), w=(), wa=(), deps=()):
        waits = self._waits(e, self._deps(r, w, wa, deps))
        self.cnt[e] += 1
        tok = (self.esem[e], self.cnt[e])
        self.ops[e].append((waits, fn, self.esem[e], 1))
        return self._book(tok, r, w, wa)

    def dma(self, q, fn, stream, r=(), w=(), wa=(), deps=(), inc=16):
        waits = self._waits(q, self._deps(r, w, wa, deps))
        if stream not in self.dsem:
            self.dsem[stream] = [self.nc.alloc_semaphore("ds_" + stream), 0]
        ent = self.dsem[stream]
        ent[1] += inc
        tok = (ent[0], ent[1])
        self.ops[q].append((waits, fn, ent[0], inc))
        return self._book(tok, r, w, wa)

    def cc(self, fn, stream, r=(), w=(), deps=()):
        return self.dma("pool", fn, stream, r=r, w=w, deps=deps, inc=1)

    def wait_only(self, e, deps):
        waits = self._waits(e, [d for d in deps if d is not None])
        if waits:
            self.ops[e].append((waits, None, None, 0))

    def emit(self):
        nc = self.nc

        def mk(e):
            def body(eng):
                for waits, fn, sem, inc in self.ops[e]:
                    for (s, v) in waits:
                        eng.wait_ge(s, v)
                    if fn is not None:
                        inst = fn(eng)
                        if inst is not None and sem is not None:
                            inst.then_inc(sem, inc)
            return body

        with nc.Block() as block:
            block.sync(mk("sp"))
            block.gpsimd(mk("pool"))
            block.tensor(mk("pe"))
            block.vector(mk("dve"))
            block.scalar(mk("act"))


def build_program(cfg: Cfg, stop_phase=99):
    nc = bass.Bass("TRN2", target_bir_lowering=False)
    DM, S, H, HKV, G, HD = cfg.DM, cfg.S, cfg.H, cfg.HKV, cfg.G, cfg.HD
    F, QF, KF, NT, NCL, ST = cfg.F, cfg.QF, cfg.KF, cfg.NT, cfg.NCL, cfg.ST
    TPC = cfg.TPC
    HC = QF // 128
    KC = KF // 128
    QCG, QW = (QF // 512, 512) if QF >= 512 else (1, QF)
    KCG, KW = (KF // 512, 512) if KF >= 512 else (1, KF)
    DCG, DW = (DM // 512, 512) if DM >= 512 else (1, DM)
    NSL = 4 if F % 4 == 0 else F
    NTK = 2 * NT
    BIGA = HKV * S
    BIGC = max(2 * ST * KF, F * TPC)
    PCNT = min(NT, 8)

    d_x = nc.dram_tensor("xt", [TPC, DM], F32, kind="ExternalInput").ap()
    d_c2 = nc.dram_tensor("c2in", [128, NT * (NTK + 1)], F32, kind="ExternalInput").ap()
    d_msk = nc.dram_tensor("mskin", [128, 3 * 512], BF16, kind="ExternalInput").ap()
    d_wq = nc.dram_tensor("wqT", [DM, QF], F32, kind="ExternalInput").ap()
    d_wk = nc.dram_tensor("wkT", [DM, KF], F32, kind="ExternalInput").ap()
    d_wv = nc.dram_tensor("wvT", [DM, KF], F32, kind="ExternalInput").ap()
    d_wo = nc.dram_tensor("woT", [QF, DM], F32, kind="ExternalInput").ap()
    d_wf = nc.dram_tensor("wfT", [DM, DM], F32, kind="ExternalInput").ap()
    d_ct = nc.dram_tensor("ctT", [DM, NCL], F32, kind="ExternalInput").ap()
    d_bias = nc.dram_tensor("fbias", [1, DM], F32, kind="ExternalInput").ap()
    d_qw = nc.dram_tensor("qnw", [1, HD], F32, kind="ExternalInput").ap()
    d_kw = nc.dram_tensor("knw", [1, HD], F32, kind="ExternalInput").ap()
    d_asn = nc.dram_tensor("asn", [1, DM], I32, kind="ExternalInput").ap()
    d_out = nc.dram_tensor("out", [TPC, DM], F32, kind="ExternalOutput").ap()

    d_xnT = nc.dram_tensor("xnT_s", [128, F * TPC], BF16).ap()
    d_qh = nc.dram_tensor("qh_s", [TPC, QF], BF16).ap()
    d_ot = nc.dram_tensor("ot_s", [HC, 128, NT * 128], BF16).ap()
    d_exi = nc.dram_tensor("exi_s", [TPC, 2 * KF], BF16)
    d_exo = nc.dram_tensor("exo_s", [2 * TPC, 2 * KF], BF16)
    d_cli = nc.dram_tensor("cli_s", [NCL, 1], F32)
    d_clo = nc.dram_tensor("clo_s", [NCL, 1], F32)

    ident_bf = nc.inline_tensor(np.eye(128, dtype=np.float32).astype(BF), "c_ident").ap()
    ident_f32 = nc.inline_tensor(np.eye(128, dtype=np.float32), "c_identf").ap()
    dnp = (np.arange(128)[None, :] - np.arange(128)[:, None]).astype(np.float32)
    d_D = nc.inline_tensor(dnp, "c_D").ap()
    ones_bf = nc.inline_tensor(np.ones((1, 128), dtype=np.float32).astype(BF), "c_ones").ap()
    iota_cl = nc.inline_tensor(np.arange(NCL, dtype=np.float32)[:, None], "c_iota").ap()
    ones_f32r = nc.inline_tensor(np.ones((1, 128), dtype=np.float32), "c_onesf").ap()
    ones_bcol = nc.inline_tensor(np.ones((128, 1), dtype=np.float32).astype(BF), "c_onesc").ap()

    A = nc.alloc_sbuf_tensor
    big = A("bigb", [128, BIGC], BF16)
    argb = A("argb", [128, NTK * 512], F32)

    class _View:
        __slots__ = ("t", "off")

        def __init__(self, t, off):
            self.t, self.off = t, off

        def __getitem__(self, idx):
            p, f = idx
            return self.t[p, f.start + self.off:f.stop + self.off]

    xa = _View(argb, 0)
    xb = _View(argb, DM)
    wsl = [A(f"wsl{i}", [128, NSL * 512], BF16) for i in range(3)]
    qs = [A(f"qs{i}", [128, max(QF, DM, TPC)], BF16) for i in range(2)]
    qt = A("qt", [128, max(QF, DM)], BF16)
    tr = A("tr", [128, max(NTK * 128, F * NCL)], F32)
    sc1 = A("sc1", [128, DM], BF16)
    sc2 = A("sc2", [128, max(DM, QF)], BF16)
    kvst = [A(f"kvst{i}", [128, max(2 * KF, NTK * 128)], BF16) for i in range(2)]
    stg4 = [A(f"stg{i}", [128, 512], BF16) for i in range(4)]
    rsb = A("rsb", [128, DM], BF16)
    CLF = F * NCL
    assert 2 * CLF <= DM
    outst = [A(f"outst{i}", [128, DW], F32) for i in range(2)]
    yf = A("yf", [128, 2 * DW], F32)
    wbck = A("wbck", [128, 128], BF16)
    mskt = A("mskt", [128, 3 * 512], BF16)
    pt = [_View(sc1, 0), _View(sc1, 512), _View(sc1, 1024), _View(sc1, 1536)]
    outsg = [_View(sc1, 2048), _View(sc1, 2560)]
    sgb = _View(sc1, 3072)
    cmx = A("cmx", [128, 2 * NTK], F32)
    biasg = A("biasg", [128, 4 * NTK], F32)
    r1b = A("r1b", [1, 128], F32)
    onesf = A("onesf", [1, 128], F32)
    onescol = A("onescol", [128, 1], BF16)
    smc = A("smc", [128, 16], F32)
    qselc = A("qselc", [128, NT], F32)
    invq = A("invq", [128, 4 * H], F32)
    invk = A("invk", [128, 4 * HKV], F32)
    clm = A("clm", [NCL, 1], F32)
    clsel = A("clsel", [NCL, 1], BF16)
    rowsb = A("rowsb", [1, 4 * 512], BF16)
    qwr = A("qwr", [1, 2 * HD], F32)
    qwrb = A("qwrb", [1, 2 * HD], BF16)
    ic = A("ic", [128, 128], BF16)
    icf = A("icf", [128, 128], F32)
    dtt = A("dtt", [128, 128], F32)
    onr = A("onr", [1, 128], BF16)
    iotc = A("iotc", [NCL, 1], F32)
    c2s = A("c2s", [128, NT * (NTK + 1)], F32)
    asn_i = A("asni", [1, 512], I32)

    PP = [nc.alloc_psum_tensor(f"pp{i}", [128, 512], F32) for i in range(8)]

    P = Prog(nc)
    bBA, bBB = Buf(), Buf()
    bXa, bXb = Buf(), Buf()
    bW = [Buf() for _ in wsl]
    bQs = [Buf(), Buf()]
    bQt = Buf()
    bTr = Buf()
    bS1, bS2 = Buf(), Buf()
    bArg = [Buf() for _ in range(NTK)]
    bPt = [Buf() for _ in range(4)]
    bOsg = [Buf(), Buf()]
    bCm, bBias, bR1, bSg, bR4, bMsk = (Buf() for _ in range(6))
    bKv = [Buf(), Buf()]
    bStg = [Buf() for _ in stg4]
    bRsb = Buf()
    bOut = [Buf(), Buf()]
    bYf = Buf()
    bWbk = Buf()
    bSmc = Buf()
    bQsel = Buf()
    bInvq, bInvk = Buf(), Buf()
    bClm, bClsel = Buf(), Buf()
    bRowsb = Buf()
    bQwr, bQwrb = Buf(), Buf()
    bConst = Buf()
    bC2 = Buf()
    bAsn = Buf()
    bP = [Buf() for _ in PP]
    bDxnT, bDqh, bDot, bDexi, bDexo, bDcli, bDclo, bDout = (Buf() for _ in range(8))

    def xn_buf(c):
        return bBA if c < F // 2 else bBB

    def MM(out_ap, lhsT_ap, rhs_ap, start, stop):
        def fn(e, a=out_ap, b=lhsT_ap, c=rhs_ap, s1=start, s2=stop):
            return e.matmul(a, b, c, start=s1, stop=s2)
        return fn

    P.dma("pool", lambda e: e.dma_start(ic[:, :], ident_bf), "cst1", w=[bConst])
    P.dma("pool", lambda e: e.dma_start(icf[:, :], ident_f32), "cst2", w=[bConst])
    P.dma("pool", lambda e: e.dma_start(dtt[:, :], d_D), "cst3", w=[bConst])
    P.dma("pool", lambda e: e.dma_start(onr[:, :], ones_bf), "cst4", w=[bConst])
    P.dma("pool", lambda e: e.dma_start(iotc[:, :], iota_cl), "cst5", w=[bConst])
    P.dma("pool", lambda e: e.dma_start(c2s[:, :], d_c2), "cst6", w=[bC2])
    P.dma("pool", lambda e: e.dma_start(mskt[:, :], d_msk), "cst12", w=[bMsk])
    P.dma("pool", lambda e: e.dma_start(onesf[:, :], ones_f32r), "cst13", w=[bConst])
    P.dma("pool", lambda e: e.dma_start(onescol[:, :], ones_bcol), "cst14", w=[bConst])
    P.dma("pool", lambda e: e.dma_start(qwr[:, 0:HD], d_qw), "cst8", w=[bQwr])
    P.dma("pool", lambda e: e.dma_start(qwr[:, HD:2 * HD], d_kw), "cst9", w=[bQwr])
    P.dma("pool", lambda e: e.dma_start(
        tr[:, 0:F * NCL].rearrange("p (c n) -> p c n", c=F),
        d_ct.rearrange("(c p) n -> p c n", p=128)), "cst11", w=[bTr])

    P.do("act", lambda e: e.activation(rsb[:, 0:CLF], tr[:, 0:F * NCL], AF.Copy),
         r=[bTr], w=[bRsb])
    P.do("dve", lambda e: e.tensor_tensor(rsb[:, CLF:2 * CLF], tr[:, 0:F * NCL], rsb[:, 0:CLF],
                                          ALU.subtract), r=[bTr, bRsb], w=[bRsb])
    P.do("dve", lambda e: e.tensor_copy(qwrb[:, HD:2 * HD], qwr[:, HD:2 * HD]),
         r=[bQwr], w=[bQwrb])
    P.do("pe", MM(PP[7][:, 0:128], onr[:, :], qwrb[:, HD:2 * HD], True, True),
         r=[bConst, bQwrb], w=[bP[7]])
    P.do("act", lambda e: e.activation(wbck[:, :], PP[7][:, 0:128], AF.Copy), r=[bP[7]], w=[bWbk])

    def _finish0():
        toks = []
        for name, (sem, val) in P.dsem.items():
            if val:
                toks.append((sem, val))
        P.wait_only("sp", toks)
        P.emit()
        return nc

    if stop_phase <= 0:
        return _finish0()
    NT1 = NT if stop_phase > 90 or stop_phase >= 1.5 else NT
    import os
    SUB = int(os.environ.get("KSUB", "99"))
    for m in range(NT if SUB >= 10 else 1):
        P.dma("sp", lambda e, m=m: e.dma_start(xa[:, 0:DM], d_x[m * 128:(m + 1) * 128, :]),
              "x0", w=[bXa])
        P.do("dve", lambda e: e.reduce_sum(smc[:, 0:1], xa[:, 0:DM], AXX), r=[bXa], w=[bSmc])
        P.do("dve", lambda e: e.tensor_scalar(smc[:, 1:2], smc[:, 0:1], 1.0 / DM, None, ALU.mult),
             r=[bSmc], w=[bSmc])
        P.do("act", lambda e: e.activation(sc2[:, 0:DM], xa[:, 0:DM], AF.Square,
                                           bias=smc[:, 1:2], scale=-1.0, accum_out=smc[:, 2:3]),
             r=[bXa, bSmc], w=[bS2, bSmc])
        P.do("dve", lambda e: e.tensor_scalar(smc[:, 3:4], smc[:, 2:3], 1.0 / DM, cfg.EPS_LN,
                                              ALU.mult, ALU.add), r=[bSmc], w=[bSmc])
        P.do("act", lambda e: e.activation(smc[:, 4:5], smc[:, 3:4], AF.Ln), r=[bSmc], w=[bSmc])
        P.do("act", lambda e: e.activation(smc[:, 5:6], smc[:, 4:5], AF.Exp, scale=-0.5),
             r=[bSmc], w=[bSmc])
        P.do("dve", lambda e: e.tensor_scalar(xb[:, 0:DM], xa[:, 0:DM], smc[:, 1:2], smc[:, 5:6],
                                              ALU.subtract, ALU.mult), r=[bXa, bSmc], w=[bXb])
        P.do("act", lambda e: e.activation(sc2[:, 0:DM], xb[:, 0:DM], AF.Copy), r=[bXb], w=[bS2])
        P.do("dve", lambda e: e.tensor_tensor(sc1[:, 0:DM], xb[:, 0:DM], sc2[:, 0:DM],
                                              ALU.subtract), r=[bXb, bS2], w=[bS1])
        if SUB < 2:
            continue
        for cb in range(0, F, 4):
            nb = min(4, F - cb)
            pih = (cb // 4) % 2
            pil = 2 + (cb // 4) % 2

            def fn_hi(e, cb=cb, nb=nb, pih=pih):
                last = None
                for i in range(nb):
                    last = e.matmul(PP[pih][:, i * 128:(i + 1) * 128],
                                    sc2[:, (cb + i) * 128:(cb + i + 1) * 128], ic[:, :],
                                    start=True, stop=True)
                return last
            P.do("pe", fn_hi, r=[bS2, bConst], w=[bP[pih]])

            def fn_lo(e, cb=cb, nb=nb, pil=pil):
                last = None
                for i in range(nb):
                    last = e.matmul(PP[pil][:, i * 128:(i + 1) * 128],
                                    sc1[:, (cb + i) * 128:(cb + i + 1) * 128], ic[:, :],
                                    start=True, stop=True)
                return last
            P.do("pe", fn_lo, r=[bS1, bConst], w=[bP[pil]])
            for i in range(nb):
                c = cb + i
                P.do("act", lambda e, c=c, i=i, pih=pih, m=m:
                     e.activation(big[:, c * TPC + m * 128:c * TPC + (m + 1) * 128],
                                  PP[pih][:, i * 128:(i + 1) * 128], AF.Copy),
                     r=[bP[pih]], w=[xn_buf(c)])
            P.do("act", lambda e, cb=cb, nb=nb, pil=pil:
                 e.activation(qs[0][:, cb * 128:(cb + nb) * 128], PP[pil][:, 0:nb * 128], AF.Copy),
                 r=[bP[pil]], w=[bQs[0]])

        if SUB < 3:
            continue

        def fn_cent(e, m=m):
            last = None
            for c in range(F):
                hiT = big[:, c * TPC + m * 128:c * TPC + (m + 1) * 128]
                loT = qs[0][:, c * 128:(c + 1) * 128]
                last = e.matmul(PP[7][:, 0:NCL], hiT, rsb[:, c * NCL:(c + 1) * NCL],
                                start=(c == 0), stop=False)
                last = e.matmul(PP[7][:, 0:NCL], hiT, rsb[:, CLF + c * NCL:CLF + (c + 1) * NCL],
                                start=False, stop=False)
                last = e.matmul(PP[7][:, 0:NCL], loT, rsb[:, c * NCL:(c + 1) * NCL],
                                start=False, stop=(c == F - 1))
            return last
        P.do("pe", fn_cent, r=[bBA, bBB, bQs[0], bRsb], w=[bP[7]])
        if SUB < 4:
            continue
        SUB2 = int(os.environ.get("KSUB2", "99"))
        P.do("dve", lambda e: e.reduce_max(smc[:, 6:7], PP[7][:, 0:NCL], AXX, negate=True),
             r=[bP[7]], w=[bSmc])
        if SUB2 < 1:
            continue
        P.do("dve", lambda e: e.tensor_scalar(smc[:, 7:8], smc[:, 6:7], 1.0 / cfg.TEMP, None,
                                              ALU.mult), r=[bSmc], w=[bSmc])
        if SUB2 < 2:
            continue
        EXPV = int(os.environ.get("KEXPV", "0"))
        if EXPV == 1:
            P.do("act", lambda e: e.activation(tr[:, 0:NCL], PP[7][:, 0:NCL], AF.Exp,
                                               bias=smc[:, 7:8], scale=1.0 / cfg.TEMP),
                 r=[bP[7], bSmc], w=[bTr, bSmc])
        elif EXPV == 2:
            P.do("act", lambda e: e.activation(tr[:, 0:NCL], PP[7][:, 0:NCL], AF.Exp,
                                               scale=1.0 / cfg.TEMP, accum_out=smc[:, 8:9]),
                 r=[bP[7], bSmc], w=[bTr, bSmc])
        elif EXPV == 3:
            P.do("dve", lambda e: e.tensor_copy(xb[:, 0:NCL], PP[7][:, 0:NCL]), r=[bP[7]], w=[bXb])
            P.do("act", lambda e: e.activation(tr[:, 0:NCL], xb[:, 0:NCL], AF.Exp,
                                               bias=smc[:, 7:8], scale=1.0 / cfg.TEMP,
                                               accum_out=smc[:, 8:9]),
                 r=[bXb, bSmc], w=[bTr, bSmc])
        elif EXPV == 5:
            P.do("act", lambda e: e.activation(smc[:, 8:9], smc[:, 7:8], AF.Copy),
                 r=[bSmc], w=[bSmc])
        elif EXPV == 6:
            P.do("act", lambda e: e.activation(sc1[:, 0:NCL], PP[7][:, 0:NCL], AF.Exp,
                                               bias=smc[:, 7:8], scale=1.0 / cfg.TEMP,
                                               accum_out=smc[:, 8:9]),
                 r=[bP[7], bSmc], w=[bS1, bSmc])
        elif EXPV == 4:
            P.do("act", lambda e: e.activation(tr[:, 0:NCL], PP[7][:, 0:NCL], AF.Copy,
                                               bias=smc[:, 7:8], scale=1.0 / cfg.TEMP,
                                               accum_out=smc[:, 8:9]),
                 r=[bP[7], bSmc], w=[bTr, bSmc])
        else:
            P.do("act", lambda e: e.activation(tr[:, 0:NCL], PP[7][:, 0:NCL], AF.Exp,
                                               bias=smc[:, 7:8], scale=1.0 / cfg.TEMP,
                                               accum_out=smc[:, 8:9]),
                 r=[bP[7], bSmc], w=[bTr, bSmc])
        if SUB2 < 3:
            continue
        P.do("dve", lambda e, m=m: e.tensor_scalar(qselc[:, m:m + 1], smc[:, 8:9],
                                                   1.0 / cfg.THRESH, None, ALU.is_lt),
             r=[bSmc], w=[bQsel])
        if SUB2 < 4:
            continue
        P.do("dve", lambda e: e.reciprocal(smc[:, 9:10], smc[:, 8:9]), r=[bSmc], w=[bSmc])
        if SUB2 < 5:
            continue
        P.do("dve", lambda e: e.tensor_scalar(tr[:, 0:NCL], tr[:, 0:NCL], smc[:, 9:10], None,
                                              ALU.mult), r=[bSmc], w=[bTr])
        if SUB < 5:
            continue
        P.do("pe", lambda e: e.transpose(PP[6][0:NCL, 0:128], tr[:, 0:NCL], icf[:, :]),
             r=[bTr, bConst], w=[bP[6]])
        P.do("dve", lambda e: e.reduce_max(smc[0:NCL, 10:11], PP[6][0:NCL, 0:128], AXX),
             r=[bP[6]], w=[bSmc])
        if m == 0:
            P.do("dve", lambda e: e.tensor_copy(clm[:, :], smc[0:NCL, 10:11]), r=[bSmc], w=[bClm])
        else:
            P.do("dve", lambda e: e.tensor_tensor(clm[:, :], clm[:, :], smc[0:NCL, 10:11],
                                                  ALU.max), r=[bSmc, bClm], w=[bClm])

    for i in range(4):
        Wc = F * TPC // 4
        P.dma("sp", lambda e, i=i, Wc=Wc: e.dma_start(d_xnT[:, i * Wc:(i + 1) * Wc],
                                                      big[:, i * Wc:(i + 1) * Wc]),
              f"xnw{i}", r=[bBA, bBB], wa=[bDxnT])
    P.dma("sp", lambda e: e.dma_start(d_cli.ap(), clm[:, :]), "clw", r=[bClm], wa=[bDcli])

    slab_ctr = [0]

    def load_slab(dram_ap, c0, nchunks, col0, width, stream):
        si = slab_ctr[0] % len(wsl)
        slab_ctr[0] += 1
        t, b = wsl[si], bW[si]
        stream = f"wsl{si}"
        src = dram_ap[c0 * 128:(c0 + nchunks) * 128, col0:col0 + width]
        P.dma("pool", lambda e, t=t, src=src, nchunks=nchunks, width=width:
              e.dma_start(t[:, 0:nchunks * width].rearrange("p (c n) -> p c n", c=nchunks),
                          src.rearrange("(c p) n -> p c n", p=128)),
              stream, w=[b])
        return t, b

    def proj_phase(dram_w, ncg, cw, nchunks_total, out_cb, which, after_first_slab=None):
        for cg in range(ncg):
            for c0 in range(0, F, NSL):
                t, b = load_slab(dram_w, c0, NSL, cg * cw, cw, "w" + which)
                if after_first_slab is not None:
                    after_first_slab()
                    after_first_slab = None
                for m in range(NT):
                    def fn(e, c0=c0, t=t, m=m, cw=cw):
                        last = None
                        for i in range(NSL):
                            c = c0 + i
                            last = e.matmul(PP[m % 8][:, 0:cw],
                                            big[:, c * TPC + m * 128:c * TPC + (m + 1) * 128],
                                            t[:, i * cw:(i + 1) * cw],
                                            start=(c == 0), stop=(c == F - 1))
                        return last
                    P.do("pe", fn, r=[bBA, bBB, b], w=[bP[m % 8]])
            for m in range(NT):
                out_cb(m, cg, PP[m % 8], bP[m % 8])

    def _finish():
        toks = []
        for name, (sem, val) in P.dsem.items():
            if val:
                toks.append((sem, val))
        P.wait_only("sp", toks)
        P.emit()
        return nc

    if stop_phase <= 1:
        return _finish()
    def kv_evac(kind):
        def cb(m, cg, psum, pbuf):
            nh = KW // 128
            if kind == "v":
                for hb in range(nh):
                    si = (m * nh + hb) % 4
                    kvh = cg * nh + hb
                    P.do("dve", lambda e, si=si, hb=hb, psum=psum:
                         e.tensor_copy(stg4[si][:, 0:128], psum[:, hb * 128:(hb + 1) * 128]),
                         r=[pbuf], w=[bStg[si]])
                    P.dma("sp", lambda e, si=si, m=m, kvh=kvh:
                          e.dma_start(d_exi.ap()[m * 128:(m + 1) * 128, KF + kvh * 128:KF + (kvh + 1) * 128],
                                      stg4[si][:, 0:128]),
                          f"stg{si}", r=[bStg[si]], wa=[bDexi])
            else:
                for hb in range(nh):
                    kvh = cg * nh + hb
                    P.do("act", lambda e, hb=hb, kvh=kvh, psum=psum:
                         e.activation(sc1[:, 0:128], psum[:, hb * 128:(hb + 1) * 128], AF.Square,
                                      accum_out=invk[:, kvh:kvh + 1]),
                         r=[pbuf], w=[bS1, bInvk])
                h0 = cg * nh
                P.do("dve", lambda e, h0=h0, nh=nh:
                     e.tensor_scalar(invk[:, HKV + h0:HKV + h0 + nh], invk[:, h0:h0 + nh],
                                     1.0 / HD, cfg.EPS_RMS, ALU.mult, ALU.add),
                     r=[bInvk], w=[bInvk])
                P.do("act", lambda e, h0=h0, nh=nh:
                     e.activation(invk[:, 2 * HKV + h0:2 * HKV + h0 + nh],
                                  invk[:, HKV + h0:HKV + h0 + nh], AF.Ln), r=[bInvk], w=[bInvk])
                P.do("act", lambda e, h0=h0, nh=nh:
                     e.activation(invk[:, 3 * HKV + h0:3 * HKV + h0 + nh],
                                  invk[:, 2 * HKV + h0:2 * HKV + h0 + nh], AF.Exp, scale=-0.5),
                     r=[bInvk], w=[bInvk])
                for hb in range(nh):
                    si = (m * nh + hb) % 4
                    kvh = cg * nh + hb
                    P.do("dve", lambda e, si=si, hb=hb, kvh=kvh, psum=psum:
                         e.scalar_tensor_tensor(stg4[si][:, 0:128], psum[:, hb * 128:(hb + 1) * 128],
                                                invk[:, 3 * HKV + kvh:3 * HKV + kvh + 1],
                                                wbck[:, :], ALU.mult, ALU.mult),
                         r=[pbuf, bInvk, bWbk], w=[bStg[si]])
                    P.dma("sp", lambda e, si=si, m=m, kvh=kvh:
                          e.dma_start(d_exi.ap()[m * 128:(m + 1) * 128, kvh * 128:(kvh + 1) * 128],
                                      stg4[si][:, 0:128]),
                          f"stg{si}", r=[bStg[si]], wa=[bDexi])
        return cb

    proj_phase(d_wk, KCG, KW, KC, kv_evac("k"), "k")
    proj_phase(d_wv, KCG, KW, KC, kv_evac("v"), "v")

    if stop_phase <= 2:
        return _finish()
    for h in range(H):
        sc = 1.0 / (math.sqrt(HD) * cfg.slopes[h])
        P.do("dve", lambda e, sc=sc: e.tensor_scalar(qwrb[:, 0:HD], qwr[:, 0:HD], sc, None,
                                                     ALU.mult), r=[bQwr], w=[bQwrb])
        pi = 4 + (h % 2)
        P.do("pe", MM(PP[pi][:, 0:128], onr[:, :], qwrb[:, 0:HD], True, True),
             r=[bConst, bQwrb], w=[bP[pi]])
        P.do("act", lambda e, h=h, pi=pi: e.activation(sc2[:, h * 128:(h + 1) * 128],
                                                       PP[pi][:, 0:128], AF.Copy),
             r=[bP[pi]], w=[bS2])

    def q_evac(m, cg, psum, pbuf):
        nh = QW // 128
        for hb in range(nh):
            h = cg * nh + hb
            P.do("act", lambda e, hb=hb, h=h, psum=psum:
                 e.activation(sc1[:, 0:128], psum[:, hb * 128:(hb + 1) * 128], AF.Square,
                              accum_out=invq[:, h:h + 1]), r=[pbuf], w=[bS1, bInvq])
        h0 = cg * nh
        P.do("dve", lambda e, h0=h0, nh=nh:
             e.tensor_scalar(invq[:, H + h0:H + h0 + nh], invq[:, h0:h0 + nh], 1.0 / HD,
                             cfg.EPS_RMS, ALU.mult, ALU.add), r=[bInvq], w=[bInvq])
        P.do("act", lambda e, h0=h0, nh=nh:
             e.activation(invq[:, 2 * H + h0:2 * H + h0 + nh], invq[:, H + h0:H + h0 + nh], AF.Ln),
             r=[bInvq], w=[bInvq])
        P.do("act", lambda e, h0=h0, nh=nh:
             e.activation(invq[:, 3 * H + h0:3 * H + h0 + nh], invq[:, 2 * H + h0:2 * H + h0 + nh],
                          AF.Exp, scale=-0.5), r=[bInvq], w=[bInvq])
        for hb in range(nh):
            si = (m * nh + hb) % 4
            h = cg * nh + hb
            P.do("dve", lambda e, si=si, hb=hb, h=h, psum=psum:
                 e.scalar_tensor_tensor(stg4[si][:, 0:128], psum[:, hb * 128:(hb + 1) * 128],
                                        invq[:, 3 * H + h:3 * H + h + 1],
                                        sc2[:, h * 128:(h + 1) * 128], ALU.mult, ALU.mult),
                 r=[pbuf, bInvq, bS2], w=[bStg[si]])
            P.dma("sp", lambda e, si=si, m=m, h=h:
                  e.dma_start(d_qh[m * 128:(m + 1) * 128, h * 128:(h + 1) * 128], stg4[si][:, 0:128]),
                  f"stg{si}", r=[bStg[si]], wa=[bDqh])

    def fn_ar(e):
        return e.collective_compute("AllReduce", ALU.max,
                                    replica_groups=[[0, 1, 2, 3, 4, 5, 6, 7]],
                                    ins=[d_cli.ap().opt()], outs=[d_clo.ap().opt()])

    def fn_ag(e):
        return e.collective_compute("AllGather", ALU.bypass,
                                    replica_groups=[[0, 1], [2, 3], [4, 5], [6, 7]],
                                    ins=[d_exi.ap().opt()], outs=[d_exo.ap().opt()])

    def emit_ccs():
        P.cc(fn_ag, "ccag", r=[bDexi], w=[bDexo])
        P.cc(fn_ar, "ccar", r=[bDcli], w=[bDclo])

    proj_phase(d_wq, QCG, QW, HC, q_evac, "q", after_first_slab=emit_ccs)

    if stop_phase <= 3:
        return _finish()
    for t in range(ST):
        rb = (t % 2) * TPC + (t // 2) * 128
        st, sb = kvst[t % 2], bKv[t % 2]
        P.dma("sp", lambda e, st=st, rb=rb: e.dma_start(st[:, 0:KF],
                                                        d_exo.ap()[rb:rb + 128, 0:KF]),
              f"exr{t % 2}", r=[bDexo], w=[sb])
        P.dma("sp", lambda e, t=t, rb=rb: e.dma_start(big[:, BIGA + t * KF:BIGA + (t + 1) * KF],
                                                      d_exo.ap()[rb:rb + 128, KF:2 * KF]),
              "exrv", r=[bDexo], w=[bBB])
        for kb in range(0, KC, 4):
            nb = min(4, KC - kb)
            pi = (kb // 4) % 2

            def fn_tr(e, st=st, kb=kb, nb=nb, pi=pi):
                last = None
                for i in range(nb):
                    last = e.matmul(PP[pi][:, i * 128:(i + 1) * 128],
                                    st[:, (kb + i) * 128:(kb + i + 1) * 128], ic[:, :],
                                    start=True, stop=True)
                return last
            P.do("pe", fn_tr, r=[sb, bConst], w=[bP[pi]])
            for i in range(nb):
                kv = kb + i
                P.do("act", lambda e, kv=kv, t=t, pi=pi, i=i:
                     e.activation(big[:, kv * S + t * 128:kv * S + (t + 1) * 128],
                                  PP[pi][:, i * 128:(i + 1) * 128], AF.Copy),
                     r=[bP[pi]], w=[bBA])

    if stop_phase <= 35:
        return _finish()
    fxab = []
    for _b in (bXa, bXb, bTr, bS1):
        if _b.w is not None:
            fxab.append(_b.w)
        fxab.extend(_b.rs)
        fxab.extend(_b.ws.values())
    r4bt = tr
    DTT0, MSKA, MSKB = 0, 512, 1024

    for j in range(NT):
        stiles = 2 * j + 2
        stg, sbq = qs[j % 2], bQs[j % 2]
        if j == 0:
            P.dma("sp", lambda e, stg=stg: e.dma_start(stg[:, 0:QF], d_qh[0:128, :]),
                  "qhr0", r=[bDqh], w=[sbq])
        for cb in range(0, HC, 4):
            pi = 6 + (cb // 4) % 2

            def fn_trq(e, stg=stg, cb=cb, pi=pi):
                last = None
                for i in range(4):
                    last = e.matmul(PP[pi][:, i * 128:(i + 1) * 128],
                                    stg[:, (cb + i) * 128:(cb + i + 1) * 128], ic[:, :],
                                    start=True, stop=True)
                return last
            P.do("pe", fn_trq, r=[sbq, bConst], w=[bP[pi]])
            P.do("act", lambda e, cb=cb, pi=pi:
                 e.activation(qt[:, cb * 128:(cb + 4) * 128], PP[pi][:, 0:512], AF.Copy),
                 r=[bP[pi]], w=[bQt])
        if j + 1 < NT:
            stg2, sbq2 = qs[(j + 1) % 2], bQs[(j + 1) % 2]
            P.dma("sp", lambda e, stg2=stg2, j=j:
                  e.dma_start(stg2[:, 0:QF], d_qh[(j + 1) * 128:(j + 2) * 128, :]),
                  f"qhr{(j + 1) % 2}", r=[bDqh], w=[sbq2])

        cc0 = j * (NTK + 1)
        P.do("pe", lambda e, cc0=cc0, stiles=stiles:
             e.matmul(PP[6][:, 0:stiles], onesf[:, :], c2s[0:1, cc0:cc0 + stiles],
                      start=True, stop=True),
             r=[bConst, bC2], w=[bP[6]])
        P.do("dve", lambda e, stiles=stiles:
             e.tensor_copy(cmx[:, NTK:NTK + stiles], PP[6][:, 0:stiles]),
             r=[bP[6]], w=[bCm])

        def emit_sttr(g, kt, j=j):
            pb = kt % 2
            mo = MSKA if kt == 2 * j else (MSKB if kt == 2 * j + 1 else DTT0)
            P.do("pe", MM(PP[pb][:, 0:512],
                          big[:, g * S + kt * 128:g * S + (kt + 1) * 128],
                          qt[:, g * 512:(g + 1) * 512], True, True),
                 r=[bBA, bQt], w=[bP[pb]])
            P.do("dve", lambda e, kt=kt, pb=pb, mo=mo:
                 e.scalar_tensor_tensor(argb[:, kt * 512:(kt + 1) * 512],
                                        PP[pb][:, 0:512], cmx[:, NTK + kt:NTK + kt + 1],
                                        mskt[:, mo:mo + 512], ALU.add, ALU.add),
                 r=[bP[pb], bMsk, bCm], w=[bArg[kt]], deps=fxab)
            P.do("dve", lambda e, kt=kt:
                 e.reduce_max(cmx[:, kt:kt + 1], argb[:, kt * 512:(kt + 1) * 512],
                              AXX),
                 r=[bArg[kt]], w=[bCm])

        def emit_b1tail(g, j=j, stiles=stiles):
            P.do("dve", lambda e, stiles=stiles:
                 e.reduce_max(smc[:, 0:1], cmx[:, 0:stiles], AXX),
                 r=[bCm], w=[bSmc])

        def emit_b2(g, j=j, stiles=stiles):
            P.do("pe", lambda e: e.matmul(PP[6][0:1, 0:128], smc[:, 0:1], icf[:, :],
                                          start=True, stop=True),
                 r=[bSmc, bConst], w=[bP[6]])
            P.do("dve", lambda e: e.reduce_max(r1b[0:1, 32:33], PP[6][0:1, 0:128], AXX),
                 r=[bP[6]], w=[bR1])
            for i in range(4):
                h = 4 * g + i
                P.do("dve", lambda e, i=i, h=h:
                     e.tensor_scalar(r1b[0:1, 64 + i:64 + i + 1],
                                     r1b[0:1, 32:33], -cfg.slopes[h], None, ALU.mult),
                     r=[bR1], w=[bR1])
            P.do("pe", lambda e: e.matmul(PP[6][:, 0:4], onesf[:, :], r1b[0:1, 64:68],
                                          start=True, stop=True),
                 r=[bR1, bConst], w=[bP[6]])
            P.do("dve", lambda e: e.tensor_copy(biasg[:, 0:4], PP[6][:, 0:4]),
                 r=[bP[6]], w=[bBias])

        def emit_cexp(g, kt0, j=j, stiles=stiles):
            qn = min(4, stiles - kt0)
            for i in range(4):
                h = 4 * g + i
                inv = argb[:, kt0 * 512:(kt0 + qn) * 512].rearrange(
                    "p (u c) -> p u c", u=qn)[:, :, i * 128:(i + 1) * 128]
                outv = sc1[:, 0:qn * 512].rearrange(
                    "p (u c) -> p u c", u=qn)[:, :, i * 128:(i + 1) * 128]
                P.do("act", lambda e, inv=inv, outv=outv, h=h:
                     e.activation(outv, inv, AF.Exp, bias=biasg[:, h - 4 * g:h - 4 * g + 1],
                                  scale=cfg.slopes[h]),
                     r=[bArg[u] for u in range(kt0, kt0 + qn)] + [bBias],
                     w=[bPt[u] for u in range(qn)], deps=fxab)

        def emit_ck(g, kt, j=j, stiles=stiles):
            po, ps = 2 + (g % 2), 4 + (g % 2)
            tp = kt % 4
            P.do("pe", lambda e, kt=kt, tp=tp, ps=ps, stiles=stiles:
                 e.matmul(PP[ps][0:1, 0:512], onescol[:, :], pt[tp][:, 0:512],
                          start=(kt == 0), stop=(kt == stiles - 1)),
                 r=[bPt[tp], bConst], w=[bP[ps]])
            P.do("pe", lambda e, kt=kt, tp=tp, po=po, g=g, stiles=stiles:
                 e.matmul(PP[po][:, 0:512],
                          big[:, BIGA + kt * KF + g * 128:BIGA + kt * KF + g * 128 + 128],
                          pt[tp][:, 0:512],
                          start=(kt == 0), stop=(kt == stiles - 1)),
                 r=[bPt[tp], bBB], w=[bP[po]])

        def emit_d(g, j=j):
            po, ps = 2 + (g % 2), 4 + (g % 2)
            og = g % 2
            P.do("dve", lambda e, ps=ps: e.tensor_copy(sgb[0:1, 0:512], PP[ps][0:1, 0:512]),
                 r=[bP[ps]], w=[bSg], deps=fxab)
            P.do("pe", lambda e: e.matmul(PP[7][:, 0:512], onr[:, :], sgb[0:1, 0:512],
                                          start=True, stop=True),
                 r=[bSg, bConst], w=[bP[7]])
            P.do("act", lambda e: e.activation(r4bt[:, 512:1024], PP[7][:, 0:512], AF.Ln),
                 r=[bP[7]], w=[bR4], deps=fxab)
            P.do("act", lambda e: e.activation(r4bt[:, 1024:1536], r4bt[:, 512:1024],
                                               AF.Exp, scale=-1.0),
                 r=[bR4], w=[bR4], deps=fxab)
            P.do("dve", lambda e, po=po, og=og:
                 e.tensor_tensor(outsg[og][:, 0:512], PP[po][:, 0:512], r4bt[:, 1024:1536],
                                 ALU.mult),
                 r=[bP[po], bR4], w=[bOsg[og]], deps=fxab)
            for i in range(4):
                h = 4 * g + i
                P.dma("sp", lambda e, h=h, j=j, i=i, og=og:
                      e.dma_start(d_ot[h, :, j * 128:(j + 1) * 128],
                                  outsg[og][:, i * 128:(i + 1) * 128]),
                      f"otw{og}", r=[bOsg[og]], wa=[bDot])

        for g in range(HKV):
            for kt in range(stiles):
                if g > 0:
                    if kt % 4 == 0:
                        emit_cexp(g - 1, kt)
                    emit_ck(g - 1, kt)
                emit_sttr(g, kt)
            emit_b1tail(g)
            emit_b2(g)
            if g > 1:
                emit_d(g - 2)
        for kt in range(stiles):
            if kt % 4 == 0:
                emit_cexp(HKV - 1, kt)
            emit_ck(HKV - 1, kt)
        emit_d(HKV - 2)
        emit_d(HKV - 1)

    if stop_phase <= 4:
        return _finish()
    for i in range(4):
        Wc = F * TPC // 4
        P.dma("act", lambda e, i=i, Wc=Wc: e.dma_start(
            big[:, i * Wc:(i + 1) * Wc], d_xnT[:, i * Wc:(i + 1) * Wc]),
              f"xnr{i}", r=[bDxnT], w=[bBA] if i < 2 else [bBB])
    P.dma("sp", lambda e: e.dma_start(clm[:, :], d_clo.ap()), "clr", r=[bDclo], w=[bClm])
    P.do("dve", lambda e: e.tensor_scalar(clsel[:, :], clm[:, :], cfg.THRESH, None, ALU.is_gt),
         r=[bClm], w=[bClsel])
    for gc in range(DCG):
        P.dma("pool", lambda e, gc=gc: e.dma_start(asn_i[:, 0:DW], d_asn[:, gc * DW:(gc + 1) * DW]),
              "asnl", r=[bAsn], w=[bAsn])
        P.do("dve", lambda e: e.tensor_copy(yf[0:1, 0:DW], asn_i[:, 0:DW]), r=[bAsn], w=[bYf])
        P.do("dve", lambda e: e.tensor_copy(rowsb[:, 1024:1024 + DW], yf[0:1, 0:DW]),
             r=[bYf], w=[bRowsb])
        P.do("pe", MM(PP[2][0:NCL, 0:DW], onr[:, 0:NCL], rowsb[:, 1024:1024 + DW], True, True),
             r=[bConst, bRowsb], w=[bP[2]])
        P.do("dve", lambda e, gc=gc: e.tensor_scalar(qt[0:NCL, gc * DW:(gc + 1) * DW],
                                                     PP[2][0:NCL, 0:DW], iotc[:, :], None,
                                                     ALU.is_equal),
             r=[bP[2], bConst], w=[bQt])
        P.do("pe", lambda e, gc=gc: e.matmul(PP[3][0:1, 0:DW], clsel[:, :],
                                             qt[0:NCL, gc * DW:(gc + 1) * DW],
                                             start=True, stop=True),
             r=[bClsel, bQt], w=[bP[3]])
        P.do("dve", lambda e: e.tensor_copy(rowsb[:, 1536:1536 + DW], PP[3][0:1, 0:DW]),
             r=[bP[3]], w=[bRowsb])
        P.do("pe", lambda e: e.matmul(PP[2][:, 0:DW], onr[:, :], rowsb[:, 1536:1536 + DW],
                                      start=True, stop=True),
             r=[bConst, bRowsb], w=[bP[2]])
        P.do("dve", lambda e, gc=gc: e.tensor_copy(rsb[:, gc * DW:(gc + 1) * DW], PP[2][:, 0:DW]),
             r=[bP[2]], w=[bRsb])

    fx5 = []
    for _b in bArg:
        if _b.w is not None:
            fx5.append(_b.w)
        fx5.extend(_b.rs)
        fx5.extend(_b.ws.values())
    fq5 = []
    for _b in bQs:
        if _b.w is not None:
            fq5.append(_b.w)
        fq5.extend(_b.rs)
        fq5.extend(_b.ws.values())
    bOt6 = [Buf() for _ in range(6)]
    for cg in range(DCG):
        P.dma("sp", lambda e, cg=cg: e.dma_start(
            xa[:, 0:NT * DW].rearrange("p (m n) -> p m n", m=NT),
            d_x[:, cg * DW:(cg + 1) * DW].rearrange("(m p) n -> p m n", p=128)),
            "xs", w=[bXa], deps=fx5)
        P.dma("pool", lambda e, cg=cg: e.dma_start(yf[0:1, 0:DW], d_bias[:, cg * DW:(cg + 1) * DW]),
              "bil", r=[bYf], w=[bYf])
        P.do("act", lambda e: e.activation(rowsb[:, 0:DW], yf[0:1, 0:DW], AF.Copy),
             r=[bYf], w=[bRowsb])
        P.do("dve", lambda e: e.tensor_tensor(rowsb[:, 512:512 + DW], yf[0:1, 0:DW],
                                              rowsb[:, 0:DW], ALU.subtract),
             r=[bYf, bRowsb], w=[bRowsb])
        allP = [bP[mm % 8] for mm in range(PCNT)]
        OTQ = ("sp", "act")
        for c0 in range(0, HC, NSL):
            ns = min(NSL, HC - c0)
            t, b = load_slab(d_wo, c0, ns, cg * DW, DW, "wo")
            for i in range(ns):
                c = c0 + i
                sl = c % 6
                stg = qs[0] if sl < 4 else qs[1]
                off = (sl % 4) * TPC
                sbo = bOt6[sl]
                P.dma(OTQ[c % 2], lambda e, stg=stg, off=off, c=c: e.dma_start(
                    stg[:, off:off + TPC], d_ot[c, :, :]),
                    f"otr{c % 2}", r=[bDot], w=[sbo], deps=fq5)

                def fn(e, stg=stg, off=off, t=t, i=i, c=c):
                    last = None
                    for m in range(NT):
                        last = e.matmul(PP[m % 8][:, 0:DW],
                                        stg[:, off + m * 128:off + (m + 1) * 128],
                                        t[:, i * DW:(i + 1) * DW],
                                        start=(c == 0), stop=(c == HC - 1))
                    return last
                P.do("pe", fn, r=[sbo, b], w=allP)
        for m in range(NT):
            P.do("dve", lambda e, m=m: e.tensor_copy(xb[:, m * DW:(m + 1) * DW],
                                                     PP[m % 8][:, 0:DW]),
                 r=[bP[m % 8]], w=[bXb], deps=fx5)
        for c0 in range(0, F, NSL):
            t, b = load_slab(d_wf, c0, NSL, cg * DW, DW, "wf")
            for i in range(NSL):
                c = c0 + i

                def fn(e, t=t, i=i, c=c):
                    last = None
                    for m in range(NT):
                        last = e.matmul(PP[m % 8][:, 0:DW],
                                        big[:, c * TPC + m * 128:c * TPC + (m + 1) * 128],
                                        t[:, i * DW:(i + 1) * DW],
                                        start=(c == 0), stop=False)
                    return last
                P.do("pe", fn, r=[bBA, bBB, b], w=allP)

        def fn_bias(e, cg=cg):
            last = None
            for m in range(NT):
                last = e.matmul(PP[m % 8][:, 0:DW], onr[:, :],
                                rowsb[:, 0:DW], start=False, stop=False)
                last = e.matmul(PP[m % 8][:, 0:DW], onr[:, :],
                                rowsb[:, 512:512 + DW], start=False, stop=True)
            return last
        P.do("pe", fn_bias, r=[bConst, bRowsb], w=allP)
        for m in range(NT):
            osl, obf = outst[m % 2], bOut[m % 2]
            P.do("dve", lambda e, m=m, cg=cg:
                 e.scalar_tensor_tensor(yf[:, 0:DW], PP[m % 8][:, 0:DW], qselc[:, m:m + 1],
                                        rsb[:, cg * DW:(cg + 1) * DW], ALU.mult, ALU.mult),
                 r=[bP[m % 8], bQsel, bRsb], w=[bYf])
            P.do("dve", lambda e, m=m:
                 e.tensor_tensor(yf[:, DW:2 * DW], yf[:, 0:DW], xa[:, m * DW:(m + 1) * DW],
                                 ALU.add), r=[bYf, bXa], w=[bYf])
            P.do("dve", lambda e, m=m, osl=osl:
                 e.tensor_tensor(osl[:, :], yf[:, DW:2 * DW], xb[:, m * DW:(m + 1) * DW],
                                 ALU.add), r=[bYf, bXb], w=[obf])
            P.dma("sp", lambda e, m=m, cg=cg, osl=osl:
                  e.dma_start(d_out[m * 128:(m + 1) * 128, cg * DW:(cg + 1) * DW], osl[:, :]),
                  f"outw{m % 2}", r=[obf], wa=[bDout])

    P.wait_only("sp", list(bDout.ws.values()) + ([bDout.w] if bDout.w else []))
    P.emit()
    return nc



def make_inputs(cfg, x, Wq, Wk, Wv, Wo, qw, kw, Wf, fbias, cent, asn):
    B, S, DM, NT = cfg.B, cfg.S, cfg.DM, cfg.NT
    x = np.asarray(x, dtype=np.float32)
    shared = {
        "wqT": np.ascontiguousarray(np.asarray(Wq, dtype=np.float32).T),
        "wkT": np.ascontiguousarray(np.asarray(Wk, dtype=np.float32).T),
        "wvT": np.ascontiguousarray(np.asarray(Wv, dtype=np.float32).T),
        "woT": np.ascontiguousarray(np.asarray(Wo, dtype=np.float32).T),
        "wfT": np.ascontiguousarray(np.asarray(Wf, dtype=np.float32).T),
        "ctT": np.ascontiguousarray(np.asarray(cent, dtype=np.float32).T),
        "fbias": np.asarray(fbias, dtype=np.float32).reshape(1, DM),
        "qnw": np.asarray(qw, dtype=np.float32).reshape(1, cfg.HD),
        "knw": np.asarray(kw, dtype=np.float32).reshape(1, cfg.HD),
        "asn": np.asarray(asn).reshape(1, DM).astype(np.int32),
    }
    xt = x.reshape(B, cfg.ST, 128, DM)
    NTK = 2 * NT
    dttn = (np.arange(128)[:, None] - np.arange(128)[None, :]).astype(np.float32)
    tri = (dttn > 0).astype(np.float32)
    in_maps = []
    for c in range(8):
        b, par = c // 2, c % 2
        xs = np.ascontiguousarray(xt[b, par::2].reshape(cfg.TPC, DM))
        c2 = np.zeros((128, NT * (NTK + 1)), dtype=np.float32)
        for j in range(NT):
            for kt in range(2 * j + 2):
                c2[:, j * (NTK + 1) + kt] = 128.0 * (kt - (2 * j + par))
        if par == 0:
            mskA = dttn + tri * (-cfg.NEGBIG)
            mskB = dttn - cfg.NEGBIG
        else:
            mskA = dttn
            mskB = dttn + tri * (-cfg.NEGBIG)
        m3 = np.concatenate([np.tile(dttn, (1, 4)), np.tile(mskA, (1, 4)),
                             np.tile(mskB, (1, 4))], axis=1).astype(BF)
        in_maps.append(dict(shared, xt=xs, c2in=c2, mskin=m3))
    return in_maps


def assemble(cfg, outs):
    B, S, DM, NT = cfg.B, cfg.S, cfg.DM, cfg.NT
    full = np.zeros((B, S, DM), dtype=np.float32)
    for c in range(8):
        b, par = c // 2, c % 2
        o = np.asarray(outs[c]).reshape(NT, 128, DM)
        for t in range(NT):
            full[b, (2 * t + par) * 128:(2 * t + par + 1) * 128, :] = o[t]
    return full


_FULL = Cfg()


def kernel(L0_attn_x, L0_attn_Wq, L0_attn_Wk, L0_attn_Wv, L0_attn_Wo,
           L0_attn_q_norm_w, L0_attn_k_norm_w,
           L0_ffn_weight, L0_ffn_bias, L0_ffn_centroids, L0_ffn_assignments):
    cfg = _FULL
    nc = build_program(cfg)
    in_maps = make_inputs(cfg, L0_attn_x, L0_attn_Wq, L0_attn_Wk, L0_attn_Wv, L0_attn_Wo,
                          L0_attn_q_norm_w, L0_attn_k_norm_w, L0_ffn_weight, L0_ffn_bias,
                          L0_ffn_centroids, L0_ffn_assignments)
    res = run_bass_kernel_spmd(nc, in_maps, core_ids=list(range(8)))
    return assemble(cfg, [r["out"] for r in res.results])


# version 14
# speedup vs baseline: 1.0633x; 1.0130x over previous
import math
import numpy as np
import ml_dtypes
import concourse.bass as bass
import concourse.mybir as mybir
from concourse.bass_utils import run_bass_kernel_spmd

F32 = mybir.dt.float32
BF16 = mybir.dt.bfloat16
I32 = mybir.dt.int32
AF = mybir.ActivationFunctionType
ALU = mybir.AluOpType
AXX = mybir.AxisListType.X
BF = ml_dtypes.bfloat16


class Cfg:
    def __init__(self, DM=4096, S=2048, H=32, HKV=8, NCL=64):
        self.B = 4
        self.DM = DM
        self.S = S
        self.H = H
        self.HKV = HKV
        self.G = H // HKV
        self.HD = 128
        self.NCL = NCL
        self.F = DM // 128
        self.QF = H * self.HD
        self.KF = HKV * self.HD
        self.NT = S // 256
        self.TPC = self.NT * 128
        self.ST = S // 128
        start = 2 ** (-(2 ** (-(math.log2(H) - 3))))
        self.slopes = [start * start ** i for i in range(H)]
        self.EPS_LN = 1e-5
        self.EPS_RMS = 1e-6
        self.TEMP = 0.1
        self.THRESH = 0.01
        self.NEGBIG = 3.0e32


ENGS = ("pe", "dve", "act", "pool", "sp")


class Buf:
    __slots__ = ("w", "rs", "ws")

    def __init__(self):
        self.w = None
        self.rs = []
        self.ws = {}


class Prog:
    def __init__(self, nc):
        self.nc = nc
        self.ops = {e: [] for e in ENGS}
        self.cnt = {e: 0 for e in ENGS}
        self.esem = {e: nc.alloc_semaphore(f"es_{e}") for e in ENGS}
        self.pe_sem_name = self.esem["pe"].name
        self.wm = {e: {} for e in ENGS}
        self.dsem = {}

    def _waits(self, e, deps):
        best = {}
        for d in deps:
            if d is None:
                continue
            sem, val = d
            k = sem.name
            if e == "pe" and k == self.pe_sem_name:
                continue
            if k not in best or val > best[k][1]:
                best[k] = (sem, val)
        waits = []
        for k, (sem, val) in sorted(best.items()):
            if self.wm[e].get(k, 0) >= val:
                continue
            waits.append((sem, val))
            self.wm[e][k] = val
        return waits

    def _deps(self, r, w, wa, deps):
        out = list(deps)
        for b in r:
            if b.w is not None:
                out.append(b.w)
            out.extend(b.ws.values())
        for b in w:
            if b.w is not None:
                out.append(b.w)
            out.extend(b.ws.values())
            out.extend(b.rs)
        for b in wa:
            pass
        return out

    def _book(self, tok, r, w, wa):
        for b in r:
            b.rs.append(tok)
        for b in w:
            b.w = tok
            b.rs = []
        for b in wa:
            b.ws[tok[0].name] = tok
        return tok

    def do(self, e, fn, r=(), w=(), wa=(), deps=()):
        waits = self._waits(e, self._deps(r, w, wa, deps))
        self.cnt[e] += 1
        tok = (self.esem[e], self.cnt[e])
        self.ops[e].append((waits, fn, self.esem[e], 1))
        return self._book(tok, r, w, wa)

    def dma(self, q, fn, stream, r=(), w=(), wa=(), deps=(), inc=16):
        waits = self._waits(q, self._deps(r, w, wa, deps))
        if stream not in self.dsem:
            self.dsem[stream] = [self.nc.alloc_semaphore("ds_" + stream), 0]
        ent = self.dsem[stream]
        ent[1] += inc
        tok = (ent[0], ent[1])
        self.ops[q].append((waits, fn, ent[0], inc))
        return self._book(tok, r, w, wa)

    def cc(self, fn, stream, r=(), w=(), deps=()):
        return self.dma("pool", fn, stream, r=r, w=w, deps=deps, inc=1)

    def wait_only(self, e, deps):
        waits = self._waits(e, [d for d in deps if d is not None])
        if waits:
            self.ops[e].append((waits, None, None, 0))

    def emit(self):
        nc = self.nc

        def mk(e):
            def body(eng):
                for waits, fn, sem, inc in self.ops[e]:
                    for (s, v) in waits:
                        eng.wait_ge(s, v)
                    if fn is not None:
                        inst = fn(eng)
                        if inst is not None and sem is not None:
                            inst.then_inc(sem, inc)
            return body

        with nc.Block() as block:
            block.sync(mk("sp"))
            block.gpsimd(mk("pool"))
            block.tensor(mk("pe"))
            block.vector(mk("dve"))
            block.scalar(mk("act"))


def build_program(cfg: Cfg, stop_phase=99):
    nc = bass.Bass("TRN2", target_bir_lowering=False)
    DM, S, H, HKV, G, HD = cfg.DM, cfg.S, cfg.H, cfg.HKV, cfg.G, cfg.HD
    F, QF, KF, NT, NCL, ST = cfg.F, cfg.QF, cfg.KF, cfg.NT, cfg.NCL, cfg.ST
    TPC = cfg.TPC
    HC = QF // 128
    KC = KF // 128
    QCG, QW = (QF // 512, 512) if QF >= 512 else (1, QF)
    KCG, KW = (KF // 512, 512) if KF >= 512 else (1, KF)
    DCG, DW = (DM // 512, 512) if DM >= 512 else (1, DM)
    NSL = 4 if F % 4 == 0 else F
    NTK = 2 * NT
    BIGA = HKV * S
    BIGC = max(2 * ST * KF, F * TPC)
    PCNT = min(NT, 8)

    d_x = nc.dram_tensor("xt", [TPC, DM], F32, kind="ExternalInput").ap()
    d_c2 = nc.dram_tensor("c2in", [128, NT * (NTK + 1)], F32, kind="ExternalInput").ap()
    d_msk = nc.dram_tensor("mskin", [128, 3 * 512], BF16, kind="ExternalInput").ap()
    d_wq = nc.dram_tensor("wqT", [DM, QF], F32, kind="ExternalInput").ap()
    d_wk = nc.dram_tensor("wkT", [DM, KF], F32, kind="ExternalInput").ap()
    d_wv = nc.dram_tensor("wvT", [DM, KF], F32, kind="ExternalInput").ap()
    d_wo = nc.dram_tensor("woT", [QF, DM], F32, kind="ExternalInput").ap()
    d_wf = nc.dram_tensor("wfT", [DM, DM], F32, kind="ExternalInput").ap()
    d_ct = nc.dram_tensor("ctT", [DM, NCL], F32, kind="ExternalInput").ap()
    d_bias = nc.dram_tensor("fbias", [1, DM], F32, kind="ExternalInput").ap()
    d_qw = nc.dram_tensor("qnw", [1, HD], F32, kind="ExternalInput").ap()
    d_kw = nc.dram_tensor("knw", [1, HD], F32, kind="ExternalInput").ap()
    d_asn = nc.dram_tensor("asn", [1, DM], I32, kind="ExternalInput").ap()
    d_out = nc.dram_tensor("out", [TPC, DM], F32, kind="ExternalOutput").ap()

    d_xnT = nc.dram_tensor("xnT_s", [128, F * TPC], BF16).ap()
    d_qh = nc.dram_tensor("qh_s", [TPC, QF], BF16).ap()
    d_ot = nc.dram_tensor("ot_s", [HC, 128, NT * 128], BF16).ap()
    d_exi = nc.dram_tensor("exi_s", [TPC, 2 * KF], BF16)
    d_exo = nc.dram_tensor("exo_s", [2 * TPC, 2 * KF], BF16)
    d_cli = nc.dram_tensor("cli_s", [NCL, 1], F32)
    d_clo = nc.dram_tensor("clo_s", [NCL, 1], F32)

    ident_bf = nc.inline_tensor(np.eye(128, dtype=np.float32).astype(BF), "c_ident").ap()
    ident_f32 = nc.inline_tensor(np.eye(128, dtype=np.float32), "c_identf").ap()
    dnp = (np.arange(128)[None, :] - np.arange(128)[:, None]).astype(np.float32)
    d_D = nc.inline_tensor(dnp, "c_D").ap()
    ones_bf = nc.inline_tensor(np.ones((1, 128), dtype=np.float32).astype(BF), "c_ones").ap()
    iota_cl = nc.inline_tensor(np.arange(NCL, dtype=np.float32)[:, None], "c_iota").ap()
    ones_f32r = nc.inline_tensor(np.ones((1, 128), dtype=np.float32), "c_onesf").ap()
    rcol_c = nc.inline_tensor(np.arange(128, dtype=np.float32)[:, None], "c_rcol").ap()
    ones_bcol = nc.inline_tensor(np.ones((128, 1), dtype=np.float32).astype(BF), "c_onesc").ap()

    A = nc.alloc_sbuf_tensor
    big = A("bigb", [128, BIGC], BF16)
    argb = A("argb", [128, NTK * 512], F32)

    class _View:
        __slots__ = ("t", "off")

        def __init__(self, t, off):
            self.t, self.off = t, off

        def __getitem__(self, idx):
            p, f = idx
            return self.t[p, f.start + self.off:f.stop + self.off]

    xa = _View(argb, 0)
    xb = _View(argb, DM)
    wsl = [A(f"wsl{i}", [128, NSL * 512], BF16) for i in range(3)]
    qs = [A(f"qs{i}", [128, max(QF, DM, TPC)], BF16) for i in range(2)]
    qt = A("qt", [128, max(QF, DM)], BF16)
    tr = A("tr", [128, max(NTK * 128, F * NCL)], F32)
    sc1 = A("sc1", [128, DM], BF16)
    sc2 = A("sc2", [128, max(DM, QF)], BF16)
    kvst = [A(f"kvst{i}", [128, max(2 * KF, NTK * 128)], BF16) for i in range(2)]
    stg4 = [A(f"stg{i}", [128, 512], BF16) for i in range(4)]
    rsb = A("rsb", [128, DM], BF16)
    CLF = F * NCL
    assert 2 * CLF <= DM
    outst = [A(f"outst{i}", [128, DW], F32) for i in range(2)]
    yf = A("yf", [128, 2 * DW], F32)
    wbck = A("wbck", [128, 128], BF16)
    mskt = A("mskt", [128, 3 * 512], BF16)
    pt = [_View(sc1, 0), _View(sc1, 512), _View(sc1, 1024), _View(sc1, 1536)]
    outsg = [_View(sc1, 2048), _View(sc1, 2560)]
    sgb = _View(sc1, 3072)
    cmx = A("cmx", [128, 2 * NTK], F32)
    biasg = A("biasg", [128, 4 * NTK], F32)
    r1b = A("r1b", [1, 128], F32)
    onesf = A("onesf", [1, 128], F32)
    onescol = A("onescol", [128, 1], BF16)
    rcol = A("rcol", [128, 1], F32)
    smc = A("smc", [128, 16], F32)
    qselc = A("qselc", [128, NT], F32)
    invq = A("invq", [128, 4 * H], F32)
    invk = A("invk", [128, 4 * HKV], F32)
    clm = A("clm", [NCL, 1], F32)
    clsel = A("clsel", [NCL, 1], BF16)
    rowsb = A("rowsb", [1, 4 * 512], BF16)
    qwr = A("qwr", [1, 2 * HD], F32)
    qwrb = A("qwrb", [1, 2 * HD], BF16)
    ic = A("ic", [128, 128], BF16)
    icf = A("icf", [128, 128], F32)
    dtt = A("dtt", [128, 128], F32)
    onr = A("onr", [1, 128], BF16)
    iotc = A("iotc", [NCL, 1], F32)
    c2s = A("c2s", [128, NT * (NTK + 1)], F32)
    asn_i = A("asni", [1, 512], I32)

    PP = [nc.alloc_psum_tensor(f"pp{i}", [128, 512], F32) for i in range(8)]

    P = Prog(nc)
    bBA, bBB = Buf(), Buf()
    bXa, bXb = Buf(), Buf()
    bW = [Buf() for _ in wsl]
    bQs = [Buf(), Buf()]
    bQt = Buf()
    bTr = Buf()
    bS1, bS2 = Buf(), Buf()
    bArg = [Buf() for _ in range(NTK)]
    bPt = [Buf() for _ in range(4)]
    bOsg = [Buf(), Buf()]
    bCm, bBias, bR1, bSg, bR4, bMsk = (Buf() for _ in range(6))
    bKv = [Buf(), Buf()]
    bStg = [Buf() for _ in stg4]
    bRsb = Buf()
    bOut = [Buf(), Buf()]
    bYf = Buf()
    bWbk = Buf()
    bSmc = Buf()
    bQsel = Buf()
    bInvq, bInvk = Buf(), Buf()
    bClm, bClsel = Buf(), Buf()
    bRowsb = Buf()
    bQwr, bQwrb = Buf(), Buf()
    bConst = Buf()
    bC2 = Buf()
    bAsn = Buf()
    bP = [Buf() for _ in PP]
    bDxnT, bDqh, bDot, bDexi, bDexo, bDcli, bDclo, bDout = (Buf() for _ in range(8))

    def xn_buf(c):
        return bBA if c < F // 2 else bBB

    def MM(out_ap, lhsT_ap, rhs_ap, start, stop):
        def fn(e, a=out_ap, b=lhsT_ap, c=rhs_ap, s1=start, s2=stop):
            return e.matmul(a, b, c, start=s1, stop=s2)
        return fn

    P.dma("pool", lambda e: e.dma_start(ic[:, :], ident_bf), "cst1", w=[bConst])
    P.dma("pool", lambda e: e.dma_start(icf[:, :], ident_f32), "cst2", w=[bConst])
    P.dma("pool", lambda e: e.dma_start(dtt[:, :], d_D), "cst3", w=[bConst])
    P.dma("pool", lambda e: e.dma_start(onr[:, :], ones_bf), "cst4", w=[bConst])
    P.dma("pool", lambda e: e.dma_start(iotc[:, :], iota_cl), "cst5", w=[bConst])
    P.dma("pool", lambda e: e.dma_start(c2s[:, :], d_c2), "cst6", w=[bC2])
    P.dma("pool", lambda e: e.dma_start(mskt[:, :], d_msk), "cst12", w=[bMsk])
    P.dma("pool", lambda e: e.dma_start(onesf[:, :], ones_f32r), "cst13", w=[bConst])
    P.dma("pool", lambda e: e.dma_start(onescol[:, :], ones_bcol), "cst14", w=[bConst])
    P.dma("pool", lambda e: e.dma_start(rcol[:, :], rcol_c), "cst15", w=[bConst])
    P.dma("pool", lambda e: e.dma_start(qwr[:, 0:HD], d_qw), "cst8", w=[bQwr])
    P.dma("pool", lambda e: e.dma_start(qwr[:, HD:2 * HD], d_kw), "cst9", w=[bQwr])
    P.dma("pool", lambda e: e.dma_start(
        tr[:, 0:F * NCL].rearrange("p (c n) -> p c n", c=F),
        d_ct.rearrange("(c p) n -> p c n", p=128)), "cst11", w=[bTr])

    P.do("act", lambda e: e.activation(rsb[:, 0:CLF], tr[:, 0:F * NCL], AF.Copy),
         r=[bTr], w=[bRsb])
    P.do("dve", lambda e: e.tensor_tensor(rsb[:, CLF:2 * CLF], tr[:, 0:F * NCL], rsb[:, 0:CLF],
                                          ALU.subtract), r=[bTr, bRsb], w=[bRsb])
    P.do("dve", lambda e: e.tensor_copy(qwrb[:, HD:2 * HD], qwr[:, HD:2 * HD]),
         r=[bQwr], w=[bQwrb])
    P.do("pe", MM(PP[7][:, 0:128], onr[:, :], qwrb[:, HD:2 * HD], True, True),
         r=[bConst, bQwrb], w=[bP[7]])
    P.do("act", lambda e: e.activation(wbck[:, :], PP[7][:, 0:128], AF.Copy), r=[bP[7]], w=[bWbk])

    def _finish0():
        toks = []
        for name, (sem, val) in P.dsem.items():
            if val:
                toks.append((sem, val))
        P.wait_only("sp", toks)
        P.emit()
        return nc

    if stop_phase <= 0:
        return _finish0()
    NT1 = NT if stop_phase > 90 or stop_phase >= 1.5 else NT
    import os
    SUB = int(os.environ.get("KSUB", "99"))
    for m in range(NT if SUB >= 10 else 1):
        P.dma("sp", lambda e, m=m: e.dma_start(xa[:, 0:DM], d_x[m * 128:(m + 1) * 128, :]),
              "x0", w=[bXa])
        P.do("dve", lambda e: e.reduce_sum(smc[:, 0:1], xa[:, 0:DM], AXX), r=[bXa], w=[bSmc])
        P.do("dve", lambda e: e.tensor_scalar(smc[:, 1:2], smc[:, 0:1], 1.0 / DM, None, ALU.mult),
             r=[bSmc], w=[bSmc])
        P.do("act", lambda e: e.activation(sc2[:, 0:DM], xa[:, 0:DM], AF.Square,
                                           bias=smc[:, 1:2], scale=-1.0, accum_out=smc[:, 2:3]),
             r=[bXa, bSmc], w=[bS2, bSmc])
        P.do("dve", lambda e: e.tensor_scalar(smc[:, 3:4], smc[:, 2:3], 1.0 / DM, cfg.EPS_LN,
                                              ALU.mult, ALU.add), r=[bSmc], w=[bSmc])
        P.do("act", lambda e: e.activation(smc[:, 4:5], smc[:, 3:4], AF.Ln), r=[bSmc], w=[bSmc])
        P.do("act", lambda e: e.activation(smc[:, 5:6], smc[:, 4:5], AF.Exp, scale=-0.5),
             r=[bSmc], w=[bSmc])
        P.do("dve", lambda e: e.tensor_scalar(xb[:, 0:DM], xa[:, 0:DM], smc[:, 1:2], smc[:, 5:6],
                                              ALU.subtract, ALU.mult), r=[bXa, bSmc], w=[bXb])
        P.do("act", lambda e: e.activation(sc2[:, 0:DM], xb[:, 0:DM], AF.Copy), r=[bXb], w=[bS2])
        P.do("dve", lambda e: e.tensor_tensor(sc1[:, 0:DM], xb[:, 0:DM], sc2[:, 0:DM],
                                              ALU.subtract), r=[bXb, bS2], w=[bS1])
        if SUB < 2:
            continue
        for cb in range(0, F, 4):
            nb = min(4, F - cb)
            pih = (cb // 4) % 2
            pil = 2 + (cb // 4) % 2

            def fn_hi(e, cb=cb, nb=nb, pih=pih):
                last = None
                for i in range(nb):
                    last = e.matmul(PP[pih][:, i * 128:(i + 1) * 128],
                                    sc2[:, (cb + i) * 128:(cb + i + 1) * 128], ic[:, :],
                                    start=True, stop=True)
                return last
            P.do("pe", fn_hi, r=[bS2, bConst], w=[bP[pih]])

            def fn_lo(e, cb=cb, nb=nb, pil=pil):
                last = None
                for i in range(nb):
                    last = e.matmul(PP[pil][:, i * 128:(i + 1) * 128],
                                    sc1[:, (cb + i) * 128:(cb + i + 1) * 128], ic[:, :],
                                    start=True, stop=True)
                return last
            P.do("pe", fn_lo, r=[bS1, bConst], w=[bP[pil]])
            for i in range(nb):
                c = cb + i
                P.do("act", lambda e, c=c, i=i, pih=pih, m=m:
                     e.activation(big[:, c * TPC + m * 128:c * TPC + (m + 1) * 128],
                                  PP[pih][:, i * 128:(i + 1) * 128], AF.Copy),
                     r=[bP[pih]], w=[xn_buf(c)])
            P.do("act", lambda e, cb=cb, nb=nb, pil=pil:
                 e.activation(qs[0][:, cb * 128:(cb + nb) * 128], PP[pil][:, 0:nb * 128], AF.Copy),
                 r=[bP[pil]], w=[bQs[0]])

        if SUB < 3:
            continue

        def fn_cent(e, m=m):
            last = None
            for c in range(F):
                hiT = big[:, c * TPC + m * 128:c * TPC + (m + 1) * 128]
                loT = qs[0][:, c * 128:(c + 1) * 128]
                last = e.matmul(PP[7][:, 0:NCL], hiT, rsb[:, c * NCL:(c + 1) * NCL],
                                start=(c == 0), stop=False)
                last = e.matmul(PP[7][:, 0:NCL], hiT, rsb[:, CLF + c * NCL:CLF + (c + 1) * NCL],
                                start=False, stop=False)
                last = e.matmul(PP[7][:, 0:NCL], loT, rsb[:, c * NCL:(c + 1) * NCL],
                                start=False, stop=(c == F - 1))
            return last
        P.do("pe", fn_cent, r=[bBA, bBB, bQs[0], bRsb], w=[bP[7]])
        if SUB < 4:
            continue
        SUB2 = int(os.environ.get("KSUB2", "99"))
        P.do("dve", lambda e: e.reduce_max(smc[:, 6:7], PP[7][:, 0:NCL], AXX, negate=True),
             r=[bP[7]], w=[bSmc])
        if SUB2 < 1:
            continue
        P.do("dve", lambda e: e.tensor_scalar(smc[:, 7:8], smc[:, 6:7], 1.0 / cfg.TEMP, None,
                                              ALU.mult), r=[bSmc], w=[bSmc])
        if SUB2 < 2:
            continue
        EXPV = int(os.environ.get("KEXPV", "0"))
        if EXPV == 1:
            P.do("act", lambda e: e.activation(tr[:, 0:NCL], PP[7][:, 0:NCL], AF.Exp,
                                               bias=smc[:, 7:8], scale=1.0 / cfg.TEMP),
                 r=[bP[7], bSmc], w=[bTr, bSmc])
        elif EXPV == 2:
            P.do("act", lambda e: e.activation(tr[:, 0:NCL], PP[7][:, 0:NCL], AF.Exp,
                                               scale=1.0 / cfg.TEMP, accum_out=smc[:, 8:9]),
                 r=[bP[7], bSmc], w=[bTr, bSmc])
        elif EXPV == 3:
            P.do("dve", lambda e: e.tensor_copy(xb[:, 0:NCL], PP[7][:, 0:NCL]), r=[bP[7]], w=[bXb])
            P.do("act", lambda e: e.activation(tr[:, 0:NCL], xb[:, 0:NCL], AF.Exp,
                                               bias=smc[:, 7:8], scale=1.0 / cfg.TEMP,
                                               accum_out=smc[:, 8:9]),
                 r=[bXb, bSmc], w=[bTr, bSmc])
        elif EXPV == 5:
            P.do("act", lambda e: e.activation(smc[:, 8:9], smc[:, 7:8], AF.Copy),
                 r=[bSmc], w=[bSmc])
        elif EXPV == 6:
            P.do("act", lambda e: e.activation(sc1[:, 0:NCL], PP[7][:, 0:NCL], AF.Exp,
                                               bias=smc[:, 7:8], scale=1.0 / cfg.TEMP,
                                               accum_out=smc[:, 8:9]),
                 r=[bP[7], bSmc], w=[bS1, bSmc])
        elif EXPV == 4:
            P.do("act", lambda e: e.activation(tr[:, 0:NCL], PP[7][:, 0:NCL], AF.Copy,
                                               bias=smc[:, 7:8], scale=1.0 / cfg.TEMP,
                                               accum_out=smc[:, 8:9]),
                 r=[bP[7], bSmc], w=[bTr, bSmc])
        else:
            P.do("act", lambda e: e.activation(tr[:, 0:NCL], PP[7][:, 0:NCL], AF.Exp,
                                               bias=smc[:, 7:8], scale=1.0 / cfg.TEMP,
                                               accum_out=smc[:, 8:9]),
                 r=[bP[7], bSmc], w=[bTr, bSmc])
        if SUB2 < 3:
            continue
        P.do("dve", lambda e, m=m: e.tensor_scalar(qselc[:, m:m + 1], smc[:, 8:9],
                                                   1.0 / cfg.THRESH, None, ALU.is_lt),
             r=[bSmc], w=[bQsel])
        if SUB2 < 4:
            continue
        P.do("dve", lambda e: e.reciprocal(smc[:, 9:10], smc[:, 8:9]), r=[bSmc], w=[bSmc])
        if SUB2 < 5:
            continue
        P.do("dve", lambda e: e.tensor_scalar(tr[:, 0:NCL], tr[:, 0:NCL], smc[:, 9:10], None,
                                              ALU.mult), r=[bSmc], w=[bTr])
        if SUB < 5:
            continue
        P.do("pe", lambda e: e.transpose(PP[6][0:NCL, 0:128], tr[:, 0:NCL], icf[:, :]),
             r=[bTr, bConst], w=[bP[6]])
        P.do("dve", lambda e: e.reduce_max(smc[0:NCL, 10:11], PP[6][0:NCL, 0:128], AXX),
             r=[bP[6]], w=[bSmc])
        if m == 0:
            P.do("dve", lambda e: e.tensor_copy(clm[:, :], smc[0:NCL, 10:11]), r=[bSmc], w=[bClm])
        else:
            P.do("dve", lambda e: e.tensor_tensor(clm[:, :], clm[:, :], smc[0:NCL, 10:11],
                                                  ALU.max), r=[bSmc, bClm], w=[bClm])

    for i in range(4):
        Wc = F * TPC // 4
        P.dma("sp", lambda e, i=i, Wc=Wc: e.dma_start(d_xnT[:, i * Wc:(i + 1) * Wc],
                                                      big[:, i * Wc:(i + 1) * Wc]),
              f"xnw{i}", r=[bBA, bBB], wa=[bDxnT])
    P.dma("sp", lambda e: e.dma_start(d_cli.ap(), clm[:, :]), "clw", r=[bClm], wa=[bDcli])

    slab_ctr = [0]

    def load_slab(dram_ap, c0, nchunks, col0, width, stream):
        si = slab_ctr[0] % len(wsl)
        slab_ctr[0] += 1
        t, b = wsl[si], bW[si]
        stream = f"wsl{si}"
        src = dram_ap[c0 * 128:(c0 + nchunks) * 128, col0:col0 + width]
        P.dma("pool", lambda e, t=t, src=src, nchunks=nchunks, width=width:
              e.dma_start(t[:, 0:nchunks * width].rearrange("p (c n) -> p c n", c=nchunks),
                          src.rearrange("(c p) n -> p c n", p=128)),
              stream, w=[b])
        return t, b

    def proj_phase(dram_w, ncg, cw, nchunks_total, out_cb, which, after_first_slab=None):
        for cg in range(ncg):
            for c0 in range(0, F, NSL):
                t, b = load_slab(dram_w, c0, NSL, cg * cw, cw, "w" + which)
                if after_first_slab is not None:
                    after_first_slab()
                    after_first_slab = None
                for m in range(NT):
                    def fn(e, c0=c0, t=t, m=m, cw=cw):
                        last = None
                        for i in range(NSL):
                            c = c0 + i
                            last = e.matmul(PP[m % 8][:, 0:cw],
                                            big[:, c * TPC + m * 128:c * TPC + (m + 1) * 128],
                                            t[:, i * cw:(i + 1) * cw],
                                            start=(c == 0), stop=(c == F - 1))
                        return last
                    P.do("pe", fn, r=[bBA, bBB, b], w=[bP[m % 8]])
            for m in range(NT):
                out_cb(m, cg, PP[m % 8], bP[m % 8])

    def _finish():
        toks = []
        for name, (sem, val) in P.dsem.items():
            if val:
                toks.append((sem, val))
        P.wait_only("sp", toks)
        P.emit()
        return nc

    if stop_phase <= 1:
        return _finish()
    def kv_evac(kind):
        def cb(m, cg, psum, pbuf):
            nh = KW // 128
            if kind == "v":
                for hb in range(nh):
                    si = (m * nh + hb) % 4
                    kvh = cg * nh + hb
                    P.do("dve", lambda e, si=si, hb=hb, psum=psum:
                         e.tensor_copy(stg4[si][:, 0:128], psum[:, hb * 128:(hb + 1) * 128]),
                         r=[pbuf], w=[bStg[si]])
                    P.dma("sp", lambda e, si=si, m=m, kvh=kvh:
                          e.dma_start(d_exi.ap()[m * 128:(m + 1) * 128, KF + kvh * 128:KF + (kvh + 1) * 128],
                                      stg4[si][:, 0:128]),
                          f"stg{si}", r=[bStg[si]], wa=[bDexi])
            else:
                for hb in range(nh):
                    kvh = cg * nh + hb
                    P.do("act", lambda e, hb=hb, kvh=kvh, psum=psum:
                         e.activation(sc1[:, 0:128], psum[:, hb * 128:(hb + 1) * 128], AF.Square,
                                      accum_out=invk[:, kvh:kvh + 1]),
                         r=[pbuf], w=[bS1, bInvk])
                h0 = cg * nh
                P.do("dve", lambda e, h0=h0, nh=nh:
                     e.tensor_scalar(invk[:, HKV + h0:HKV + h0 + nh], invk[:, h0:h0 + nh],
                                     1.0 / HD, cfg.EPS_RMS, ALU.mult, ALU.add),
                     r=[bInvk], w=[bInvk])
                P.do("act", lambda e, h0=h0, nh=nh:
                     e.activation(invk[:, 2 * HKV + h0:2 * HKV + h0 + nh],
                                  invk[:, HKV + h0:HKV + h0 + nh], AF.Ln), r=[bInvk], w=[bInvk])
                P.do("act", lambda e, h0=h0, nh=nh:
                     e.activation(invk[:, 3 * HKV + h0:3 * HKV + h0 + nh],
                                  invk[:, 2 * HKV + h0:2 * HKV + h0 + nh], AF.Exp, scale=-0.5),
                     r=[bInvk], w=[bInvk])
                for hb in range(nh):
                    si = (m * nh + hb) % 4
                    kvh = cg * nh + hb
                    P.do("dve", lambda e, si=si, hb=hb, kvh=kvh, psum=psum:
                         e.scalar_tensor_tensor(stg4[si][:, 0:128], psum[:, hb * 128:(hb + 1) * 128],
                                                invk[:, 3 * HKV + kvh:3 * HKV + kvh + 1],
                                                wbck[:, :], ALU.mult, ALU.mult),
                         r=[pbuf, bInvk, bWbk], w=[bStg[si]])
                    P.dma("sp", lambda e, si=si, m=m, kvh=kvh:
                          e.dma_start(d_exi.ap()[m * 128:(m + 1) * 128, kvh * 128:(kvh + 1) * 128],
                                      stg4[si][:, 0:128]),
                          f"stg{si}", r=[bStg[si]], wa=[bDexi])
        return cb

    proj_phase(d_wk, KCG, KW, KC, kv_evac("k"), "k")
    proj_phase(d_wv, KCG, KW, KC, kv_evac("v"), "v")

    if stop_phase <= 2:
        return _finish()
    for h in range(H):
        sc = 1.0 / (math.sqrt(HD) * cfg.slopes[h])
        P.do("dve", lambda e, sc=sc: e.tensor_scalar(qwrb[:, 0:HD], qwr[:, 0:HD], sc, None,
                                                     ALU.mult), r=[bQwr], w=[bQwrb])
        pi = 4 + (h % 2)
        P.do("pe", MM(PP[pi][:, 0:128], onr[:, :], qwrb[:, 0:HD], True, True),
             r=[bConst, bQwrb], w=[bP[pi]])
        P.do("act", lambda e, h=h, pi=pi: e.activation(sc2[:, h * 128:(h + 1) * 128],
                                                       PP[pi][:, 0:128], AF.Copy),
             r=[bP[pi]], w=[bS2])

    def q_evac(m, cg, psum, pbuf):
        nh = QW // 128
        for hb in range(nh):
            h = cg * nh + hb
            P.do("act", lambda e, hb=hb, h=h, psum=psum:
                 e.activation(sc1[:, 0:128], psum[:, hb * 128:(hb + 1) * 128], AF.Square,
                              accum_out=invq[:, h:h + 1]), r=[pbuf], w=[bS1, bInvq])
        h0 = cg * nh
        P.do("dve", lambda e, h0=h0, nh=nh:
             e.tensor_scalar(invq[:, H + h0:H + h0 + nh], invq[:, h0:h0 + nh], 1.0 / HD,
                             cfg.EPS_RMS, ALU.mult, ALU.add), r=[bInvq], w=[bInvq])
        P.do("act", lambda e, h0=h0, nh=nh:
             e.activation(invq[:, 2 * H + h0:2 * H + h0 + nh], invq[:, H + h0:H + h0 + nh], AF.Ln),
             r=[bInvq], w=[bInvq])
        P.do("act", lambda e, h0=h0, nh=nh:
             e.activation(invq[:, 3 * H + h0:3 * H + h0 + nh], invq[:, 2 * H + h0:2 * H + h0 + nh],
                          AF.Exp, scale=-0.5), r=[bInvq], w=[bInvq])
        for hb in range(nh):
            si = (m * nh + hb) % 4
            h = cg * nh + hb
            P.do("dve", lambda e, si=si, hb=hb, h=h, psum=psum:
                 e.scalar_tensor_tensor(stg4[si][:, 0:128], psum[:, hb * 128:(hb + 1) * 128],
                                        invq[:, 3 * H + h:3 * H + h + 1],
                                        sc2[:, h * 128:(h + 1) * 128], ALU.mult, ALU.mult),
                 r=[pbuf, bInvq, bS2], w=[bStg[si]])
            P.dma("sp", lambda e, si=si, m=m, h=h:
                  e.dma_start(d_qh[m * 128:(m + 1) * 128, h * 128:(h + 1) * 128], stg4[si][:, 0:128]),
                  f"stg{si}", r=[bStg[si]], wa=[bDqh])

    def fn_ar(e):
        return e.collective_compute("AllReduce", ALU.max,
                                    replica_groups=[[0, 1, 2, 3, 4, 5, 6, 7]],
                                    ins=[d_cli.ap().opt()], outs=[d_clo.ap().opt()])

    def fn_ag(e):
        return e.collective_compute("AllGather", ALU.bypass,
                                    replica_groups=[[0, 1], [2, 3], [4, 5], [6, 7]],
                                    ins=[d_exi.ap().opt()], outs=[d_exo.ap().opt()])

    def emit_ccs():
        P.cc(fn_ag, "ccag", r=[bDexi], w=[bDexo])
        P.cc(fn_ar, "ccar", r=[bDcli], w=[bDclo])

    proj_phase(d_wq, QCG, QW, HC, q_evac, "q", after_first_slab=emit_ccs)

    if stop_phase <= 3:
        return _finish()
    for t in range(ST):
        rb = (t % 2) * TPC + (t // 2) * 128
        st, sb = kvst[t % 2], bKv[t % 2]
        P.dma("sp", lambda e, st=st, rb=rb: e.dma_start(st[:, 0:KF],
                                                        d_exo.ap()[rb:rb + 128, 0:KF]),
              f"exr{t % 2}", r=[bDexo], w=[sb])
        P.dma("sp", lambda e, t=t, rb=rb: e.dma_start(big[:, BIGA + t * KF:BIGA + (t + 1) * KF],
                                                      d_exo.ap()[rb:rb + 128, KF:2 * KF]),
              "exrv", r=[bDexo], w=[bBB])
        for kb in range(0, KC, 4):
            nb = min(4, KC - kb)
            pi = (kb // 4) % 2

            def fn_tr(e, st=st, kb=kb, nb=nb, pi=pi):
                last = None
                for i in range(nb):
                    last = e.matmul(PP[pi][:, i * 128:(i + 1) * 128],
                                    st[:, (kb + i) * 128:(kb + i + 1) * 128], ic[:, :],
                                    start=True, stop=True)
                return last
            P.do("pe", fn_tr, r=[sb, bConst], w=[bP[pi]])
            for i in range(nb):
                kv = kb + i
                P.do("act", lambda e, kv=kv, t=t, pi=pi, i=i:
                     e.activation(big[:, kv * S + t * 128:kv * S + (t + 1) * 128],
                                  PP[pi][:, i * 128:(i + 1) * 128], AF.Copy),
                     r=[bP[pi]], w=[bBA])

    if stop_phase <= 35:
        return _finish()
    fxab = []
    for _b in (bXa, bXb, bTr, bS1):
        if _b.w is not None:
            fxab.append(_b.w)
        fxab.extend(_b.rs)
        fxab.extend(_b.ws.values())
    r4bt = tr
    DTT0, MSKA, MSKB = 0, 512, 1024

    for j in range(NT):
        stiles = 2 * j + 2
        stg, sbq = qs[j % 2], bQs[j % 2]
        if j == 0:
            P.dma("sp", lambda e, stg=stg: e.dma_start(stg[:, 0:QF], d_qh[0:128, :]),
                  "qhr0", r=[bDqh], w=[sbq])
        for cb in range(0, HC, 4):
            pi = 6 + (cb // 4) % 2

            def fn_trq(e, stg=stg, cb=cb, pi=pi):
                last = None
                for i in range(4):
                    last = e.matmul(PP[pi][:, i * 128:(i + 1) * 128],
                                    stg[:, (cb + i) * 128:(cb + i + 1) * 128], ic[:, :],
                                    start=True, stop=True)
                return last
            P.do("pe", fn_trq, r=[sbq, bConst], w=[bP[pi]])
            P.do("act", lambda e, cb=cb, pi=pi:
                 e.activation(qt[:, cb * 128:(cb + 4) * 128], PP[pi][:, 0:512], AF.Copy),
                 r=[bP[pi]], w=[bQt])
        if j + 1 < NT:
            stg2, sbq2 = qs[(j + 1) % 2], bQs[(j + 1) % 2]
            P.dma("sp", lambda e, stg2=stg2, j=j:
                  e.dma_start(stg2[:, 0:QF], d_qh[(j + 1) * 128:(j + 2) * 128, :]),
                  f"qhr{(j + 1) % 2}", r=[bDqh], w=[sbq2])

        cc0 = j * (NTK + 1)

        def emit_sttr(g, kt, j=j):
            pb = kt % 2
            P.do("pe", MM(PP[pb][:, 0:512],
                          big[:, g * S + kt * 128:g * S + (kt + 1) * 128],
                          qt[:, g * 512:(g + 1) * 512], True, True),
                 r=[bBA, bQt], w=[bP[pb]])
            mo = MSKA if kt == 2 * j else (MSKB if kt == 2 * j + 1 else DTT0)
            P.do("dve", lambda e, kt=kt, pb=pb, mo=mo, cc0=cc0:
                 e.scalar_tensor_tensor(argb[:, kt * 512:(kt + 1) * 512],
                                        PP[pb][:, 0:512], c2s[:, cc0 + kt:cc0 + kt + 1],
                                        mskt[:, mo:mo + 512], ALU.add, ALU.add),
                 r=[bP[pb], bMsk, bC2], w=[bArg[kt]], deps=fxab)
            P.do("dve", lambda e, kt=kt:
                 e.reduce_max(cmx[:, kt:kt + 1], argb[:, kt * 512:(kt + 1) * 512],
                              AXX),
                 r=[bArg[kt]], w=[bCm])

        def emit_b1tail(g, j=j, stiles=stiles):
            P.do("dve", lambda e, stiles=stiles:
                 e.reduce_max(smc[:, 0:1], cmx[:, 0:stiles], AXX),
                 r=[bCm], w=[bSmc])

        def emit_b2(g, j=j, stiles=stiles):
            P.do("pe", lambda e: e.matmul(PP[6][0:1, 0:128], smc[:, 0:1], icf[:, :],
                                          start=True, stop=True),
                 r=[bSmc, bConst], w=[bP[6]])
            P.do("dve", lambda e: e.reduce_max(r1b[0:1, 32:33], PP[6][0:1, 0:128], AXX),
                 r=[bP[6]], w=[bR1])
            for i in range(4):
                h = 4 * g + i
                P.do("dve", lambda e, i=i, h=h:
                     e.tensor_scalar(r1b[0:1, 64 + i:64 + i + 1],
                                     r1b[0:1, 32:33], -cfg.slopes[h], None, ALU.mult),
                     r=[bR1], w=[bR1])
            P.do("pe", lambda e: e.matmul(PP[6][:, 0:4], onesf[:, :], r1b[0:1, 64:68],
                                          start=True, stop=True),
                 r=[bR1, bConst], w=[bP[6]])
            P.do("dve", lambda e: e.tensor_copy(biasg[:, 0:4], PP[6][:, 0:4]),
                 r=[bP[6]], w=[bBias])

        def emit_cexp(g, kt0, j=j, stiles=stiles):
            qn = min(4, stiles - kt0)
            for i in range(4):
                h = 4 * g + i
                inv = argb[:, kt0 * 512:(kt0 + qn) * 512].rearrange(
                    "p (u c) -> p u c", u=qn)[:, :, i * 128:(i + 1) * 128]
                outv = sc1[:, 0:qn * 512].rearrange(
                    "p (u c) -> p u c", u=qn)[:, :, i * 128:(i + 1) * 128]
                P.do("act", lambda e, inv=inv, outv=outv, h=h:
                     e.activation(outv, inv, AF.Exp, bias=biasg[:, h - 4 * g:h - 4 * g + 1],
                                  scale=cfg.slopes[h]),
                     r=[bArg[u] for u in range(kt0, kt0 + qn)] + [bBias],
                     w=[bPt[u] for u in range(qn)], deps=fxab)

        def emit_ck(g, kt, j=j, stiles=stiles):
            po, ps = 2 + (g % 2), 4 + (g % 2)
            tp = kt % 4
            P.do("pe", lambda e, kt=kt, tp=tp, ps=ps, stiles=stiles:
                 e.matmul(PP[ps][0:1, 0:512], onescol[:, :], pt[tp][:, 0:512],
                          start=(kt == 0), stop=(kt == stiles - 1)),
                 r=[bPt[tp], bConst], w=[bP[ps]])
            P.do("pe", lambda e, kt=kt, tp=tp, po=po, g=g, stiles=stiles:
                 e.matmul(PP[po][:, 0:512],
                          big[:, BIGA + kt * KF + g * 128:BIGA + kt * KF + g * 128 + 128],
                          pt[tp][:, 0:512],
                          start=(kt == 0), stop=(kt == stiles - 1)),
                 r=[bPt[tp], bBB], w=[bP[po]])

        def emit_d(g, j=j):
            po, ps = 2 + (g % 2), 4 + (g % 2)
            og = g % 2
            P.do("dve", lambda e, ps=ps: e.tensor_copy(sgb[0:1, 0:512], PP[ps][0:1, 0:512]),
                 r=[bP[ps]], w=[bSg], deps=fxab)
            P.do("pe", lambda e: e.matmul(PP[7][:, 0:512], onr[:, :], sgb[0:1, 0:512],
                                          start=True, stop=True),
                 r=[bSg, bConst], w=[bP[7]])
            P.do("act", lambda e: e.activation(r4bt[:, 512:1024], PP[7][:, 0:512], AF.Ln),
                 r=[bP[7]], w=[bR4], deps=fxab)
            P.do("act", lambda e: e.activation(r4bt[:, 1024:1536], r4bt[:, 512:1024],
                                               AF.Exp, scale=-1.0),
                 r=[bR4], w=[bR4], deps=fxab)
            P.do("dve", lambda e, po=po, og=og:
                 e.tensor_tensor(outsg[og][:, 0:512], PP[po][:, 0:512], r4bt[:, 1024:1536],
                                 ALU.mult),
                 r=[bP[po], bR4], w=[bOsg[og]], deps=fxab)
            for i in range(4):
                h = 4 * g + i
                P.dma("sp", lambda e, h=h, j=j, i=i, og=og:
                      e.dma_start(d_ot[h, :, j * 128:(j + 1) * 128],
                                  outsg[og][:, i * 128:(i + 1) * 128]),
                      f"otw{og}", r=[bOsg[og]], wa=[bDot])

        for g in range(HKV):
            for kt in range(stiles):
                if g > 0:
                    if kt % 4 == 0:
                        emit_cexp(g - 1, kt)
                    emit_ck(g - 1, kt)
                emit_sttr(g, kt)
            emit_b1tail(g)
            emit_b2(g)
            if g > 1:
                emit_d(g - 2)
        for kt in range(stiles):
            if kt % 4 == 0:
                emit_cexp(HKV - 1, kt)
            emit_ck(HKV - 1, kt)
        emit_d(HKV - 2)
        emit_d(HKV - 1)

    if stop_phase <= 4:
        return _finish()
    for i in range(4):
        Wc = F * TPC // 4
        P.dma("act", lambda e, i=i, Wc=Wc: e.dma_start(
            big[:, i * Wc:(i + 1) * Wc], d_xnT[:, i * Wc:(i + 1) * Wc]),
              f"xnr{i}", r=[bDxnT], w=[bBA] if i < 2 else [bBB])
    P.dma("sp", lambda e: e.dma_start(clm[:, :], d_clo.ap()), "clr", r=[bDclo], w=[bClm])
    P.do("dve", lambda e: e.tensor_scalar(clsel[:, :], clm[:, :], cfg.THRESH, None, ALU.is_gt),
         r=[bClm], w=[bClsel])
    for gc in range(DCG):
        P.dma("pool", lambda e, gc=gc: e.dma_start(asn_i[:, 0:DW], d_asn[:, gc * DW:(gc + 1) * DW]),
              "asnl", r=[bAsn], w=[bAsn])
        P.do("dve", lambda e: e.tensor_copy(yf[0:1, 0:DW], asn_i[:, 0:DW]), r=[bAsn], w=[bYf])
        P.do("dve", lambda e: e.tensor_copy(rowsb[:, 1024:1024 + DW], yf[0:1, 0:DW]),
             r=[bYf], w=[bRowsb])
        P.do("pe", MM(PP[2][0:NCL, 0:DW], onr[:, 0:NCL], rowsb[:, 1024:1024 + DW], True, True),
             r=[bConst, bRowsb], w=[bP[2]])
        P.do("dve", lambda e, gc=gc: e.tensor_scalar(qt[0:NCL, gc * DW:(gc + 1) * DW],
                                                     PP[2][0:NCL, 0:DW], iotc[:, :], None,
                                                     ALU.is_equal),
             r=[bP[2], bConst], w=[bQt])
        P.do("pe", lambda e, gc=gc: e.matmul(PP[3][0:1, 0:DW], clsel[:, :],
                                             qt[0:NCL, gc * DW:(gc + 1) * DW],
                                             start=True, stop=True),
             r=[bClsel, bQt], w=[bP[3]])
        P.do("dve", lambda e: e.tensor_copy(rowsb[:, 1536:1536 + DW], PP[3][0:1, 0:DW]),
             r=[bP[3]], w=[bRowsb])
        P.do("pe", lambda e: e.matmul(PP[2][:, 0:DW], onr[:, :], rowsb[:, 1536:1536 + DW],
                                      start=True, stop=True),
             r=[bConst, bRowsb], w=[bP[2]])
        P.do("dve", lambda e, gc=gc: e.tensor_copy(rsb[:, gc * DW:(gc + 1) * DW], PP[2][:, 0:DW]),
             r=[bP[2]], w=[bRsb])

    fx5 = []
    for _b in bArg:
        if _b.w is not None:
            fx5.append(_b.w)
        fx5.extend(_b.rs)
        fx5.extend(_b.ws.values())
    fq5 = []
    for _b in bQs:
        if _b.w is not None:
            fq5.append(_b.w)
        fq5.extend(_b.rs)
        fq5.extend(_b.ws.values())
    bOt6 = [Buf() for _ in range(6)]
    for cg in range(DCG):
        P.dma("sp", lambda e, cg=cg: e.dma_start(
            xa[:, 0:NT * DW].rearrange("p (m n) -> p m n", m=NT),
            d_x[:, cg * DW:(cg + 1) * DW].rearrange("(m p) n -> p m n", p=128)),
            "xs", w=[bXa], deps=fx5)
        P.dma("pool", lambda e, cg=cg: e.dma_start(yf[0:1, 0:DW], d_bias[:, cg * DW:(cg + 1) * DW]),
              "bil", r=[bYf], w=[bYf])
        P.do("act", lambda e: e.activation(rowsb[:, 0:DW], yf[0:1, 0:DW], AF.Copy),
             r=[bYf], w=[bRowsb])
        P.do("dve", lambda e: e.tensor_tensor(rowsb[:, 512:512 + DW], yf[0:1, 0:DW],
                                              rowsb[:, 0:DW], ALU.subtract),
             r=[bYf, bRowsb], w=[bRowsb])
        allP = [bP[mm % 8] for mm in range(PCNT)]
        OTQ = ("sp", "act")
        for c0 in range(0, HC, NSL):
            ns = min(NSL, HC - c0)
            t, b = load_slab(d_wo, c0, ns, cg * DW, DW, "wo")
            for i in range(ns):
                c = c0 + i
                sl = c % 6
                stg = qs[0] if sl < 4 else qs[1]
                off = (sl % 4) * TPC
                sbo = bOt6[sl]
                P.dma(OTQ[c % 2], lambda e, stg=stg, off=off, c=c: e.dma_start(
                    stg[:, off:off + TPC], d_ot[c, :, :]),
                    f"otr{c % 2}", r=[bDot], w=[sbo], deps=fq5)

                def fn(e, stg=stg, off=off, t=t, i=i, c=c):
                    last = None
                    for m in range(NT):
                        last = e.matmul(PP[m % 8][:, 0:DW],
                                        stg[:, off + m * 128:off + (m + 1) * 128],
                                        t[:, i * DW:(i + 1) * DW],
                                        start=(c == 0), stop=(c == HC - 1))
                    return last
                P.do("pe", fn, r=[sbo, b], w=allP)
        for m in range(NT):
            P.do("dve", lambda e, m=m: e.tensor_copy(xb[:, m * DW:(m + 1) * DW],
                                                     PP[m % 8][:, 0:DW]),
                 r=[bP[m % 8]], w=[bXb], deps=fx5)
        for c0 in range(0, F, NSL):
            t, b = load_slab(d_wf, c0, NSL, cg * DW, DW, "wf")
            for i in range(NSL):
                c = c0 + i

                def fn(e, t=t, i=i, c=c):
                    last = None
                    for m in range(NT):
                        last = e.matmul(PP[m % 8][:, 0:DW],
                                        big[:, c * TPC + m * 128:c * TPC + (m + 1) * 128],
                                        t[:, i * DW:(i + 1) * DW],
                                        start=(c == 0), stop=False)
                    return last
                P.do("pe", fn, r=[bBA, bBB, b], w=allP)

        def fn_bias(e, cg=cg):
            last = None
            for m in range(NT):
                last = e.matmul(PP[m % 8][:, 0:DW], onr[:, :],
                                rowsb[:, 0:DW], start=False, stop=False)
                last = e.matmul(PP[m % 8][:, 0:DW], onr[:, :],
                                rowsb[:, 512:512 + DW], start=False, stop=True)
            return last
        P.do("pe", fn_bias, r=[bConst, bRowsb], w=allP)
        for m in range(NT):
            osl, obf = outst[m % 2], bOut[m % 2]
            P.do("dve", lambda e, m=m, cg=cg:
                 e.scalar_tensor_tensor(yf[:, 0:DW], PP[m % 8][:, 0:DW], qselc[:, m:m + 1],
                                        rsb[:, cg * DW:(cg + 1) * DW], ALU.mult, ALU.mult),
                 r=[bP[m % 8], bQsel, bRsb], w=[bYf])
            P.do("dve", lambda e, m=m:
                 e.tensor_tensor(yf[:, DW:2 * DW], yf[:, 0:DW], xa[:, m * DW:(m + 1) * DW],
                                 ALU.add), r=[bYf, bXa], w=[bYf])
            P.do("dve", lambda e, m=m, osl=osl:
                 e.tensor_tensor(osl[:, :], yf[:, DW:2 * DW], xb[:, m * DW:(m + 1) * DW],
                                 ALU.add), r=[bYf, bXb], w=[obf])
            P.dma("sp", lambda e, m=m, cg=cg, osl=osl:
                  e.dma_start(d_out[m * 128:(m + 1) * 128, cg * DW:(cg + 1) * DW], osl[:, :]),
                  f"outw{m % 2}", r=[obf], wa=[bDout])

    P.wait_only("sp", list(bDout.ws.values()) + ([bDout.w] if bDout.w else []))
    P.emit()
    return nc



def make_inputs(cfg, x, Wq, Wk, Wv, Wo, qw, kw, Wf, fbias, cent, asn):
    B, S, DM, NT = cfg.B, cfg.S, cfg.DM, cfg.NT
    x = np.asarray(x, dtype=np.float32)
    shared = {
        "wqT": np.ascontiguousarray(np.asarray(Wq, dtype=np.float32).T),
        "wkT": np.ascontiguousarray(np.asarray(Wk, dtype=np.float32).T),
        "wvT": np.ascontiguousarray(np.asarray(Wv, dtype=np.float32).T),
        "woT": np.ascontiguousarray(np.asarray(Wo, dtype=np.float32).T),
        "wfT": np.ascontiguousarray(np.asarray(Wf, dtype=np.float32).T),
        "ctT": np.ascontiguousarray(np.asarray(cent, dtype=np.float32).T),
        "fbias": np.asarray(fbias, dtype=np.float32).reshape(1, DM),
        "qnw": np.asarray(qw, dtype=np.float32).reshape(1, cfg.HD),
        "knw": np.asarray(kw, dtype=np.float32).reshape(1, cfg.HD),
        "asn": np.asarray(asn).reshape(1, DM).astype(np.int32),
    }
    xt = x.reshape(B, cfg.ST, 128, DM)
    NTK = 2 * NT
    dttn = (np.arange(128)[:, None] - np.arange(128)[None, :]).astype(np.float32)
    tri = (dttn > 0).astype(np.float32)
    in_maps = []
    for c in range(8):
        b, par = c // 2, c % 2
        xs = np.ascontiguousarray(xt[b, par::2].reshape(cfg.TPC, DM))
        c2 = np.zeros((128, NT * (NTK + 1)), dtype=np.float32)
        for j in range(NT):
            for kt in range(2 * j + 2):
                c2[:, j * (NTK + 1) + kt] = 128.0 * (kt - (2 * j + par))
        if par == 0:
            mskA = dttn + tri * (-cfg.NEGBIG)
            mskB = dttn - cfg.NEGBIG
        else:
            mskA = dttn
            mskB = dttn + tri * (-cfg.NEGBIG)
        m3 = np.concatenate([np.tile(dttn, (1, 4)), np.tile(mskA, (1, 4)),
                             np.tile(mskB, (1, 4))], axis=1).astype(BF)
        in_maps.append(dict(shared, xt=xs, c2in=c2, mskin=m3))
    return in_maps


def assemble(cfg, outs):
    B, S, DM, NT = cfg.B, cfg.S, cfg.DM, cfg.NT
    full = np.zeros((B, S, DM), dtype=np.float32)
    for c in range(8):
        b, par = c // 2, c % 2
        o = np.asarray(outs[c]).reshape(NT, 128, DM)
        for t in range(NT):
            full[b, (2 * t + par) * 128:(2 * t + par + 1) * 128, :] = o[t]
    return full


_FULL = Cfg()


def kernel(L0_attn_x, L0_attn_Wq, L0_attn_Wk, L0_attn_Wv, L0_attn_Wo,
           L0_attn_q_norm_w, L0_attn_k_norm_w,
           L0_ffn_weight, L0_ffn_bias, L0_ffn_centroids, L0_ffn_assignments):
    cfg = _FULL
    nc = build_program(cfg)
    in_maps = make_inputs(cfg, L0_attn_x, L0_attn_Wq, L0_attn_Wk, L0_attn_Wv, L0_attn_Wo,
                          L0_attn_q_norm_w, L0_attn_k_norm_w, L0_ffn_weight, L0_ffn_bias,
                          L0_ffn_centroids, L0_ffn_assignments)
    res = run_bass_kernel_spmd(nc, in_maps, core_ids=list(range(8)))
    return assemble(cfg, [r["out"] for r in res.results])


# version 16
# speedup vs baseline: 1.0710x; 1.0072x over previous
import math
import numpy as np
import ml_dtypes
import concourse.bass as bass
import concourse.mybir as mybir
from concourse.bass_utils import run_bass_kernel_spmd

F32 = mybir.dt.float32
BF16 = mybir.dt.bfloat16
I32 = mybir.dt.int32
AF = mybir.ActivationFunctionType
ALU = mybir.AluOpType
AXX = mybir.AxisListType.X
BF = ml_dtypes.bfloat16


class Cfg:
    def __init__(self, DM=4096, S=2048, H=32, HKV=8, NCL=64):
        self.B = 4
        self.DM = DM
        self.S = S
        self.H = H
        self.HKV = HKV
        self.G = H // HKV
        self.HD = 128
        self.NCL = NCL
        self.F = DM // 128
        self.QF = H * self.HD
        self.KF = HKV * self.HD
        self.NT = S // 256
        self.TPC = self.NT * 128
        self.ST = S // 128
        start = 2 ** (-(2 ** (-(math.log2(H) - 3))))
        self.slopes = [start * start ** i for i in range(H)]
        self.EPS_LN = 1e-5
        self.EPS_RMS = 1e-6
        self.TEMP = 0.1
        self.THRESH = 0.01
        self.NEGBIG = 3.0e32


ENGS = ("pe", "dve", "act", "pool", "sp")


class Buf:
    __slots__ = ("w", "rs", "ws")

    def __init__(self):
        self.w = None
        self.rs = []
        self.ws = {}


class Prog:
    def __init__(self, nc):
        self.nc = nc
        self.ops = {e: [] for e in ENGS}
        self.cnt = {e: 0 for e in ENGS}
        self.esem = {e: nc.alloc_semaphore(f"es_{e}") for e in ENGS}
        self.pe_sem_name = self.esem["pe"].name
        self.wm = {e: {} for e in ENGS}
        self.dsem = {}

    def _waits(self, e, deps):
        best = {}
        for d in deps:
            if d is None:
                continue
            sem, val = d
            k = sem.name
            if e == "pe" and k == self.pe_sem_name:
                continue
            if k not in best or val > best[k][1]:
                best[k] = (sem, val)
        waits = []
        for k, (sem, val) in sorted(best.items()):
            if self.wm[e].get(k, 0) >= val:
                continue
            waits.append((sem, val))
            self.wm[e][k] = val
        return waits

    def _deps(self, r, w, wa, deps):
        out = list(deps)
        for b in r:
            if b.w is not None:
                out.append(b.w)
            out.extend(b.ws.values())
        for b in w:
            if b.w is not None:
                out.append(b.w)
            out.extend(b.ws.values())
            out.extend(b.rs)
        for b in wa:
            pass
        return out

    def _book(self, tok, r, w, wa):
        for b in r:
            b.rs.append(tok)
        for b in w:
            b.w = tok
            b.rs = []
        for b in wa:
            b.ws[tok[0].name] = tok
        return tok

    def do(self, e, fn, r=(), w=(), wa=(), deps=()):
        waits = self._waits(e, self._deps(r, w, wa, deps))
        self.cnt[e] += 1
        tok = (self.esem[e], self.cnt[e])
        self.ops[e].append((waits, fn, self.esem[e], 1))
        return self._book(tok, r, w, wa)

    def dma(self, q, fn, stream, r=(), w=(), wa=(), deps=(), inc=16):
        waits = self._waits(q, self._deps(r, w, wa, deps))
        if stream not in self.dsem:
            self.dsem[stream] = [self.nc.alloc_semaphore("ds_" + stream), 0]
        ent = self.dsem[stream]
        ent[1] += inc
        tok = (ent[0], ent[1])
        self.ops[q].append((waits, fn, ent[0], inc))
        return self._book(tok, r, w, wa)

    def cc(self, fn, stream, r=(), w=(), deps=()):
        return self.dma("pool", fn, stream, r=r, w=w, deps=deps, inc=1)

    def wait_only(self, e, deps):
        waits = self._waits(e, [d for d in deps if d is not None])
        if waits:
            self.ops[e].append((waits, None, None, 0))

    def emit(self):
        nc = self.nc

        def mk(e):
            def body(eng):
                for waits, fn, sem, inc in self.ops[e]:
                    for (s, v) in waits:
                        eng.wait_ge(s, v)
                    if fn is not None:
                        inst = fn(eng)
                        if inst is not None and sem is not None:
                            inst.then_inc(sem, inc)
            return body

        with nc.Block() as block:
            block.sync(mk("sp"))
            block.gpsimd(mk("pool"))
            block.tensor(mk("pe"))
            block.vector(mk("dve"))
            block.scalar(mk("act"))


def build_program(cfg: Cfg, stop_phase=99):
    nc = bass.Bass("TRN2", target_bir_lowering=False)
    DM, S, H, HKV, G, HD = cfg.DM, cfg.S, cfg.H, cfg.HKV, cfg.G, cfg.HD
    F, QF, KF, NT, NCL, ST = cfg.F, cfg.QF, cfg.KF, cfg.NT, cfg.NCL, cfg.ST
    TPC = cfg.TPC
    HC = QF // 128
    KC = KF // 128
    QCG, QW = (QF // 512, 512) if QF >= 512 else (1, QF)
    KCG, KW = (KF // 512, 512) if KF >= 512 else (1, KF)
    DCG, DW = (DM // 512, 512) if DM >= 512 else (1, DM)
    NSL = 4 if F % 4 == 0 else F
    NTK = 2 * NT
    BIGA = HKV * S
    BIGC = max(2 * ST * KF, F * TPC)
    PCNT = min(NT, 8)

    d_x = nc.dram_tensor("xt", [TPC, DM], F32, kind="ExternalInput").ap()
    d_c2 = nc.dram_tensor("c2in", [128, NT * (NTK + 1)], F32, kind="ExternalInput").ap()
    d_msk = nc.dram_tensor("mskin", [128, 3 * 512], BF16, kind="ExternalInput").ap()
    d_wq = nc.dram_tensor("wqT", [DM, QF], F32, kind="ExternalInput").ap()
    d_wk = nc.dram_tensor("wkT", [DM, KF], F32, kind="ExternalInput").ap()
    d_wv = nc.dram_tensor("wvT", [DM, KF], F32, kind="ExternalInput").ap()
    d_wo = nc.dram_tensor("woT", [QF, DM], F32, kind="ExternalInput").ap()
    d_wf = nc.dram_tensor("wfT", [DM, DM], F32, kind="ExternalInput").ap()
    d_ct = nc.dram_tensor("ctT", [DM, NCL], F32, kind="ExternalInput").ap()
    d_bias = nc.dram_tensor("fbias", [1, DM], F32, kind="ExternalInput").ap()
    d_qw = nc.dram_tensor("qnw", [1, HD], F32, kind="ExternalInput").ap()
    d_kw = nc.dram_tensor("knw", [1, HD], F32, kind="ExternalInput").ap()
    d_asn = nc.dram_tensor("asn", [1, DM], I32, kind="ExternalInput").ap()
    d_out = nc.dram_tensor("out", [TPC, DM], F32, kind="ExternalOutput").ap()

    d_xnT = nc.dram_tensor("xnT_s", [128, F * TPC], BF16).ap()
    d_qh = nc.dram_tensor("qh_s", [TPC, QF], BF16).ap()
    d_ot = nc.dram_tensor("ot_s", [HC, 128, NT * 128], BF16).ap()
    d_exi = nc.dram_tensor("exi_s", [TPC, 2 * KF], BF16)
    d_exo = nc.dram_tensor("exo_s", [2 * TPC, 2 * KF], BF16)
    d_cli = nc.dram_tensor("cli_s", [NCL, 1], F32)
    d_clo = nc.dram_tensor("clo_s", [NCL, 1], F32)

    ident_bf = nc.inline_tensor(np.eye(128, dtype=np.float32).astype(BF), "c_ident").ap()
    ident_f32 = nc.inline_tensor(np.eye(128, dtype=np.float32), "c_identf").ap()
    dnp = (np.arange(128)[None, :] - np.arange(128)[:, None]).astype(np.float32)
    d_D = nc.inline_tensor(dnp, "c_D").ap()
    ones_bf = nc.inline_tensor(np.ones((1, 128), dtype=np.float32).astype(BF), "c_ones").ap()
    iota_cl = nc.inline_tensor(np.arange(NCL, dtype=np.float32)[:, None], "c_iota").ap()
    ones_f32r = nc.inline_tensor(np.ones((1, 128), dtype=np.float32), "c_onesf").ap()
    rcol_c = nc.inline_tensor(np.arange(128, dtype=np.float32)[:, None], "c_rcol").ap()
    ones_bcol = nc.inline_tensor(np.ones((128, 1), dtype=np.float32).astype(BF), "c_onesc").ap()

    A = nc.alloc_sbuf_tensor
    big = A("bigb", [128, BIGC], BF16)
    argb = A("argb", [128, NTK * 512], F32)

    class _View:
        __slots__ = ("t", "off")

        def __init__(self, t, off):
            self.t, self.off = t, off

        def __getitem__(self, idx):
            p, f = idx
            return self.t[p, f.start + self.off:f.stop + self.off]

    xa = _View(argb, 0)
    xb = _View(argb, DM)
    wsl = [A(f"wsl{i}", [128, NSL * 512], BF16) for i in range(3)]
    qs = [A(f"qs{i}", [128, max(QF, DM, TPC)], BF16) for i in range(2)]
    qt = A("qt", [128, max(QF, DM)], BF16)
    tr = A("tr", [128, max(NTK * 128, F * NCL)], F32)
    sc1 = A("sc1", [128, DM], BF16)
    sc2 = A("sc2", [128, max(DM, QF)], BF16)
    kvst = [A(f"kvst{i}", [128, max(2 * KF, NTK * 128)], BF16) for i in range(2)]
    stg4 = [A(f"stg{i}", [128, 512], BF16) for i in range(4)]
    rsb = A("rsb", [128, DM], BF16)
    CLF = F * NCL
    assert 2 * CLF <= DM
    outst = [A(f"outst{i}", [128, DW], F32) for i in range(2)]
    yf = A("yf", [128, 2 * DW], F32)
    wbck = A("wbck", [128, 128], BF16)
    mskt = A("mskt", [128, 3 * 512], BF16)
    pt = [_View(sc1, 0), _View(sc1, 512), _View(sc1, 1024), _View(sc1, 1536)]
    outsg = [_View(sc1, 2048), _View(sc1, 2560)]
    sgb = _View(sc1, 3072)
    cmx = A("cmx", [128, 2 * NTK], F32)
    biasg = A("biasg", [128, 4 * NTK], F32)
    r1b = A("r1b", [1, 128], F32)
    onesf = A("onesf", [1, 128], F32)
    onescol = A("onescol", [128, 1], BF16)
    rcol = A("rcol", [128, 1], F32)
    smc = A("smc", [128, 16], F32)
    qselc = A("qselc", [128, NT], F32)
    invq = A("invq", [128, 4 * H], F32)
    invk = A("invk", [128, 4 * HKV], F32)
    clm = A("clm", [NCL, 1], F32)
    clsel = A("clsel", [NCL, 1], BF16)
    rowsb = A("rowsb", [1, 4 * 512], BF16)
    qwr = A("qwr", [1, 2 * HD], F32)
    qwrb = A("qwrb", [1, 2 * HD], BF16)
    ic = A("ic", [128, 128], BF16)
    icf = A("icf", [128, 128], F32)
    dtt = A("dtt", [128, 128], F32)
    onr = A("onr", [1, 128], BF16)
    iotc = A("iotc", [NCL, 1], F32)
    c2s = A("c2s", [128, NT * (NTK + 1)], F32)
    asn_i = A("asni", [1, 512], I32)

    PP = [nc.alloc_psum_tensor(f"pp{i}", [128, 512], F32) for i in range(8)]

    P = Prog(nc)
    bBA, bBB = Buf(), Buf()
    bXa, bXb = Buf(), Buf()
    bW = [Buf() for _ in wsl]
    bQs = [Buf(), Buf()]
    bQt = Buf()
    bTr = Buf()
    bS1, bS2 = Buf(), Buf()
    bArg = [Buf() for _ in range(NTK)]
    bPt = [Buf() for _ in range(4)]
    bOsg = [Buf(), Buf()]
    bCm, bBias, bR1, bSg, bR4, bMsk = (Buf() for _ in range(6))
    bKv = [Buf(), Buf()]
    bStg = [Buf() for _ in stg4]
    bRsb = Buf()
    bOut = [Buf(), Buf()]
    bYf = Buf()
    bWbk = Buf()
    bSmc = Buf()
    bQsel = Buf()
    bInvq, bInvk = Buf(), Buf()
    bClm, bClsel = Buf(), Buf()
    bRowsb = Buf()
    bQwr, bQwrb = Buf(), Buf()
    bConst = Buf()
    bC2 = Buf()
    bAsn = Buf()
    bP = [Buf() for _ in PP]
    bDxnT, bDqh, bDot, bDexi, bDexo, bDcli, bDclo, bDout = (Buf() for _ in range(8))

    def xn_buf(c):
        return bBA if c < F // 2 else bBB

    def MM(out_ap, lhsT_ap, rhs_ap, start, stop):
        def fn(e, a=out_ap, b=lhsT_ap, c=rhs_ap, s1=start, s2=stop):
            return e.matmul(a, b, c, start=s1, stop=s2)
        return fn

    P.dma("pool", lambda e: e.dma_start(ic[:, :], ident_bf), "cst1", w=[bConst])
    P.dma("pool", lambda e: e.dma_start(icf[:, :], ident_f32), "cst2", w=[bConst])
    P.dma("pool", lambda e: e.dma_start(dtt[:, :], d_D), "cst3", w=[bConst])
    P.dma("pool", lambda e: e.dma_start(onr[:, :], ones_bf), "cst4", w=[bConst])
    P.dma("pool", lambda e: e.dma_start(iotc[:, :], iota_cl), "cst5", w=[bConst])
    P.dma("pool", lambda e: e.dma_start(c2s[:, :], d_c2), "cst6", w=[bC2])
    P.dma("pool", lambda e: e.dma_start(mskt[:, :], d_msk), "cst12", w=[bMsk])
    P.dma("pool", lambda e: e.dma_start(onesf[:, :], ones_f32r), "cst13", w=[bConst])
    P.dma("pool", lambda e: e.dma_start(onescol[:, :], ones_bcol), "cst14", w=[bConst])
    P.dma("pool", lambda e: e.dma_start(rcol[:, :], rcol_c), "cst15", w=[bConst])
    P.dma("pool", lambda e: e.dma_start(qwr[:, 0:HD], d_qw), "cst8", w=[bQwr])
    P.dma("pool", lambda e: e.dma_start(qwr[:, HD:2 * HD], d_kw), "cst9", w=[bQwr])
    P.dma("pool", lambda e: e.dma_start(
        tr[:, 0:F * NCL].rearrange("p (c n) -> p c n", c=F),
        d_ct.rearrange("(c p) n -> p c n", p=128)), "cst11", w=[bTr])

    P.do("act", lambda e: e.activation(rsb[:, 0:CLF], tr[:, 0:F * NCL], AF.Copy),
         r=[bTr], w=[bRsb])
    P.do("dve", lambda e: e.tensor_tensor(rsb[:, CLF:2 * CLF], tr[:, 0:F * NCL], rsb[:, 0:CLF],
                                          ALU.subtract), r=[bTr, bRsb], w=[bRsb])
    P.do("dve", lambda e: e.tensor_copy(qwrb[:, HD:2 * HD], qwr[:, HD:2 * HD]),
         r=[bQwr], w=[bQwrb])
    P.do("pe", MM(PP[7][:, 0:128], onr[:, :], qwrb[:, HD:2 * HD], True, True),
         r=[bConst, bQwrb], w=[bP[7]])
    P.do("act", lambda e: e.activation(wbck[:, :], PP[7][:, 0:128], AF.Copy), r=[bP[7]], w=[bWbk])

    def _finish0():
        toks = []
        for name, (sem, val) in P.dsem.items():
            if val:
                toks.append((sem, val))
        P.wait_only("sp", toks)
        P.emit()
        return nc

    if stop_phase <= 0:
        return _finish0()
    NT1 = NT if stop_phase > 90 or stop_phase >= 1.5 else NT
    import os
    SUB = int(os.environ.get("KSUB", "99"))
    for m in range(NT if SUB >= 10 else 1):
        P.dma("sp", lambda e, m=m: e.dma_start(xa[:, 0:DM], d_x[m * 128:(m + 1) * 128, :]),
              "x0", w=[bXa])
        P.do("dve", lambda e: e.reduce_sum(smc[:, 0:1], xa[:, 0:DM], AXX), r=[bXa], w=[bSmc])
        P.do("dve", lambda e: e.tensor_scalar(smc[:, 1:2], smc[:, 0:1], 1.0 / DM, None, ALU.mult),
             r=[bSmc], w=[bSmc])
        P.do("act", lambda e: e.activation(sc2[:, 0:DM], xa[:, 0:DM], AF.Square,
                                           bias=smc[:, 1:2], scale=-1.0, accum_out=smc[:, 2:3]),
             r=[bXa, bSmc], w=[bS2, bSmc])
        P.do("dve", lambda e: e.tensor_scalar(smc[:, 3:4], smc[:, 2:3], 1.0 / DM, cfg.EPS_LN,
                                              ALU.mult, ALU.add), r=[bSmc], w=[bSmc])
        P.do("act", lambda e: e.activation(smc[:, 4:5], smc[:, 3:4], AF.Ln), r=[bSmc], w=[bSmc])
        P.do("act", lambda e: e.activation(smc[:, 5:6], smc[:, 4:5], AF.Exp, scale=-0.5),
             r=[bSmc], w=[bSmc])
        P.do("dve", lambda e: e.tensor_scalar(xb[:, 0:DM], xa[:, 0:DM], smc[:, 1:2], smc[:, 5:6],
                                              ALU.subtract, ALU.mult), r=[bXa, bSmc], w=[bXb])
        P.do("act", lambda e: e.activation(sc2[:, 0:DM], xb[:, 0:DM], AF.Copy), r=[bXb], w=[bS2])
        P.do("dve", lambda e: e.tensor_tensor(sc1[:, 0:DM], xb[:, 0:DM], sc2[:, 0:DM],
                                              ALU.subtract), r=[bXb, bS2], w=[bS1])
        if SUB < 2:
            continue
        for cb in range(0, F, 4):
            nb = min(4, F - cb)
            pih = (cb // 4) % 2
            pil = 2 + (cb // 4) % 2

            def fn_hi(e, cb=cb, nb=nb, pih=pih):
                last = None
                for i in range(nb):
                    last = e.matmul(PP[pih][:, i * 128:(i + 1) * 128],
                                    sc2[:, (cb + i) * 128:(cb + i + 1) * 128], ic[:, :],
                                    start=True, stop=True)
                return last
            P.do("pe", fn_hi, r=[bS2, bConst], w=[bP[pih]])

            def fn_lo(e, cb=cb, nb=nb, pil=pil):
                last = None
                for i in range(nb):
                    last = e.matmul(PP[pil][:, i * 128:(i + 1) * 128],
                                    sc1[:, (cb + i) * 128:(cb + i + 1) * 128], ic[:, :],
                                    start=True, stop=True)
                return last
            P.do("pe", fn_lo, r=[bS1, bConst], w=[bP[pil]])
            for i in range(nb):
                c = cb + i
                P.do("act", lambda e, c=c, i=i, pih=pih, m=m:
                     e.activation(big[:, c * TPC + m * 128:c * TPC + (m + 1) * 128],
                                  PP[pih][:, i * 128:(i + 1) * 128], AF.Copy),
                     r=[bP[pih]], w=[xn_buf(c)])
            P.do("act", lambda e, cb=cb, nb=nb, pil=pil:
                 e.activation(qs[0][:, cb * 128:(cb + nb) * 128], PP[pil][:, 0:nb * 128], AF.Copy),
                 r=[bP[pil]], w=[bQs[0]])

        if SUB < 3:
            continue

        def fn_cent(e, m=m):
            last = None
            for c in range(F):
                hiT = big[:, c * TPC + m * 128:c * TPC + (m + 1) * 128]
                loT = qs[0][:, c * 128:(c + 1) * 128]
                last = e.matmul(PP[7][:, 0:NCL], hiT, rsb[:, c * NCL:(c + 1) * NCL],
                                start=(c == 0), stop=False)
                last = e.matmul(PP[7][:, 0:NCL], hiT, rsb[:, CLF + c * NCL:CLF + (c + 1) * NCL],
                                start=False, stop=False)
                last = e.matmul(PP[7][:, 0:NCL], loT, rsb[:, c * NCL:(c + 1) * NCL],
                                start=False, stop=(c == F - 1))
            return last
        P.do("pe", fn_cent, r=[bBA, bBB, bQs[0], bRsb], w=[bP[7]])
        if SUB < 4:
            continue
        SUB2 = int(os.environ.get("KSUB2", "99"))
        P.do("dve", lambda e: e.reduce_max(smc[:, 6:7], PP[7][:, 0:NCL], AXX, negate=True),
             r=[bP[7]], w=[bSmc])
        if SUB2 < 1:
            continue
        P.do("dve", lambda e: e.tensor_scalar(smc[:, 7:8], smc[:, 6:7], 1.0 / cfg.TEMP, None,
                                              ALU.mult), r=[bSmc], w=[bSmc])
        if SUB2 < 2:
            continue
        EXPV = int(os.environ.get("KEXPV", "0"))
        if EXPV == 1:
            P.do("act", lambda e: e.activation(tr[:, 0:NCL], PP[7][:, 0:NCL], AF.Exp,
                                               bias=smc[:, 7:8], scale=1.0 / cfg.TEMP),
                 r=[bP[7], bSmc], w=[bTr, bSmc])
        elif EXPV == 2:
            P.do("act", lambda e: e.activation(tr[:, 0:NCL], PP[7][:, 0:NCL], AF.Exp,
                                               scale=1.0 / cfg.TEMP, accum_out=smc[:, 8:9]),
                 r=[bP[7], bSmc], w=[bTr, bSmc])
        elif EXPV == 3:
            P.do("dve", lambda e: e.tensor_copy(xb[:, 0:NCL], PP[7][:, 0:NCL]), r=[bP[7]], w=[bXb])
            P.do("act", lambda e: e.activation(tr[:, 0:NCL], xb[:, 0:NCL], AF.Exp,
                                               bias=smc[:, 7:8], scale=1.0 / cfg.TEMP,
                                               accum_out=smc[:, 8:9]),
                 r=[bXb, bSmc], w=[bTr, bSmc])
        elif EXPV == 5:
            P.do("act", lambda e: e.activation(smc[:, 8:9], smc[:, 7:8], AF.Copy),
                 r=[bSmc], w=[bSmc])
        elif EXPV == 6:
            P.do("act", lambda e: e.activation(sc1[:, 0:NCL], PP[7][:, 0:NCL], AF.Exp,
                                               bias=smc[:, 7:8], scale=1.0 / cfg.TEMP,
                                               accum_out=smc[:, 8:9]),
                 r=[bP[7], bSmc], w=[bS1, bSmc])
        elif EXPV == 4:
            P.do("act", lambda e: e.activation(tr[:, 0:NCL], PP[7][:, 0:NCL], AF.Copy,
                                               bias=smc[:, 7:8], scale=1.0 / cfg.TEMP,
                                               accum_out=smc[:, 8:9]),
                 r=[bP[7], bSmc], w=[bTr, bSmc])
        else:
            P.do("act", lambda e: e.activation(tr[:, 0:NCL], PP[7][:, 0:NCL], AF.Exp,
                                               bias=smc[:, 7:8], scale=1.0 / cfg.TEMP,
                                               accum_out=smc[:, 8:9]),
                 r=[bP[7], bSmc], w=[bTr, bSmc])
        if SUB2 < 3:
            continue
        P.do("dve", lambda e, m=m: e.tensor_scalar(qselc[:, m:m + 1], smc[:, 8:9],
                                                   1.0 / cfg.THRESH, None, ALU.is_lt),
             r=[bSmc], w=[bQsel])
        if SUB2 < 4:
            continue
        P.do("dve", lambda e: e.reciprocal(smc[:, 9:10], smc[:, 8:9]), r=[bSmc], w=[bSmc])
        if SUB2 < 5:
            continue
        P.do("dve", lambda e: e.tensor_scalar(tr[:, 0:NCL], tr[:, 0:NCL], smc[:, 9:10], None,
                                              ALU.mult), r=[bSmc], w=[bTr])
        if SUB < 5:
            continue
        P.do("pe", lambda e: e.transpose(PP[6][0:NCL, 0:128], tr[:, 0:NCL], icf[:, :]),
             r=[bTr, bConst], w=[bP[6]])
        P.do("dve", lambda e: e.reduce_max(smc[0:NCL, 10:11], PP[6][0:NCL, 0:128], AXX),
             r=[bP[6]], w=[bSmc])
        if m == 0:
            P.do("dve", lambda e: e.tensor_copy(clm[:, :], smc[0:NCL, 10:11]), r=[bSmc], w=[bClm])
        else:
            P.do("dve", lambda e: e.tensor_tensor(clm[:, :], clm[:, :], smc[0:NCL, 10:11],
                                                  ALU.max), r=[bSmc, bClm], w=[bClm])

    for i in range(4):
        Wc = F * TPC // 4
        P.dma("sp", lambda e, i=i, Wc=Wc: e.dma_start(d_xnT[:, i * Wc:(i + 1) * Wc],
                                                      big[:, i * Wc:(i + 1) * Wc]),
              f"xnw{i}", r=[bBA, bBB], wa=[bDxnT])
    P.dma("sp", lambda e: e.dma_start(d_cli.ap(), clm[:, :]), "clw", r=[bClm], wa=[bDcli])

    slab_ctr = [0, 0]

    def load_slab(dram_ap, c0, nchunks, col0, width, stream):
        si = slab_ctr[0] % len(wsl)
        slab_ctr[0] += 1
        t, b = wsl[si], bW[si]
        war = list(b.rs) + ([b.w] if b.w is not None else []) + list(b.ws.values())
        for i in range(nchunks):
            src = dram_ap[(c0 + i) * 128:(c0 + i + 1) * 128, col0:col0 + width]
            sn = f"wld{slab_ctr[1] % 8}"
            slab_ctr[1] += 1
            if i == 0:
                P.dma("pool", lambda e, t=t, src=src, i=i, width=width:
                      e.dma_start(t[:, i * width:(i + 1) * width], src),
                      sn, w=[b], deps=war)
            else:
                P.dma("pool", lambda e, t=t, src=src, i=i, width=width:
                      e.dma_start(t[:, i * width:(i + 1) * width], src),
                      sn, wa=[b], deps=war)
        return t, b

    def proj_phase(dram_w, ncg, cw, nchunks_total, out_cb, which, after_first_slab=None):
        for cg in range(ncg):
            for c0 in range(0, F, NSL):
                t, b = load_slab(dram_w, c0, NSL, cg * cw, cw, "w" + which)
                if after_first_slab is not None:
                    after_first_slab()
                    after_first_slab = None
                for m in range(NT):
                    def fn(e, c0=c0, t=t, m=m, cw=cw):
                        last = None
                        for i in range(NSL):
                            c = c0 + i
                            last = e.matmul(PP[m % 8][:, 0:cw],
                                            big[:, c * TPC + m * 128:c * TPC + (m + 1) * 128],
                                            t[:, i * cw:(i + 1) * cw],
                                            start=(c == 0), stop=(c == F - 1))
                        return last
                    P.do("pe", fn, r=[bBA, bBB, b], w=[bP[m % 8]])
            for m in range(NT):
                out_cb(m, cg, PP[m % 8], bP[m % 8])

    def _finish():
        toks = []
        for name, (sem, val) in P.dsem.items():
            if val:
                toks.append((sem, val))
        P.wait_only("sp", toks)
        P.emit()
        return nc

    if stop_phase <= 1:
        return _finish()
    def kv_evac(kind):
        def cb(m, cg, psum, pbuf):
            nh = KW // 128
            if kind == "v":
                for hb in range(nh):
                    si = (m * nh + hb) % 4
                    kvh = cg * nh + hb
                    P.do("dve", lambda e, si=si, hb=hb, psum=psum:
                         e.tensor_copy(stg4[si][:, 0:128], psum[:, hb * 128:(hb + 1) * 128]),
                         r=[pbuf], w=[bStg[si]])
                    P.dma("sp", lambda e, si=si, m=m, kvh=kvh:
                          e.dma_start(d_exi.ap()[m * 128:(m + 1) * 128, KF + kvh * 128:KF + (kvh + 1) * 128],
                                      stg4[si][:, 0:128]),
                          f"stg{si}", r=[bStg[si]], wa=[bDexi])
            else:
                for hb in range(nh):
                    kvh = cg * nh + hb
                    P.do("act", lambda e, hb=hb, kvh=kvh, psum=psum:
                         e.activation(sc1[:, 0:128], psum[:, hb * 128:(hb + 1) * 128], AF.Square,
                                      accum_out=invk[:, kvh:kvh + 1]),
                         r=[pbuf], w=[bS1, bInvk])
                h0 = cg * nh
                P.do("dve", lambda e, h0=h0, nh=nh:
                     e.tensor_scalar(invk[:, HKV + h0:HKV + h0 + nh], invk[:, h0:h0 + nh],
                                     1.0 / HD, cfg.EPS_RMS, ALU.mult, ALU.add),
                     r=[bInvk], w=[bInvk])
                P.do("act", lambda e, h0=h0, nh=nh:
                     e.activation(invk[:, 2 * HKV + h0:2 * HKV + h0 + nh],
                                  invk[:, HKV + h0:HKV + h0 + nh], AF.Ln), r=[bInvk], w=[bInvk])
                P.do("act", lambda e, h0=h0, nh=nh:
                     e.activation(invk[:, 3 * HKV + h0:3 * HKV + h0 + nh],
                                  invk[:, 2 * HKV + h0:2 * HKV + h0 + nh], AF.Exp, scale=-0.5),
                     r=[bInvk], w=[bInvk])
                for hb in range(nh):
                    si = (m * nh + hb) % 4
                    kvh = cg * nh + hb
                    P.do("dve", lambda e, si=si, hb=hb, kvh=kvh, psum=psum:
                         e.scalar_tensor_tensor(stg4[si][:, 0:128], psum[:, hb * 128:(hb + 1) * 128],
                                                invk[:, 3 * HKV + kvh:3 * HKV + kvh + 1],
                                                wbck[:, :], ALU.mult, ALU.mult),
                         r=[pbuf, bInvk, bWbk], w=[bStg[si]])
                    P.dma("sp", lambda e, si=si, m=m, kvh=kvh:
                          e.dma_start(d_exi.ap()[m * 128:(m + 1) * 128, kvh * 128:(kvh + 1) * 128],
                                      stg4[si][:, 0:128]),
                          f"stg{si}", r=[bStg[si]], wa=[bDexi])
        return cb

    proj_phase(d_wk, KCG, KW, KC, kv_evac("k"), "k")
    proj_phase(d_wv, KCG, KW, KC, kv_evac("v"), "v")

    if stop_phase <= 2:
        return _finish()
    for h in range(H):
        sc = 1.0 / (math.sqrt(HD) * cfg.slopes[h])
        P.do("dve", lambda e, sc=sc: e.tensor_scalar(qwrb[:, 0:HD], qwr[:, 0:HD], sc, None,
                                                     ALU.mult), r=[bQwr], w=[bQwrb])
        pi = 4 + (h % 2)
        P.do("pe", MM(PP[pi][:, 0:128], onr[:, :], qwrb[:, 0:HD], True, True),
             r=[bConst, bQwrb], w=[bP[pi]])
        P.do("act", lambda e, h=h, pi=pi: e.activation(sc2[:, h * 128:(h + 1) * 128],
                                                       PP[pi][:, 0:128], AF.Copy),
             r=[bP[pi]], w=[bS2])

    def q_evac(m, cg, psum, pbuf):
        nh = QW // 128
        for hb in range(nh):
            h = cg * nh + hb
            P.do("act", lambda e, hb=hb, h=h, psum=psum:
                 e.activation(sc1[:, 0:128], psum[:, hb * 128:(hb + 1) * 128], AF.Square,
                              accum_out=invq[:, h:h + 1]), r=[pbuf], w=[bS1, bInvq])
        h0 = cg * nh
        P.do("dve", lambda e, h0=h0, nh=nh:
             e.tensor_scalar(invq[:, H + h0:H + h0 + nh], invq[:, h0:h0 + nh], 1.0 / HD,
                             cfg.EPS_RMS, ALU.mult, ALU.add), r=[bInvq], w=[bInvq])
        P.do("act", lambda e, h0=h0, nh=nh:
             e.activation(invq[:, 2 * H + h0:2 * H + h0 + nh], invq[:, H + h0:H + h0 + nh], AF.Ln),
             r=[bInvq], w=[bInvq])
        P.do("act", lambda e, h0=h0, nh=nh:
             e.activation(invq[:, 3 * H + h0:3 * H + h0 + nh], invq[:, 2 * H + h0:2 * H + h0 + nh],
                          AF.Exp, scale=-0.5), r=[bInvq], w=[bInvq])
        for hb in range(nh):
            si = (m * nh + hb) % 4
            h = cg * nh + hb
            P.do("dve", lambda e, si=si, hb=hb, h=h, psum=psum:
                 e.scalar_tensor_tensor(stg4[si][:, 0:128], psum[:, hb * 128:(hb + 1) * 128],
                                        invq[:, 3 * H + h:3 * H + h + 1],
                                        sc2[:, h * 128:(h + 1) * 128], ALU.mult, ALU.mult),
                 r=[pbuf, bInvq, bS2], w=[bStg[si]])
            P.dma("sp", lambda e, si=si, m=m, h=h:
                  e.dma_start(d_qh[m * 128:(m + 1) * 128, h * 128:(h + 1) * 128], stg4[si][:, 0:128]),
                  f"stg{si}", r=[bStg[si]], wa=[bDqh])

    def fn_ar(e):
        return e.collective_compute("AllReduce", ALU.max,
                                    replica_groups=[[0, 1, 2, 3, 4, 5, 6, 7]],
                                    ins=[d_cli.ap().opt()], outs=[d_clo.ap().opt()])

    def fn_ag(e):
        return e.collective_compute("AllGather", ALU.bypass,
                                    replica_groups=[[0, 1], [2, 3], [4, 5], [6, 7]],
                                    ins=[d_exi.ap().opt()], outs=[d_exo.ap().opt()])

    def emit_ccs():
        P.cc(fn_ag, "ccag", r=[bDexi], w=[bDexo])
        P.cc(fn_ar, "ccar", r=[bDcli], w=[bDclo])

    proj_phase(d_wq, QCG, QW, HC, q_evac, "q", after_first_slab=emit_ccs)

    if stop_phase <= 3:
        return _finish()
    for t in range(ST):
        rb = (t % 2) * TPC + (t // 2) * 128
        st, sb = kvst[t % 2], bKv[t % 2]
        P.dma("sp", lambda e, st=st, rb=rb: e.dma_start(st[:, 0:KF],
                                                        d_exo.ap()[rb:rb + 128, 0:KF]),
              f"exr{t % 2}", r=[bDexo], w=[sb])
        P.dma("sp", lambda e, t=t, rb=rb: e.dma_start(big[:, BIGA + t * KF:BIGA + (t + 1) * KF],
                                                      d_exo.ap()[rb:rb + 128, KF:2 * KF]),
              "exrv", r=[bDexo], w=[bBB])
        for kb in range(0, KC, 4):
            nb = min(4, KC - kb)
            pi = (kb // 4) % 2

            def fn_tr(e, st=st, kb=kb, nb=nb, pi=pi):
                last = None
                for i in range(nb):
                    last = e.matmul(PP[pi][:, i * 128:(i + 1) * 128],
                                    st[:, (kb + i) * 128:(kb + i + 1) * 128], ic[:, :],
                                    start=True, stop=True)
                return last
            P.do("pe", fn_tr, r=[sb, bConst], w=[bP[pi]])
            for i in range(nb):
                kv = kb + i
                P.do("act", lambda e, kv=kv, t=t, pi=pi, i=i:
                     e.activation(big[:, kv * S + t * 128:kv * S + (t + 1) * 128],
                                  PP[pi][:, i * 128:(i + 1) * 128], AF.Copy),
                     r=[bP[pi]], w=[bBA])

    if stop_phase <= 35:
        return _finish()
    fxab = []
    for _b in (bXa, bXb, bTr, bS1):
        if _b.w is not None:
            fxab.append(_b.w)
        fxab.extend(_b.rs)
        fxab.extend(_b.ws.values())
    r4bt = tr
    DTT0, MSKA, MSKB = 0, 512, 1024

    for j in range(NT):
        stiles = 2 * j + 2
        stg, sbq = qs[j % 2], bQs[j % 2]
        if j == 0:
            P.dma("sp", lambda e, stg=stg: e.dma_start(stg[:, 0:QF], d_qh[0:128, :]),
                  "qhr0", r=[bDqh], w=[sbq])
        for cb in range(0, HC, 4):
            pi = 6 + (cb // 4) % 2

            def fn_trq(e, stg=stg, cb=cb, pi=pi):
                last = None
                for i in range(4):
                    last = e.matmul(PP[pi][:, i * 128:(i + 1) * 128],
                                    stg[:, (cb + i) * 128:(cb + i + 1) * 128], ic[:, :],
                                    start=True, stop=True)
                return last
            P.do("pe", fn_trq, r=[sbq, bConst], w=[bP[pi]])
            P.do("act", lambda e, cb=cb, pi=pi:
                 e.activation(qt[:, cb * 128:(cb + 4) * 128], PP[pi][:, 0:512], AF.Copy),
                 r=[bP[pi]], w=[bQt])
        if j + 1 < NT:
            stg2, sbq2 = qs[(j + 1) % 2], bQs[(j + 1) % 2]
            P.dma("sp", lambda e, stg2=stg2, j=j:
                  e.dma_start(stg2[:, 0:QF], d_qh[(j + 1) * 128:(j + 2) * 128, :]),
                  f"qhr{(j + 1) % 2}", r=[bDqh], w=[sbq2])

        cc0 = j * (NTK + 1)

        def emit_sttr(g, kt, j=j):
            pb = kt % 2
            P.do("pe", MM(PP[pb][:, 0:512],
                          big[:, g * S + kt * 128:g * S + (kt + 1) * 128],
                          qt[:, g * 512:(g + 1) * 512], True, True),
                 r=[bBA, bQt], w=[bP[pb]])
            mo = MSKA if kt == 2 * j else (MSKB if kt == 2 * j + 1 else DTT0)
            P.do("dve", lambda e, kt=kt, pb=pb, mo=mo, cc0=cc0:
                 e.scalar_tensor_tensor(argb[:, kt * 512:(kt + 1) * 512],
                                        PP[pb][:, 0:512], c2s[:, cc0 + kt:cc0 + kt + 1],
                                        mskt[:, mo:mo + 512], ALU.add, ALU.add),
                 r=[bP[pb], bMsk, bC2], w=[bArg[kt]], deps=fxab)
            P.do("dve", lambda e, kt=kt:
                 e.reduce_max(cmx[:, kt:kt + 1], argb[:, kt * 512:(kt + 1) * 512],
                              AXX),
                 r=[bArg[kt]], w=[bCm])

        def emit_b1tail(g, j=j, stiles=stiles):
            P.do("dve", lambda e, stiles=stiles:
                 e.reduce_max(smc[:, 0:1], cmx[:, 0:stiles], AXX),
                 r=[bCm], w=[bSmc])

        def emit_b2(g, j=j, stiles=stiles):
            P.do("pe", lambda e: e.matmul(PP[6][0:1, 0:128], smc[:, 0:1], icf[:, :],
                                          start=True, stop=True),
                 r=[bSmc, bConst], w=[bP[6]])
            P.do("dve", lambda e: e.reduce_max(r1b[0:1, 32:33], PP[6][0:1, 0:128], AXX),
                 r=[bP[6]], w=[bR1])
            for i in range(4):
                h = 4 * g + i
                P.do("dve", lambda e, i=i, h=h:
                     e.tensor_scalar(r1b[0:1, 64 + i:64 + i + 1],
                                     r1b[0:1, 32:33], -cfg.slopes[h], None, ALU.mult),
                     r=[bR1], w=[bR1])
            P.do("pe", lambda e: e.matmul(PP[6][:, 0:4], onesf[:, :], r1b[0:1, 64:68],
                                          start=True, stop=True),
                 r=[bR1, bConst], w=[bP[6]])
            P.do("dve", lambda e: e.tensor_copy(biasg[:, 0:4], PP[6][:, 0:4]),
                 r=[bP[6]], w=[bBias])

        def emit_cexp(g, kt0, j=j, stiles=stiles):
            qn = min(4, stiles - kt0)
            for i in range(4):
                h = 4 * g + i
                inv = argb[:, kt0 * 512:(kt0 + qn) * 512].rearrange(
                    "p (u c) -> p u c", u=qn)[:, :, i * 128:(i + 1) * 128]
                outv = sc1[:, 0:qn * 512].rearrange(
                    "p (u c) -> p u c", u=qn)[:, :, i * 128:(i + 1) * 128]
                P.do("act", lambda e, inv=inv, outv=outv, h=h:
                     e.activation(outv, inv, AF.Exp, bias=biasg[:, h - 4 * g:h - 4 * g + 1],
                                  scale=cfg.slopes[h]),
                     r=[bArg[u] for u in range(kt0, kt0 + qn)] + [bBias],
                     w=[bPt[u] for u in range(qn)], deps=fxab)

        def emit_ck(g, kt, j=j, stiles=stiles):
            po, ps = 2 + (g % 2), 4 + (g % 2)
            tp = kt % 4
            P.do("pe", lambda e, kt=kt, tp=tp, ps=ps, stiles=stiles:
                 e.matmul(PP[ps][0:1, 0:512], onescol[:, :], pt[tp][:, 0:512],
                          start=(kt == 0), stop=(kt == stiles - 1)),
                 r=[bPt[tp], bConst], w=[bP[ps]])
            P.do("pe", lambda e, kt=kt, tp=tp, po=po, g=g, stiles=stiles:
                 e.matmul(PP[po][:, 0:512],
                          big[:, BIGA + kt * KF + g * 128:BIGA + kt * KF + g * 128 + 128],
                          pt[tp][:, 0:512],
                          start=(kt == 0), stop=(kt == stiles - 1)),
                 r=[bPt[tp], bBB], w=[bP[po]])

        def emit_d(g, j=j):
            po, ps = 2 + (g % 2), 4 + (g % 2)
            og = g % 2
            P.do("dve", lambda e, ps=ps: e.tensor_copy(sgb[0:1, 0:512], PP[ps][0:1, 0:512]),
                 r=[bP[ps]], w=[bSg], deps=fxab)
            P.do("pe", lambda e: e.matmul(PP[7][:, 0:512], onr[:, :], sgb[0:1, 0:512],
                                          start=True, stop=True),
                 r=[bSg, bConst], w=[bP[7]])
            P.do("act", lambda e: e.activation(r4bt[:, 512:1024], PP[7][:, 0:512], AF.Ln),
                 r=[bP[7]], w=[bR4], deps=fxab)
            P.do("act", lambda e: e.activation(r4bt[:, 1024:1536], r4bt[:, 512:1024],
                                               AF.Exp, scale=-1.0),
                 r=[bR4], w=[bR4], deps=fxab)
            P.do("dve", lambda e, po=po, og=og:
                 e.tensor_tensor(outsg[og][:, 0:512], PP[po][:, 0:512], r4bt[:, 1024:1536],
                                 ALU.mult),
                 r=[bP[po], bR4], w=[bOsg[og]], deps=fxab)
            for i in range(4):
                h = 4 * g + i
                P.dma("sp", lambda e, h=h, j=j, i=i, og=og:
                      e.dma_start(d_ot[h, :, j * 128:(j + 1) * 128],
                                  outsg[og][:, i * 128:(i + 1) * 128]),
                      f"otw{og}", r=[bOsg[og]], wa=[bDot])

        for g in range(HKV):
            for kt in range(stiles):
                if g > 0:
                    if kt % 4 == 0:
                        emit_cexp(g - 1, kt)
                    emit_ck(g - 1, kt)
                emit_sttr(g, kt)
            emit_b1tail(g)
            emit_b2(g)
            if g > 1:
                emit_d(g - 2)
        for kt in range(stiles):
            if kt % 4 == 0:
                emit_cexp(HKV - 1, kt)
            emit_ck(HKV - 1, kt)
        emit_d(HKV - 2)
        emit_d(HKV - 1)

    if stop_phase <= 4:
        return _finish()
    P.dma("sp", lambda e: e.dma_start(clm[:, :], d_clo.ap()), "clr", r=[bDclo], w=[bClm])
    P.do("dve", lambda e: e.tensor_scalar(clsel[:, :], clm[:, :], cfg.THRESH, None, ALU.is_gt),
         r=[bClm], w=[bClsel])
    for gc in range(DCG):
        P.dma("pool", lambda e, gc=gc: e.dma_start(asn_i[:, 0:DW], d_asn[:, gc * DW:(gc + 1) * DW]),
              "asnl", r=[bAsn], w=[bAsn])
        P.do("dve", lambda e: e.tensor_copy(yf[0:1, 0:DW], asn_i[:, 0:DW]), r=[bAsn], w=[bYf])
        P.do("dve", lambda e: e.tensor_copy(rowsb[:, 1024:1024 + DW], yf[0:1, 0:DW]),
             r=[bYf], w=[bRowsb])
        P.do("pe", MM(PP[2][0:NCL, 0:DW], onr[:, 0:NCL], rowsb[:, 1024:1024 + DW], True, True),
             r=[bConst, bRowsb], w=[bP[2]])
        P.do("dve", lambda e, gc=gc: e.tensor_scalar(qt[0:NCL, gc * DW:(gc + 1) * DW],
                                                     PP[2][0:NCL, 0:DW], iotc[:, :], None,
                                                     ALU.is_equal),
             r=[bP[2], bConst], w=[bQt])
        P.do("pe", lambda e, gc=gc: e.matmul(PP[3][0:1, 0:DW], clsel[:, :],
                                             qt[0:NCL, gc * DW:(gc + 1) * DW],
                                             start=True, stop=True),
             r=[bClsel, bQt], w=[bP[3]])
        P.do("dve", lambda e: e.tensor_copy(rowsb[:, 1536:1536 + DW], PP[3][0:1, 0:DW]),
             r=[bP[3]], w=[bRowsb])
        P.do("pe", lambda e: e.matmul(PP[2][:, 0:DW], onr[:, :], rowsb[:, 1536:1536 + DW],
                                      start=True, stop=True),
             r=[bConst, bRowsb], w=[bP[2]])
        P.do("dve", lambda e, gc=gc: e.tensor_copy(rsb[:, gc * DW:(gc + 1) * DW], PP[2][:, 0:DW]),
             r=[bP[2]], w=[bRsb])

    fx5 = []
    for _b in bArg:
        if _b.w is not None:
            fx5.append(_b.w)
        fx5.extend(_b.rs)
        fx5.extend(_b.ws.values())
    fq5 = []
    for _b in bQs:
        if _b.w is not None:
            fq5.append(_b.w)
        fq5.extend(_b.rs)
        fq5.extend(_b.ws.values())
    bOt6 = [Buf() for _ in range(6)]
    for cg in range(DCG):
        P.dma("pool", lambda e, cg=cg: e.dma_start(
            xa[:, 0:NT * DW].rearrange("p (m n) -> p m n", m=NT),
            d_x[:, cg * DW:(cg + 1) * DW].rearrange("(m p) n -> p m n", p=128)),
            "xs", w=[bXa], deps=fx5)
        P.dma("pool", lambda e, cg=cg: e.dma_start(yf[0:1, 0:DW], d_bias[:, cg * DW:(cg + 1) * DW]),
              "bil", r=[bYf], w=[bYf])
        P.do("act", lambda e: e.activation(rowsb[:, 0:DW], yf[0:1, 0:DW], AF.Copy),
             r=[bYf], w=[bRowsb])
        P.do("dve", lambda e: e.tensor_tensor(rowsb[:, 512:512 + DW], yf[0:1, 0:DW],
                                              rowsb[:, 0:DW], ALU.subtract),
             r=[bYf, bRowsb], w=[bRowsb])
        allP = [bP[mm % 8] for mm in range(PCNT)]
        OTQ = ("sp", "act")
        for c0 in range(0, HC, NSL):
            ns = min(NSL, HC - c0)
            t, b = load_slab(d_wo, c0, ns, cg * DW, DW, "wo")
            for i in range(ns):
                c = c0 + i
                sl = c % 6
                stg = qs[0] if sl < 4 else qs[1]
                off = (sl % 4) * TPC
                sbo = bOt6[sl]
                P.dma(OTQ[c % 2], lambda e, stg=stg, off=off, c=c: e.dma_start(
                    stg[:, off:off + TPC], d_ot[c, :, :]),
                    f"otr{c % 2}", r=[bDot], w=[sbo], deps=fq5)

                def fn(e, stg=stg, off=off, t=t, i=i, c=c):
                    last = None
                    for m in range(NT):
                        last = e.matmul(PP[m % 8][:, 0:DW],
                                        stg[:, off + m * 128:off + (m + 1) * 128],
                                        t[:, i * DW:(i + 1) * DW],
                                        start=(c == 0), stop=(c == HC - 1))
                    return last
                P.do("pe", fn, r=[sbo, b], w=allP)
        for m in range(NT):
            P.do("dve", lambda e, m=m: e.tensor_copy(xb[:, m * DW:(m + 1) * DW],
                                                     PP[m % 8][:, 0:DW]),
                 r=[bP[m % 8]], w=[bXb], deps=fx5)
        if cg == 0:
            for i in range(4):
                Wc = F * TPC // 4
                P.dma(("sp", "act")[i % 2], lambda e, i=i, Wc=Wc: e.dma_start(
                    big[:, i * Wc:(i + 1) * Wc], d_xnT[:, i * Wc:(i + 1) * Wc]),
                      f"xnr{i}", r=[bDxnT], w=[bBA] if i < 2 else [bBB])
        for c0 in range(0, F, NSL):
            t, b = load_slab(d_wf, c0, NSL, cg * DW, DW, "wf")
            for i in range(NSL):
                c = c0 + i

                def fn(e, t=t, i=i, c=c):
                    last = None
                    for m in range(NT):
                        last = e.matmul(PP[m % 8][:, 0:DW],
                                        big[:, c * TPC + m * 128:c * TPC + (m + 1) * 128],
                                        t[:, i * DW:(i + 1) * DW],
                                        start=(c == 0), stop=False)
                    return last
                P.do("pe", fn, r=[bBA, bBB, b], w=allP)

        def fn_bias(e, cg=cg):
            last = None
            for m in range(NT):
                last = e.matmul(PP[m % 8][:, 0:DW], onr[:, :],
                                rowsb[:, 0:DW], start=False, stop=False)
                last = e.matmul(PP[m % 8][:, 0:DW], onr[:, :],
                                rowsb[:, 512:512 + DW], start=False, stop=True)
            return last
        P.do("pe", fn_bias, r=[bConst, bRowsb], w=allP)
        for m in range(NT):
            osl, obf = outst[m % 2], bOut[m % 2]
            P.do("dve", lambda e, m=m, cg=cg:
                 e.scalar_tensor_tensor(yf[:, 0:DW], PP[m % 8][:, 0:DW], qselc[:, m:m + 1],
                                        rsb[:, cg * DW:(cg + 1) * DW], ALU.mult, ALU.mult),
                 r=[bP[m % 8], bQsel, bRsb], w=[bYf])
            P.do("dve", lambda e, m=m:
                 e.tensor_tensor(yf[:, DW:2 * DW], yf[:, 0:DW], xa[:, m * DW:(m + 1) * DW],
                                 ALU.add), r=[bYf, bXa], w=[bYf])
            P.do("dve", lambda e, m=m, osl=osl:
                 e.tensor_tensor(osl[:, :], yf[:, DW:2 * DW], xb[:, m * DW:(m + 1) * DW],
                                 ALU.add), r=[bYf, bXb], w=[obf])
            P.dma("sp", lambda e, m=m, cg=cg, osl=osl:
                  e.dma_start(d_out[m * 128:(m + 1) * 128, cg * DW:(cg + 1) * DW], osl[:, :]),
                  f"outw{m % 2}", r=[obf], wa=[bDout])

    P.wait_only("sp", list(bDout.ws.values()) + ([bDout.w] if bDout.w else []))
    P.emit()
    return nc



def make_inputs(cfg, x, Wq, Wk, Wv, Wo, qw, kw, Wf, fbias, cent, asn):
    B, S, DM, NT = cfg.B, cfg.S, cfg.DM, cfg.NT
    x = np.asarray(x, dtype=np.float32)
    shared = {
        "wqT": np.ascontiguousarray(np.asarray(Wq, dtype=np.float32).T),
        "wkT": np.ascontiguousarray(np.asarray(Wk, dtype=np.float32).T),
        "wvT": np.ascontiguousarray(np.asarray(Wv, dtype=np.float32).T),
        "woT": np.ascontiguousarray(np.asarray(Wo, dtype=np.float32).T),
        "wfT": np.ascontiguousarray(np.asarray(Wf, dtype=np.float32).T),
        "ctT": np.ascontiguousarray(np.asarray(cent, dtype=np.float32).T),
        "fbias": np.asarray(fbias, dtype=np.float32).reshape(1, DM),
        "qnw": np.asarray(qw, dtype=np.float32).reshape(1, cfg.HD),
        "knw": np.asarray(kw, dtype=np.float32).reshape(1, cfg.HD),
        "asn": np.asarray(asn).reshape(1, DM).astype(np.int32),
    }
    xt = x.reshape(B, cfg.ST, 128, DM)
    NTK = 2 * NT
    dttn = (np.arange(128)[:, None] - np.arange(128)[None, :]).astype(np.float32)
    tri = (dttn > 0).astype(np.float32)
    in_maps = []
    for c in range(8):
        b, par = c // 2, c % 2
        xs = np.ascontiguousarray(xt[b, par::2].reshape(cfg.TPC, DM))
        c2 = np.zeros((128, NT * (NTK + 1)), dtype=np.float32)
        for j in range(NT):
            for kt in range(2 * j + 2):
                c2[:, j * (NTK + 1) + kt] = 128.0 * (kt - (2 * j + par))
        if par == 0:
            mskA = dttn + tri * (-cfg.NEGBIG)
            mskB = dttn - cfg.NEGBIG
        else:
            mskA = dttn
            mskB = dttn + tri * (-cfg.NEGBIG)
        m3 = np.concatenate([np.tile(dttn, (1, 4)), np.tile(mskA, (1, 4)),
                             np.tile(mskB, (1, 4))], axis=1).astype(BF)
        in_maps.append(dict(shared, xt=xs, c2in=c2, mskin=m3))
    return in_maps


def assemble(cfg, outs):
    B, S, DM, NT = cfg.B, cfg.S, cfg.DM, cfg.NT
    full = np.zeros((B, S, DM), dtype=np.float32)
    for c in range(8):
        b, par = c // 2, c % 2
        o = np.asarray(outs[c]).reshape(NT, 128, DM)
        for t in range(NT):
            full[b, (2 * t + par) * 128:(2 * t + par + 1) * 128, :] = o[t]
    return full


_FULL = Cfg()


def kernel(L0_attn_x, L0_attn_Wq, L0_attn_Wk, L0_attn_Wv, L0_attn_Wo,
           L0_attn_q_norm_w, L0_attn_k_norm_w,
           L0_ffn_weight, L0_ffn_bias, L0_ffn_centroids, L0_ffn_assignments):
    cfg = _FULL
    nc = build_program(cfg)
    in_maps = make_inputs(cfg, L0_attn_x, L0_attn_Wq, L0_attn_Wk, L0_attn_Wv, L0_attn_Wo,
                          L0_attn_q_norm_w, L0_attn_k_norm_w, L0_ffn_weight, L0_ffn_bias,
                          L0_ffn_centroids, L0_ffn_assignments)
    res = run_bass_kernel_spmd(nc, in_maps, core_ids=list(range(8)))
    return assemble(cfg, [r["out"] for r in res.results])
